# version 06
# baseline (speedup 1.0000x reference)
import math
from contextlib import ExitStack

import numpy as np
import ml_dtypes

import concourse.bass as bass
import concourse.mybir as mybir
from concourse.bass_utils import run_bass_kernel_spmd

F32 = mybir.dt.float32
BF16 = mybir.dt.bfloat16
I32 = mybir.dt.int32
I8 = mybir.dt.int8
AF = mybir.ActivationFunctionType
ALU = mybir.AluOpType
BF = ml_dtypes.bfloat16

P = 128
T = 4096
D = 2048
KT = D // P
NBT = T // 512
H = 16
DH = 128
C0 = 128
NC0 = T // C0
BLK1 = 256
NB1 = T // BLK1
I0 = 5504
OT0 = I0 // P
I1 = 8192
OT1 = I1 // P
N_CORES = 8

ENGINES = ("sync", "scalar", "vector", "tensor", "gpsimd")



def blkW(W):
    K, M = W.shape
    kt, mt = K // P, M // P
    return np.ascontiguousarray(
        W.reshape(kt, P, mt, P).transpose(2, 1, 0, 3).reshape(mt, P, kt * P))


def fm(x):
    return np.ascontiguousarray(x.T.reshape(KT, P, T))


def col_blocks(v):
    return np.ascontiguousarray(np.asarray(v).reshape(-1, P, 1))


def mk_consts():
    c = {}
    c["ones_bf"] = np.ones((P, 1), dtype=BF)
    c["ones_row_f32"] = np.ones((1, P), dtype=np.float32)
    c["ident"] = np.eye(P, dtype=BF)

    freq = (1.0 / 1024.0) ** np.linspace(0.0, 1.0, DH // 4, dtype=np.float32)
    freq = np.concatenate([freq, np.zeros(DH // 4, dtype=np.float32)])
    t = np.arange(T, dtype=np.float32)
    theta = (t[:, None] * freq[None, :]).astype(np.float32).astype(np.float64)
    cosT = np.cos(theta).astype(np.float32).T
    sinT = np.sin(theta).astype(np.float32).T
    c["cos2"] = np.ascontiguousarray(np.concatenate([cosT, cosT], axis=0))
    c["sin2pm"] = np.ascontiguousarray(np.concatenate([sinT, -sinT], axis=0))
    sw = np.zeros((P, P), dtype=BF)
    for m in range(P):
        sw[(m + 64) % P, m] = 1
    c["swapmat"] = sw

    start = 2.0 ** (-2.0 ** (-(math.log2(H) - 3)))
    slopes = np.array([start * start ** i for i in range(H)], dtype=np.float64)
    r = np.exp(-slopes)
    i_ = np.arange(C0, dtype=np.float64)
    qdec = r[:, None] ** (i_[None, :] + 1.0)
    kdec = r[:, None] ** (C0 - 1.0 - i_[None, :])
    diff = i_[:, None] - i_[None, :]
    dmask = np.where(diff >= 0, r[:, None, None] ** np.maximum(diff, 0.0)[None], 0.0)
    c["qdec_b"] = np.ascontiguousarray(
        np.broadcast_to(qdec[:, None, :], (H, P, C0)).astype(np.float32))
    c["kdec_pc"] = np.ascontiguousarray(kdec.astype(np.float32).T)
    c["dmaskT"] = np.ascontiguousarray(dmask.transpose(0, 2, 1).astype(np.float32))
    c["rC_pc"] = np.ascontiguousarray(
        np.broadcast_to((r ** float(C0)).astype(np.float32)[None, :], (P, H)))

    arr = np.arange(1, BLK1 + 1, dtype=np.float32)
    c["l1_arr_row_b"] = np.ascontiguousarray(np.broadcast_to(arr[None, :], (P, BLK1)))
    c["l1_rev_pc"] = np.ascontiguousarray((float(BLK1) - arr).reshape(2, P).T)
    idx = arr[:, None] - arr[None, :]
    c["l1_posT"] = np.ascontiguousarray(
        np.maximum(idx, 0.0).T.reshape(2, P, BLK1).astype(np.float32))
    c["l1_maskT"] = np.ascontiguousarray(
        (idx >= 0.0).T.astype(np.float32).reshape(2, P, BLK1))
    c["l1_blk_const"] = np.full((P, 1), float(BLK1), dtype=np.float32)
    cp = np.zeros((P, 8), dtype=np.float32)
    cp[:, 0] = 1e-5
    cp[:, 1] = 1e-6
    cp[:, 2] = 1.0 / D
    cp[:, 3] = -0.5
    cp[:, 4] = float(BLK1)
    c["cpack"] = cp
    return c


def mk_layouts(inp):
    d = {}
    x = np.asarray(inp["L0_attn_x"])
    d["x0"] = [fm(x[b]) for b in range(N_CORES)]

    d["l0_wq"] = blkW(np.asarray(inp["L0_attn_Wq"]))
    d["l0_wk"] = blkW(np.asarray(inp["L0_attn_Wk"]))
    d["l0_wv"] = blkW(np.asarray(inp["L0_attn_Wv"]))
    d["l0_wo"] = blkW(np.asarray(inp["L0_attn_Wo"]))
    d["l0_rmsw"] = np.ascontiguousarray(
        np.asarray(inp["L0_attn_rms_w"]).reshape(KT, P).T.astype(np.float32))

    lut = np.asarray(inp["L0_ffn_lut"])
    d["l0_lut"] = np.ascontiguousarray(np.broadcast_to(lut[None, :], (P, 16)).astype(np.float32))
    d["l0_giT"] = np.ascontiguousarray(np.asarray(inp["L0_ffn_gate_idx"]).T.reshape(KT, P, I0))
    d["l0_uiT"] = np.ascontiguousarray(np.asarray(inp["L0_ffn_up_idx"]).T.reshape(KT, P, I0))
    d["l0_diT"] = np.ascontiguousarray(np.asarray(inp["L0_ffn_down_idx"]).T.reshape(OT0, P, D))

    sl = np.asarray(inp["L1_attn_slope_rate"]).reshape(H)
    d["l1_slope_b"] = np.ascontiguousarray(np.broadcast_to(sl[None, :], (P, H)).astype(np.float32))
    d["l1_wqkvT"] = blkW(np.ascontiguousarray(np.asarray(inp["L1_attn_w_qkv"]).T))
    d["l1_wgateT"] = blkW(np.ascontiguousarray(np.asarray(inp["L1_attn_w_gate"]).T))
    d["l1_woutT"] = blkW(np.ascontiguousarray(np.asarray(inp["L1_attn_w_out"]).T))
    d["l1_normw"] = np.ascontiguousarray(
        np.asarray(inp["L1_attn_norm_w"]).reshape(KT, P).T.astype(np.float32))

    d["l1_wg8"] = blkW(np.asarray(inp["L1_ffn_w_gate"]))
    d["l1_wu8"] = blkW(np.asarray(inp["L1_ffn_w_up"]))
    d["l1_wdk"] = np.ascontiguousarray(
        np.asarray(inp["L1_ffn_w_down"]).reshape(OT1, P, KT, P))
    d["l1_sg"] = np.ascontiguousarray(np.asarray(inp["L1_ffn_s_gate"]).reshape(OT1, P).T.astype(np.float32))
    d["l1_su"] = np.ascontiguousarray(np.asarray(inp["L1_ffn_s_up"]).reshape(OT1, P).T.astype(np.float32))
    d["l1_sd"] = np.ascontiguousarray(np.asarray(inp["L1_ffn_s_down"]).reshape(KT, P).T.astype(np.float32))
    return d


class Slots:

    def __init__(self, b, name, n, persistent=False):
        self.b = b
        self.sems = [b.sem(f"{name}_{i}", persistent=persistent) for i in range(n)]
        self.uses = [0] * n

    def load(self, eng, slot, fn):
        self.uses[slot] += 1
        self.b.op(eng, fn, self.sems[slot], 16)
        return 16 * self.uses[slot]

    def wait(self, eng, slot, count):
        self.b.wait(eng, self.sems[slot], count)


def np_dt(dt):
    dt = np.dtype(dt)
    if dt == np.float32:
        return F32
    if dt == np.int32:
        return I32
    if dt == np.int8:
        return I8
    if dt == np.dtype(BF):
        return BF16
    raise ValueError(dt)



class Build:
    def __init__(self):
        self.nc = bass.Bass("TRN2", target_bir_lowering=False)
        self.prog = {e: [] for e in ENGINES}
        self.stack = ExitStack()
        self.sem_stack = ExitStack()
        self._sems = {}
        self._bar = None
        self.host = {}
        self._dram = {}
        self.bar_n = 0

    def sem(self, name, persistent=False):
        if name == "bar":
            if self._bar is None:
                self._bar = self.stack.enter_context(self.nc.semaphore(name))
            return self._bar
        if not hasattr(self, "_psems"):
            self._psems = {}
        if persistent or name in self._psems:
            if name not in self._psems:
                self._psems[name] = self.stack.enter_context(self.nc.semaphore(name))
            return self._psems[name]
        if name not in self._sems:
            self._sems[name] = self.sem_stack.enter_context(self.nc.semaphore(name))
        return self._sems[name]

    def phase_sems_reset(self):
        self.barrier()
        e = self.nc.sync
        for h in list(self._sems.values()):
            self.prog["sync"].append(lambda h=h: e.sem_clear(h))
        self.barrier()
        self.sem_stack.close()
        self.sem_stack = ExitStack()
        self._sems = {}

    def din(self, name, arr):
        if name not in self._dram:
            a = arr[0] if isinstance(arr, list) else arr
            tn = self.nc.dram_tensor(name, list(a.shape), np_dt(a.dtype), kind="ExternalInput")
            self._dram[name] = tn.ap()
            self.host[name] = arr
        return self._dram[name]

    def dout(self, name, shape, dtype):
        if name not in self._dram:
            self._dram[name] = self.nc.dram_tensor(name, list(shape), dtype, kind="ExternalOutput").ap()
        return self._dram[name]

    def dscratch(self, name, shape, dtype):
        if name not in self._dram:
            self._dram[name] = self.nc.dram_tensor(name, list(shape), dtype).ap()
        return self._dram[name]

    def region(self):
        return ExitStack()

    def sb(self, reg, name, shape, dtype, **kw):
        return reg.enter_context(self.nc.sbuf_tensor(name, list(shape), dtype, **kw))

    def ps(self, reg, name, shape, dtype=F32):
        return reg.enter_context(self.nc.psum_tensor(name, list(shape), dtype))

    def ei(self, eng, fn):
        self.prog[eng].append(fn)

    def wait(self, eng, sem, val):
        s = self.sem(sem) if isinstance(sem, str) else sem
        e = getattr(self.nc, eng)
        self.prog[eng].append(lambda: e.wait_ge(s, val))

    def op(self, eng, fn, sem=None, inc=1):
        s = self.sem(sem) if isinstance(sem, str) else sem
        if s is None:
            self.prog[eng].append(fn)
        else:
            self.prog[eng].append(lambda: fn().then_inc(s, inc))

    def pub(self, eng, sem, inc=1):
        s = self.sem(sem) if isinstance(sem, str) else sem
        e = getattr(self.nc, eng)
        self.prog[eng].append(lambda: e.drain().then_inc(s, inc))

    def barrier(self):
        s = self.sem("bar")
        self.bar_n += len(ENGINES)
        tgt = self.bar_n
        for eng in ENGINES:
            e = getattr(self.nc, eng)
            if eng == "sync":
                self.prog[eng].append(lambda e=e: e.nop().then_inc(s, 1))
            else:
                self.prog[eng].append(lambda e=e: e.drain().then_inc(s, 1))
        for eng in ENGINES:
            e = getattr(self.nc, eng)
            self.prog[eng].append(lambda e=e: e.wait_ge(s, tgt))

    def finalize(self, final_waits=()):
        for sem, val in final_waits:
            self.wait("sync", sem, val)
        nc = self.nc
        with self.stack:
            with nc.Block() as block:
                def mk(eng):
                    def run(_):
                        for f in self.prog[eng]:
                            f()
                    return run
                block.sync(mk("sync"))
                block.scalar(mk("scalar"))
                block.vector(mk("vector"))
                block.tensor(mk("tensor"))
                block.gpsimd(mk("gpsimd"))
        return nc



def emit_preln(b, tag, xin, c, pr):
    nc = b.nc
    r0 = b.region()
    ones_sb = b.sb(r0, f"{tag}ones", [P, 1], BF16)
    oner_sb = b.sb(r0, f"{tag}oner", [1, P], F32)
    cp_sb = b.sb(r0, f"{tag}cp", [P, 8], F32)
    rowr = [b.sb(r0, f"{tag}rr{i}", [1, 512], F32) for i in range(2)]
    rowmr = [b.sb(r0, f"{tag}rm{i}", [1, 512], F32) for i in range(2)]
    stps = b.ps(r0, f"{tag}stps", [P, 4096])

    sl_x = Slots(b, tag + "x", 2)
    s_c = b.sem(tag + "c")
    s_q = b.sem(tag + "q")
    s_m = b.sem(tag + "m")
    s_1 = b.sem(tag + "1")

    r1 = b.region()
    xa = [b.sb(r1, f"{tag}xa{i}", [P, T], F32) for i in range(2)]
    xb = [b.sb(r1, f"{tag}xb{i}", [P, T], BF16) for i in range(2)]
    xq = [b.sb(r1, f"{tag}xq{i}", [P, T], BF16) for i in range(2)]

    b.op("sync", lambda: nc.sync.dma_start(out=ones_sb[:], in_=b.din("ones_bf", c["ones_bf"])), s_1, 16)
    b.op("sync", lambda: nc.sync.dma_start(out=oner_sb[:], in_=b.din("ones_row_f32", c["ones_row_f32"])), s_1, 16)
    b.op("sync", lambda: nc.sync.dma_start(out=cp_sb[:], in_=b.din("cpack", c["cpack"])), s_1, 16)
    for k in range(KT):
        if k >= 2:
            b.wait("sync", s_c, k - 1)
            b.wait("sync", s_q, k - 1)
        sl_x.load("sync", k % 2, lambda k=k: nc.sync.dma_start(out=xa[k % 2][:], in_=xin[k]))

    for k in range(KT):
        sl_x.wait("vector", k % 2, 16 * (k // 2 + 1))
        if k >= 2:
            b.wait("vector", s_m, (k - 1) * NBT)
        b.op("vector", lambda k=k: nc.vector.tensor_copy(out=xb[k % 2][:], in_=xa[k % 2][:]))
        b.pub("vector", s_c)

    for k in range(KT):
        sl_x.wait("scalar", k % 2, 16 * (k // 2 + 1))
        if k >= 2:
            b.wait("scalar", s_m, (k - 1) * NBT)
        b.op("scalar", lambda k=k: nc.scalar.activation(out=xq[k % 2][:], in_=xa[k % 2][:], func=AF.Square))
        b.pub("scalar", s_q)

    b.wait("tensor", s_1, 48)
    for k in range(KT):
        b.wait("tensor", s_c, k + 1)
        b.wait("tensor", s_q, k + 1)
        for tb in range(NBT):
            st, sp = (k == 0), (k == KT - 1)
            b.op("tensor", lambda k=k, tb=tb, st=st, sp=sp: nc.tensor.matmul(
                out=stps.ap()[0:1, tb * 512:(tb + 1) * 512], lhsT=ones_sb[:],
                rhs=xb[k % 2][:, tb * 512:(tb + 1) * 512], start=st, stop=sp))
            b.op("tensor", lambda k=k, tb=tb, st=st, sp=sp: nc.tensor.matmul(
                out=stps.ap()[64:65, tb * 512:(tb + 1) * 512], lhsT=ones_sb[:],
                rhs=xq[k % 2][:, tb * 512:(tb + 1) * 512], start=st, stop=sp), s_m)

    b.barrier()
    r1.close()
    xn_res = b.sb(pr, tag + "xn", [P, KT * T], BF16, side="right")

    r2 = b.region()
    r_b = [b.sb(r2, f"{tag}rb{i}", [P, 512], F32) for i in range(2)]
    mr_b = [b.sb(r2, f"{tag}mrb{i}", [P, 512], F32) for i in range(2)]
    xaV = [b.sb(r2, f"{tag}xaV{i}", [P, 512], F32) for i in range(2)]
    xaP = [b.sb(r2, f"{tag}xaP{i}", [P, 512], F32) for i in range(2)]
    t1V = b.sb(r2, f"{tag}t1V", [P, 512], F32)
    t1P = b.sb(r2, f"{tag}t1P", [P, 512], F32)
    mq = b.sb(r2, f"{tag}mq", [1, NBT * 1024], F32)
    msq = b.sb(r2, f"{tag}msq", [1, 512], F32)

    sl_xv = Slots(b, tag + "xv", 2)
    sl_xp = Slots(b, tag + "xp", 2)
    s_mq = b.sem(tag + "mq")
    s_rmv = b.sem(tag + "rmv")
    s_rl = b.sem(tag + "rl")
    s_row = b.sem(tag + "row")
    s_rp = b.sem(tag + "rp")
    s_rev = b.sem(tag + "rev")
    s_av = b.sem(tag + "av")
    s_ap_ = b.sem(tag + "ap")

    def _v_apply(tb):
        for k in range(0, KT, 2):
            nv = tb * 8 + k // 2 + 1
            sl_xv.wait("vector", (nv - 1) % 2, 16 * ((nv - 1) // 2 + 1))
            b.wait("vector", s_rev, tb + 1)
            b.op("vector", lambda tb=tb, k=k, nv=nv: nc.vector.tensor_tensor(
                out=t1V[:], in0=xaV[(nv - 1) % 2][:], in1=r_b[tb % 2][:], op=ALU.mult))
            b.op("vector", lambda tb=tb, k=k: nc.vector.tensor_tensor(
                out=xn_res[:, k * T + tb * 512: k * T + (tb + 1) * 512],
                in0=t1V[:], in1=mr_b[tb % 2][:], op=ALU.subtract), s_av)

    for tb in range(NBT):
        sl = (tb * 512, (tb + 1) * 512)
        b.op("vector", lambda tb=tb, sl=sl: nc.vector.tensor_scalar(
            out=mq[0:1, tb * 1024:tb * 1024 + 512], in0=stps.ap()[0:1, sl[0]:sl[1]],
            scalar1=1.0 / D, scalar2=None, op0=ALU.mult))
        b.op("vector", lambda tb=tb, sl=sl: nc.vector.tensor_scalar(
            out=mq[0:1, tb * 1024 + 512:tb * 1024 + 1024], in0=stps.ap()[64:65, sl[0]:sl[1]],
            scalar1=1.0 / D, scalar2=None, op0=ALU.mult))
    b.pub("vector", s_mq)
    for tb in range(NBT + 1):
        if tb < NBT:
            if tb >= 2:
                b.wait("vector", s_rp, tb - 1)
            b.op("vector", lambda tb=tb: nc.vector.tensor_tensor(
                out=msq[0:1, :], in0=mq[0:1, tb * 1024:tb * 1024 + 512],
                in1=mq[0:1, tb * 1024:tb * 1024 + 512], op=ALU.mult))
            b.op("vector", lambda tb=tb: nc.vector.tensor_tensor(
                out=mq[0:1, tb * 1024 + 512:tb * 1024 + 1024],
                in0=mq[0:1, tb * 1024 + 512:tb * 1024 + 1024], in1=msq[0:1, :], op=ALU.subtract))
            b.pub("vector", s_rmv)
            b.wait("vector", s_rl, tb + 1)
            b.op("vector", lambda tb=tb: nc.vector.tensor_tensor(
                out=rowmr[tb % 2][0:1, :], in0=mq[0:1, tb * 1024:tb * 1024 + 512],
                in1=rowr[tb % 2][0:1, :], op=ALU.mult))
            b.pub("vector", s_row)
        if tb >= 1:
            _v_apply(tb - 1)

    b.wait("scalar", s_1, 48)
    for tb in range(NBT):
        b.wait("scalar", s_rmv, tb + 1)
        if tb >= 2:
            b.wait("scalar", s_rp, tb - 1)
        b.op("scalar", lambda tb=tb: nc.scalar.activation(
            out=rowr[tb % 2][0:1, :], in_=mq[0:1, tb * 1024 + 512:tb * 1024 + 1024],
            func=AF.Ln, bias=cp_sb[0:1, 0:1]))
        b.op("scalar", lambda tb=tb: nc.scalar.activation(
            out=rowr[tb % 2][0:1, :], in_=rowr[tb % 2][0:1, :], func=AF.Exp, scale=-0.5))
        b.pub("scalar", s_rl)
        b.wait("scalar", s_rp, tb + 1)
        if tb >= 2:
            b.wait("scalar", s_av, (tb - 1) * NBT)
            b.wait("scalar", s_ap_, (tb - 1) * NBT)
        b1, b2 = (2 * tb) % 8, (2 * tb + 1) % 8
        b.op("scalar", lambda tb=tb, b1=b1: nc.scalar.copy(
            out=r_b[tb % 2][:], in_=stps.ap()[:, b1 * 512:(b1 + 1) * 512]))
        b.op("scalar", lambda tb=tb, b2=b2: nc.scalar.copy(
            out=mr_b[tb % 2][:], in_=stps.ap()[:, b2 * 512:(b2 + 1) * 512]))
        b.pub("scalar", s_rev)

    b.wait("tensor", s_mq, 1)
    for tb in range(NBT):
        b.wait("tensor", s_row, tb + 1)
        if tb >= 4:
            b.wait("tensor", s_rev, tb - 3)
        b1, b2 = (2 * tb) % 8, (2 * tb + 1) % 8
        b.op("tensor", lambda tb=tb, b1=b1: nc.tensor.matmul(
            out=stps.ap()[:, b1 * 512:(b1 + 1) * 512], lhsT=oner_sb[:],
            rhs=rowr[tb % 2][0:1, :], start=True, stop=True))
        b.op("tensor", lambda tb=tb, b2=b2: nc.tensor.matmul(
            out=stps.ap()[:, b2 * 512:(b2 + 1) * 512], lhsT=oner_sb[:],
            rhs=rowmr[tb % 2][0:1, :], start=True, stop=True), s_rp)

    nv = np_ = 0
    for tb in range(NBT):
        for k in range(KT):
            if k % 2 == 0:
                nv += 1
                if nv > 2:
                    b.wait("sync", s_av, nv - 2)
                sl_xv.load("sync", (nv - 1) % 2, lambda tb=tb, k=k, nv=nv: nc.sync.dma_start(
                    out=xaV[(nv - 1) % 2][:], in_=xin[k][:, tb * 512:(tb + 1) * 512]))
            else:
                np_ += 1
                if np_ > 2:
                    b.wait("sync", s_ap_, np_ - 2)
                sl_xp.load("sync", (np_ - 1) % 2, lambda tb=tb, k=k, np_=np_: nc.sync.dma_start(
                    out=xaP[(np_ - 1) % 2][:], in_=xin[k][:, tb * 512:(tb + 1) * 512]))

    for tb in range(NBT):
        for k in range(1, KT, 2):
            np2 = tb * 8 + (k - 1) // 2 + 1
            sl_xp.wait("gpsimd", (np2 - 1) % 2, 16 * ((np2 - 1) // 2 + 1))
            b.wait("gpsimd", s_rev, tb + 1)
            b.op("gpsimd", lambda tb=tb, k=k, np2=np2: nc.gpsimd.tensor_tensor(
                out=t1P[:], in0=xaP[(np2 - 1) % 2][:], in1=r_b[tb % 2][:], op=ALU.mult))
            b.op("gpsimd", lambda tb=tb, k=k: nc.gpsimd.tensor_tensor(
                out=xn_res[:, k * T + tb * 512: k * T + (tb + 1) * 512],
                in0=t1P[:], in1=mr_b[tb % 2][:], op=ALU.subtract), s_ap_)

    b.barrier()
    r2.close()
    r0.close()
    return xn_res


def emit_l0a_qkv(b, c, lay, xn_res, phase_reg):
    nc = b.nc
    r = b.region()
    wfp = [b.sb(r, f"a2wf{i}", [P, KT * P], F32) for i in range(2)]
    wb = [b.sb(r, f"a2wb{i}", [P, KT * P], BF16) for i in range(2)]
    qtmp = [b.sb(r, f"a2qt{i}", [P, 512], BF16) for i in range(2)]
    rt1 = b.sb(r, "a2rt1", [P, 512], F32)
    rt2 = b.sb(r, "a2rt2", [P, 512], F32)
    orow = b.sb(r, "a2orow", [P, T], BF16)
    cos2 = b.sb(r, "a2cos2", [P, T], F32)
    sin2 = b.sb(r, "a2sin2", [P, T], F32)
    swp = b.sb(r, "a2swp", [P, P], BF16)
    ps = [b.ps(r, f"a2ps{i}", [P, 512]) for i in range(6)]
    psw = [b.ps(r, f"a2psw{i}", [P, 512]) for i in range(2)]

    sl_wl = Slots(b, "a2wl", 2)
    sl_out = Slots(b, "a2out", NBT, persistent=True)
    s_wc = b.sem("a2wc")
    s_pm = b.sem("a2pm")
    s_sw = b.sem("a2sw")
    s_ev = b.sem("a2ev")
    s_evd = b.sem("a2evd")
    s_rotq = b.sem("a2rotq")
    s_rotv = b.sem("a2rotv")
    s_cst = b.sem("a2cst")

    mk = b.dout if DBG == "qkv" else b.dscratch
    q_sc = mk("q_sc", [NC0, P, H * P], BF16)
    k_sc = mk("k_sc", [NC0, P, H * P], BF16)
    v_sc = mk("v_sc", [NC0, P, H * P], BF16)
    mats = [(b.din("l0_wq", lay["l0_wq"]), q_sc, True),
            (b.din("l0_wk", lay["l0_wk"]), k_sc, True),
            (b.din("l0_wv", lay["l0_wv"]), v_sc, False)]

    b.op("sync", lambda: nc.sync.dma_start(out=cos2[:], in_=b.din("cos2", c["cos2"])), s_cst, 16)
    b.op("sync", lambda: nc.sync.dma_start(out=sin2[:], in_=b.din("sin2pm", c["sin2pm"])), s_cst, 16)
    b.op("sync", lambda: nc.sync.dma_start(out=swp[:], in_=b.din("swapmat", c["swapmat"])), s_cst, 16)

    for jj in range(3 * H):
        if jj >= 2:
            b.wait("sync", s_wc, jj - 1)
        wa, _, _ = mats[jj // H]
        sl_wl.load("sync", jj % 2, lambda jj=jj, wa=wa: nc.sync.dma_start(
            out=wfp[jj % 2][:], in_=wa[jj % H]))

    b.wait("vector", s_cst, 48)
    for jj in range(3 * H):
        rot = mats[jj // H][2]
        sl_wl.wait("vector", jj % 2, 16 * (jj // 2 + 1))
        if jj >= 2:
            b.wait("vector", s_pm, (jj - 1) * 8)
        b.op("vector", lambda jj=jj: nc.vector.tensor_copy(out=wb[jj % 2][:], in_=wfp[jj % 2][:]))
        b.pub("vector", s_wc)
        if rot:
            for tb in range(NBT):
                g = jj * 8 + tb
                b.wait("vector", s_evd, g + 1)
                b.wait("vector", s_sw, g + 1)
                if jj >= 1:
                    sl_out.wait("vector", tb, 16 * jj)
                sl = (tb * 512, (tb + 1) * 512)
                b.op("vector", lambda g=g, sl=sl: nc.vector.tensor_tensor(
                    out=rt1[:], in0=qtmp[g % 2][:], in1=cos2[:, sl[0]:sl[1]], op=ALU.mult))
                b.op("vector", lambda g=g, sl=sl: nc.vector.tensor_tensor(
                    out=rt2[:], in0=psw[g % 2].ap()[:], in1=sin2[:, sl[0]:sl[1]], op=ALU.mult))
                b.op("vector", lambda sl=sl: nc.vector.tensor_tensor(
                    out=orow[:, sl[0]:sl[1]], in0=rt1[:], in1=rt2[:], op=ALU.add))
                b.pub("vector", s_rotq)

    for jj in range(3 * H):
        rot = mats[jj // H][2]
        for tb in range(NBT):
            g = jj * 8 + tb
            b.wait("scalar", s_pm, g + 1)
            if rot:
                if g >= 2:
                    b.wait("scalar", s_rotq, g - 1)
                    b.wait("scalar", s_sw, g - 1)
                b.op("scalar", lambda g=g: nc.scalar.copy(
                    out=qtmp[g % 2][:], in_=ps[g % 6].ap()[:]), s_ev)
                b.pub("scalar", s_evd)
            else:
                if jj >= 1:
                    sl_out.wait("scalar", tb, 16 * jj)
                b.op("scalar", lambda g=g, tb=tb: nc.scalar.copy(
                    out=orow[:, tb * 512:(tb + 1) * 512], in_=ps[g % 6].ap()[:]), s_ev)
                b.pub("scalar", s_rotv)

    b.wait("tensor", s_cst, 48)
    pend_swaps = []
    for jj in range(3 * H):
        rot = mats[jj // H][2]
        b.wait("tensor", s_wc, jj + 1)
        for tb in range(NBT):
            g = jj * 8 + tb
            if g >= 6:
                b.wait("tensor", s_ev, g - 5)
            for k in range(KT):
                st, sp = (k == 0), (k == KT - 1)
                b.op("tensor", lambda jj=jj, tb=tb, k=k, g=g, st=st, sp=sp: nc.tensor.matmul(
                    out=ps[g % 6].ap()[:], lhsT=wb[jj % 2][:, k * P:(k + 1) * P],
                    rhs=xn_res[:, k * T + tb * 512:k * T + (tb + 1) * 512],
                    start=st, stop=sp), s_pm if sp else None)
            if rot:
                pend_swaps.append(g)
            while pend_swaps and (pend_swaps[0] <= g - 2 or tb == NBT - 1):
                gs = pend_swaps.pop(0)
                b.wait("tensor", s_evd, gs + 1)
                if gs >= 2:
                    b.wait("tensor", s_rotq, gs - 1)
                b.op("tensor", lambda gs=gs: nc.tensor.matmul(
                    out=psw[gs % 2].ap()[:], lhsT=swp[:], rhs=qtmp[gs % 2][:],
                    start=True, stop=True), s_sw)
    assert not pend_swaps

    for jj in range(3 * H):
        _, dst, rot = mats[jj // H]
        j = jj % H
        for tb in range(NBT):
            g = jj * 8 + tb
            if rot:
                b.wait("gpsimd", s_rotq, g + 1)
            else:
                b.wait("gpsimd", s_rotv, (jj - 2 * H) * 8 + tb + 1)
            sl_out.load("gpsimd", tb, lambda dst=dst, j=j, tb=tb: nc.gpsimd.dma_start(
                out=dst[tb * 4:(tb + 1) * 4, :, j * P:(j + 1) * P].rearrange("c p m -> p c m"),
                in_=orow.ap()[:, tb * 512:(tb + 1) * 512].rearrange("p (c m) -> p c m", c=4)))
    for tb in range(NBT):
        sl_out.wait("gpsimd", tb, 16 * 3 * H)

    b.barrier()
    r.close()


def emit_l0a_scan(b, c, lay, xin, xout):
    nc = b.nc
    r = b.region()
    wob = b.sb(r, "a3wob", [P, H * KT * P], BF16)
    kv = b.sb(r, "a3kv", [P, H * P], F32)
    dm_sb = b.sb(r, "a3dm", [P, H * P], F32)
    qdb_sb = b.sb(r, "a3qdb", [P, H * P], F32)
    kdc_sb = b.sb(r, "a3kdc", [P, H], F32)
    rc_sb = b.sb(r, "a3rc", [P, H], F32)
    rmsw_sb = b.sb(r, "a3rmsw", [P, KT], F32)
    ids = b.sb(r, "a3ids", [P, P], BF16)
    ones_sb = b.sb(r, "a3ones", [P, 1], BF16)
    oner_sb = b.sb(r, "a3oner", [1, P], F32)
    cp_sb = b.sb(r, "a3cp", [P, 8], F32)
    wfp = [b.sb(r, "a3wf0", [P, KT * P], F32)]
    qch = [b.sb(r, f"a3q{i}", [P, H * P], BF16) for i in range(2)]
    kch = [b.sb(r, f"a3k{i}", [P, H * P], BF16) for i in range(2)]
    vch = [b.sb(r, f"a3v{i}", [P, H * P], BF16) for i in range(2)]
    ktd = b.sb(r, "a3ktd", [P, P], BF16)
    vtok = b.sb(r, "a3vtok", [P, P], BF16)
    scm = b.sb(r, "a3scm", [P, P], BF16)
    qd = b.sb(r, "a3qd", [P, P], F32)
    dkv = b.sb(r, "a3dkv", [P, P], F32)
    afm = b.sb(r, "a3afm", [P, H * P], F32)
    asq = b.sb(r, "a3asq", [P, P], BF16)
    rowb = b.sb(r, "a3rowb", [1, P], F32)
    rb_c = b.sb(r, "a3rbc", [P, P], F32)
    grbuf = b.sb(r, "a3gr", [P, H, 512], BF16)
    ywo = b.sb(r, "a3ywo", [P, KT, 512], F32)
    yb = [b.sb(r, f"a3yb{i}", [P, 512], BF16) for i in range(2)]
    ysq = [b.sb(r, f"a3ysq{i}", [P, 512], BF16) for i in range(2)]
    row2 = b.sb(r, "a3row2", [1, 2 * 512], F32)
    rowt = b.sb(r, "a3rowt", [1, 2 * 512], F32)
    rb2 = b.sb(r, "a3rb2", [P, 512], F32)
    mrb2 = b.sb(r, "a3mrb2", [P, 512], F32)
    xoldV = [b.sb(r, f"a3xoV{i}", [P, 512], F32) for i in range(2)]
    xoldP = [b.sb(r, f"a3xoP{i}", [P, 512], F32) for i in range(2)]
    ps_tk = b.ps(r, "a3ptk", [P, 1024], BF16)
    ps_tv = b.ps(r, "a3ptv", [P, 1024], BF16)
    ps_sc = b.ps(r, "a3psc", [P, 512], F32)
    ps_o = b.ps(r, "a3po", [P, 512], F32)
    ps_kv = b.ps(r, "a3pkv", [P, 512], F32)
    ps_rm = b.ps(r, "a3prm", [P, 512], F32)
    ps_wo = [b.ps(r, f"a3pw{i}", [P, 512], F32) for i in range(2)]

    q_sc, k_sc, v_sc = b._dram["q_sc"], b._dram["k_sc"], b._dram["v_sc"]

    s_ldc = b.sem("a3ldc")
    sl_wol = Slots(b, "a3wol", 1)
    s_wc = b.sem("a3wc")
    sl_ch = Slots(b, "a3ch", 2)
    s_t1 = b.sem("a3t1")
    s_t2 = b.sem("a3t2")
    s_m1 = b.sem("a3m1")
    s_o = b.sem("a3o")
    s_m5 = b.sem("a3m5")
    s_m6 = b.sem("a3m6")
    s_vt = b.sem("a3vt")
    s_dkv = b.sem("a3dkv")
    s_aq = b.sem("a3aq")
    s_scm = b.sem("a3scm")
    s_qd = b.sem("a3qd")
    s_ktd = b.sem("a3ktd")
    s_kv = b.sem("a3kv")
    s_rr = b.sem("a3rr")
    s_rp = b.sem("a3rp")
    s_rb = b.sem("a3rb")
    s_gr = b.sem("a3gr")
    s_wo = b.sem("a3wo")
    s_ye = b.sem("a3ye")
    s_cb = b.sem("a3cb")
    s_sq2 = b.sem("a3sq2")
    s_plm = b.sem("a3plm")
    s_r2 = b.sem("a3r2")
    s_rp2 = b.sem("a3rp2")
    s_rbe = b.sem("a3rbe")
    sl_xoV = Slots(b, "a3xoV", 2)
    sl_xoP = Slots(b, "a3xoP", 2)
    sl_xwV = Slots(b, "a3xwV", 2)
    sl_xwP = Slots(b, "a3xwP", 2)
    s_av = b.sem("a3av")
    s_ap = b.sem("a3ap")

    b.op("sync", lambda: nc.sync.dma_start(out=ids[:], in_=b.din("ident", c["ident"])), s_ldc, 16)
    b.op("sync", lambda: nc.sync.dma_start(out=ones_sb[:], in_=b.din("ones_bf", c["ones_bf"])), s_ldc, 16)
    b.op("sync", lambda: nc.sync.dma_start(out=oner_sb[:], in_=b.din("ones_row_f32", c["ones_row_f32"])), s_ldc, 16)
    b.op("sync", lambda: nc.sync.dma_start(out=cp_sb[:], in_=b.din("cpack", c["cpack"])), s_ldc, 16)
    b.op("sync", lambda: nc.sync.dma_start(
        out=dm_sb.ap().rearrange("p (h m) -> p h m", h=H),
        in_=b.din("dmaskT", c["dmaskT"]).rearrange("h p m -> p h m")), s_ldc, 16)
    b.op("sync", lambda: nc.sync.dma_start(
        out=qdb_sb.ap().rearrange("p (h m) -> p h m", h=H),
        in_=b.din("qdec_b", c["qdec_b"]).rearrange("h p m -> p h m")), s_ldc, 16)
    b.op("sync", lambda: nc.sync.dma_start(
        out=kdc_sb[:], in_=b.din("kdec_pc", c["kdec_pc"])), s_ldc, 16)
    b.op("sync", lambda: nc.sync.dma_start(
        out=rc_sb[:], in_=b.din("rC_pc", c["rC_pc"])), s_ldc, 16)
    b.op("sync", lambda: nc.sync.dma_start(
        out=rmsw_sb[:], in_=b.din("l0_rmsw", lay["l0_rmsw"])), s_ldc, 16)
    wo_d = b.din("l0_wo", lay["l0_wo"])
    for j in range(H):
        if j >= 1:
            b.wait("sync", s_wc, j)
        sl_wol.load("sync", 0, lambda j=j: nc.sync.dma_start(out=wfp[0][:], in_=wo_d[j]))
    b.wait("vector", s_ldc, 16 * 9)
    for j in range(H):
        sl_wol.wait("vector", 0, 16 * (j + 1))
        for k in range(KT):
            b.op("vector", lambda j=j, k=k: nc.vector.tensor_scalar(
                out=wob[:, (j * KT + k) * P:(j * KT + k + 1) * P],
                in0=wfp[0][:, k * P:(k + 1) * P],
                scalar1=rmsw_sb[:, k:k + 1], scalar2=None, op0=ALU.mult))
        b.pub("vector", s_wc)
    b.op("gpsimd", lambda: nc.gpsimd.memset(kv[:], 0.0))
    b.pub("gpsimd", s_kv)

    for ci in range(NC0):
        if ci >= 2:
            b.wait("sync", s_m5, (ci - 1) * H)
            b.wait("sync", s_ktd, (ci - 1) * H)
        sl_ch.load("sync", ci % 2, lambda ci=ci: nc.sync.dma_start(out=qch[ci % 2][:], in_=q_sc[ci]))
        sl_ch.uses[ci % 2] -= 1
        sl_ch.load("sync", ci % 2, lambda ci=ci: nc.sync.dma_start(out=kch[ci % 2][:], in_=k_sc[ci]))
        sl_ch.uses[ci % 2] -= 1
        sl_ch.load("sync", ci % 2, lambda ci=ci: nc.sync.dma_start(out=vch[ci % 2][:], in_=v_sc[ci]))
        if ci % 4 == 3:
            g = ci // 4

            def _ld(j, g=g):
                if j % 2 == 0:
                    nb = g * 8 + j // 2
                    if nb >= 2:
                        sl_xwV.wait("sync", nb % 2, 16 * (nb // 2))
                    sl_xoV.load("sync", nb % 2, lambda g=g, j=j, nb=nb: nc.sync.dma_start(
                        out=xoldV[nb % 2][:], in_=xin[j][:, g * 512:(g + 1) * 512]))
                else:
                    nb = g * 8 + (j - 1) // 2
                    if nb >= 2:
                        sl_xwP.wait("sync", nb % 2, 16 * (nb // 2))
                    sl_xoP.load("sync", nb % 2, lambda g=g, j=j, nb=nb: nc.sync.dma_start(
                        out=xoldP[nb % 2][:], in_=xin[j][:, g * 512:(g + 1) * 512]))

            def _st(j, g=g):
                if j % 2 == 0:
                    nb = g * 8 + j // 2
                    b.wait("sync", s_av, nb + 1)
                    sl_xwV.load("sync", nb % 2, lambda g=g, j=j, nb=nb: nc.sync.dma_start(
                        out=xout[j][:, g * 512:(g + 1) * 512], in_=xoldV[nb % 2][:]))
                else:
                    nb = g * 8 + (j - 1) // 2
                    b.wait("sync", s_ap, nb + 1)
                    sl_xwP.load("sync", nb % 2, lambda g=g, j=j, nb=nb: nc.sync.dma_start(
                        out=xout[j][:, g * 512:(g + 1) * 512], in_=xoldP[nb % 2][:]))

            for j in range(KT):
                if j >= 4:
                    _st(j - 4)
                _ld(j)
            for j in range(KT - 4, KT):
                _st(j)

    def hs(h):
        return slice(h * P, (h + 1) * P)

    b.wait("tensor", s_ldc, 16 * 9)
    b.wait("tensor", s_wc, H)
    for ci in range(NC0):
        g, cg = ci // 4, ci % 4
        sl_ch.wait("tensor", ci % 2, 48 * (ci // 2 + 1))
        if cg == 0 and ci >= 4:
            b.wait("tensor", s_rbe, ci // 4)
            b.wait("tensor", s_r2, 3 * (ci // 4))
        for h in range(H):
            n = ci * H + h
            if n >= 1:
                b.wait("tensor", s_ktd, n)
                b.wait("tensor", s_vt, n)
            b.op("tensor", lambda ci=ci, h=h: nc.tensor.transpose(
                out=ps_tk.ap()[:, 0:P], in_=kch[ci % 2][:, hs(h)], identity=ids[:]), s_t1)
            b.op("tensor", lambda ci=ci, h=h: nc.tensor.transpose(
                out=ps_tv.ap()[:, 0:P], in_=vch[ci % 2][:, hs(h)], identity=ids[:]), s_t2)
            if n >= 1:
                b.wait("tensor", s_ktd, n)
            b.op("tensor", lambda ci=ci, h=h: nc.tensor.matmul(
                out=ps_sc.ap()[:, 0:P], lhsT=kch[ci % 2][:, hs(h)],
                rhs=qch[ci % 2][:, hs(h)], start=True, stop=True), s_m1)
            b.wait("tensor", s_kv, (n + 2 - H) if ci >= 1 else 1)
            b.wait("tensor", s_ktd, n + 1)
            if n >= 1:
                b.wait("tensor", s_aq, n)
            b.op("tensor", lambda h=h: nc.tensor.matmul(
                out=ps_o.ap()[:, 0:P], lhsT=kv[:, hs(h)], rhs=qd[:], start=True, stop=False))
            b.wait("tensor", s_vt, n + 1)
            b.op("tensor", lambda: nc.tensor.matmul(
                out=ps_o.ap()[:, 0:P], lhsT=vtok[:], rhs=scm[:], start=False, stop=True), s_o)
            if n >= 1:
                b.wait("tensor", s_kv, n + 1)
            b.op("tensor", lambda: nc.tensor.matmul(
                out=ps_kv.ap()[:, 0:P], lhsT=ktd[:], rhs=vtok[:], start=True, stop=True), s_m5)
            b.wait("tensor", s_aq, n + 1)
            if h == 0 and ci >= 1:
                b.wait("tensor", s_rp, ci)
                b.wait("tensor", s_rb, ci)
            b.op("tensor", lambda h=h: nc.tensor.matmul(
                out=ps_rm.ap()[0:1, 0:P], lhsT=ones_sb[:], rhs=asq[:],
                start=(h == 0), stop=(h == H - 1)), s_m6)
        b.wait("tensor", s_rr, ci + 1)
        b.op("tensor", lambda: nc.tensor.matmul(
            out=ps_rm.ap()[:, P:2 * P], lhsT=oner_sb[:], rhs=rowb[0:1, :],
            start=True, stop=True), s_rp)
        if cg == 3:
            b.wait("tensor", s_gr, (ci + 1) * H)
            for j in range(KT):
                if g >= 1 or j >= 2:
                    b.wait("tensor", s_ye, g * KT + j - 1)
                for h in range(H):
                    b.op("tensor", lambda j=j, h=h, g=g: nc.tensor.matmul(
                        out=ps_wo[j % 2].ap()[:], lhsT=wob[:, (j * KT + h) * P:(j * KT + h + 1) * P],
                        rhs=grbuf.ap()[:, h, :], start=(h == 0), stop=(h == H - 1)),
                        s_wo if h == H - 1 else None, inc=H if h == H - 1 else 1)
            for j in range(KT):
                b.wait("tensor", s_cb, g * KT + j + 1)
                b.wait("tensor", s_sq2, g * KT + j + 1)
                b.op("tensor", lambda j=j: nc.tensor.matmul(
                    out=ps_sc.ap()[0:1, 0:512], lhsT=ones_sb[:], rhs=yb[j % 2][:],
                    start=(j == 0), stop=(j == KT - 1)))
                b.op("tensor", lambda j=j: nc.tensor.matmul(
                    out=ps_sc.ap()[64:65, 0:512], lhsT=ones_sb[:], rhs=ysq[j % 2][:],
                    start=(j == 0), stop=(j == KT - 1)), s_plm)
            b.wait("tensor", s_aq, (ci + 1) * H)
            b.wait("tensor", s_kv, (ci + 1) * H + 1)
            b.wait("tensor", s_r2, 3 * g + 3)
            if g >= 1:
                b.wait("tensor", s_rbe, g)
            b.op("tensor", lambda: nc.tensor.matmul(
                out=ps_o.ap()[:], lhsT=oner_sb[:], rhs=row2[0:1, 0:512], start=True, stop=True))
            b.op("tensor", lambda: nc.tensor.matmul(
                out=ps_kv.ap()[:], lhsT=oner_sb[:], rhs=row2[0:1, 512:1024], start=True, stop=True), s_rp2, inc=2)

    b.wait("vector", s_ldc, 16 * 9)
    for ci in range(NC0):
        g, cg = ci // 4, ci % 4
        sl_ch.wait("vector", ci % 2, 48 * (ci // 2 + 1))
        for h in range(H):
            n = ci * H + h
            b.wait("vector", s_t1, n + 1)
            b.wait("vector", s_m5, n)
            b.op("vector", lambda h=h: nc.vector.tensor_scalar(
                out=ktd[:], in0=ps_tk.ap()[:, 0:P], scalar1=kdc_sb[:, h:h + 1],
                scalar2=None, op0=ALU.mult))
            b.wait("vector", s_m1, n + 1)
            b.wait("vector", s_o, n)
            b.op("vector", lambda h=h: nc.vector.tensor_tensor(
                out=scm[:], in0=ps_sc.ap()[:, 0:P], in1=dm_sb[:, hs(h)], op=ALU.mult))
            b.op("vector", lambda ci=ci, h=h: nc.vector.tensor_tensor(
                out=qd[:], in0=qch[ci % 2][:, hs(h)], in1=qdb_sb[:, hs(h)], op=ALU.mult))
            b.pub("vector", s_ktd)
            b.wait("vector", s_m5, n + 1)
            b.op("vector", lambda h=h: nc.vector.scalar_tensor_tensor(
                out=kv[:, hs(h)], in0=kv[:, hs(h)], scalar=rc_sb[:, h:h + 1],
                in1=ps_kv.ap()[:, 0:P], op0=ALU.mult, op1=ALU.add))
            b.pub("vector", s_kv)
        b.wait("vector", s_rb, ci + 1)
        b.wait("vector", s_aq, (ci + 1) * H)
        if ci >= 4:
            b.wait("vector", s_wo, (ci // 4) * KT * H)
        for h in range(H):
            b.op("vector", lambda h=h, cg=cg: nc.vector.tensor_tensor(
                out=grbuf.ap()[:, h, cg * P:(cg + 1) * P], in0=afm[:, hs(h)], in1=rb_c[:], op=ALU.mult))
        b.pub("vector", s_gr, inc=H)
        if cg == 3:
            for j in range(KT):
                b.wait("vector", s_ye, g * KT + j + 1)
                if g * KT + j >= 2:
                    b.wait("vector", s_plm, g * KT + j - 1)
                b.op("vector", lambda j=j: nc.vector.tensor_copy(out=yb[j % 2][:], in_=ywo.ap()[:, j, :]))
                b.pub("vector", s_cb)
            b.wait("vector", s_plm, (g + 1) * KT)
            b.op("vector", lambda: nc.vector.tensor_scalar(
                out=rowt[0:1, 0:512], in0=ps_sc.ap()[0:1, 0:512], scalar1=1.0 / D, scalar2=None, op0=ALU.mult))
            b.op("vector", lambda: nc.vector.tensor_scalar(
                out=rowt[0:1, 512:1024], in0=ps_sc.ap()[64:65, 0:512], scalar1=1.0 / D, scalar2=None, op0=ALU.mult))
            b.op("vector", lambda: nc.vector.tensor_tensor(
                out=row2[0:1, 512:1024], in0=rowt[0:1, 0:512], in1=rowt[0:1, 0:512], op=ALU.mult))
            b.op("vector", lambda: nc.vector.tensor_tensor(
                out=row2[0:1, 512:1024], in0=rowt[0:1, 512:1024], in1=row2[0:1, 512:1024], op=ALU.subtract))
            b.pub("vector", s_r2)
            b.wait("vector", s_r2, 3 * g + 2)
            b.op("vector", lambda: nc.vector.tensor_tensor(
                out=row2[0:1, 512:1024], in0=rowt[0:1, 0:512], in1=row2[0:1, 0:512], op=ALU.mult))
            b.pub("vector", s_r2)
            for j in range(0, KT, 2):
                nb = g * 8 + j // 2
                sl_xoV.wait("vector", nb % 2, 16 * (nb // 2 + 1))
                b.wait("vector", s_rbe, g + 1)
                if nb >= 2:
                    sl_xwV.wait("vector", nb % 2, 16 * (nb // 2))
                b.op("vector", lambda j=j: nc.vector.tensor_tensor(
                    out=ywo.ap()[:, j, :], in0=ywo.ap()[:, j, :], in1=rb2[:], op=ALU.mult))
                b.op("vector", lambda j=j: nc.vector.tensor_tensor(
                    out=ywo.ap()[:, j, :], in0=ywo.ap()[:, j, :], in1=mrb2[:], op=ALU.subtract))
                b.op("vector", lambda j=j, nb=nb: nc.vector.tensor_tensor(
                    out=xoldV[nb % 2][:], in0=ywo.ap()[:, j, :], in1=xoldV[nb % 2][:], op=ALU.add))
                b.pub("vector", s_av)

    if DBG == "scan":
        dbg_attn = b.dout("dbg_attn", [NC0, P, H * P], F32)
        s_dbga = b.sem("a3dbga")
        dbg_ywo = b.dout("dbg_ywo", [NBT, P, KT * 512], F32)
        s_dbgy = b.sem("a3dbgy")
    b.wait("scalar", s_ldc, 16 * 9)
    for ci in range(NC0):
        g, cg = ci // 4, ci % 4
        for h in range(H):
            n = ci * H + h
            b.wait("scalar", s_t2, n + 1)
            b.wait("scalar", s_m5, n)
            b.op("scalar", lambda: nc.scalar.copy(out=vtok[:], in_=ps_tv.ap()[:, 0:P]))
            b.pub("scalar", s_vt)
            b.wait("scalar", s_o, n + 1)
            if DBG == "scan" and ci >= 1 and h == 0:
                b.wait("scalar", s_dbga, 17 * ci)
            if ci >= 1:
                b.wait("scalar", s_gr, n - H + 1)
            b.wait("scalar", s_m6, n)
            b.op("scalar", lambda h=h: nc.scalar.copy(out=afm[:, hs(h)], in_=ps_o.ap()[:, 0:P]))
            b.op("scalar", lambda: nc.scalar.activation(out=asq[:], in_=ps_o.ap()[:, 0:P], func=AF.Square))
            b.pub("scalar", s_aq)
        if DBG == "scan":
            b.pub("scalar", s_dbga)
            b.op("scalar", lambda ci=ci: nc.scalar.dma_start(
                out=dbg_attn[ci], in_=afm[:]), s_dbga, 16)
        b.wait("scalar", s_m6, (ci + 1) * H)
        b.wait("scalar", s_rp, ci)
        b.op("scalar", lambda: nc.scalar.activation(
            out=rowb[0:1, :], in_=ps_rm.ap()[0:1, 0:P], func=AF.Ln, scale=1.0 / D, bias=cp_sb[0:1, 0:1]))
        b.op("scalar", lambda: nc.scalar.activation(
            out=rowb[0:1, :], in_=rowb[0:1, :], func=AF.Exp, scale=-0.5))
        b.pub("scalar", s_rr)
        b.wait("scalar", s_rp, ci + 1)
        b.wait("scalar", s_gr, ci * H)
        b.op("scalar", lambda: nc.scalar.copy(out=rb_c[:], in_=ps_rm.ap()[:, P:2 * P]))
        b.pub("scalar", s_rb)
        if cg == 3:
            for j in range(KT):
                b.wait("scalar", s_wo, g * KT * H + (j + 1) * H)
                if g >= 1:
                    b.wait("scalar", s_av, g * 8)
                    b.wait("scalar", s_ap, g * 8)
                b.op("scalar", lambda j=j: nc.scalar.copy(
                    out=ywo.ap()[:, j, :], in_=ps_wo[j % 2].ap()[:]))
                b.pub("scalar", s_ye)
            for j in range(KT):
                if g * KT + j >= 2:
                    b.wait("scalar", s_plm, g * KT + j - 1)
                b.op("scalar", lambda j=j: nc.scalar.activation(
                    out=ysq[j % 2][:], in_=ywo.ap()[:, j, :], func=AF.Square))
                b.pub("scalar", s_sq2)
            if DBG == "scan":
                b.pub("scalar", s_dbgy)
                b.op("scalar", lambda g=g: nc.scalar.dma_start(
                    out=dbg_ywo[g], in_=ywo.ap().rearrange("p j m -> p (j m)")), s_dbgy, 16)
            b.wait("scalar", s_r2, 3 * g + 1)
            b.op("scalar", lambda: nc.scalar.activation(
                out=row2[0:1, 0:512], in_=row2[0:1, 512:1024], func=AF.Ln, bias=cp_sb[0:1, 0:1]))
            b.op("scalar", lambda: nc.scalar.activation(
                out=row2[0:1, 0:512], in_=row2[0:1, 0:512], func=AF.Exp, scale=-0.5))
            b.pub("scalar", s_r2)
            b.wait("scalar", s_rp2, 2 * (g + 1))
            b.op("scalar", lambda: nc.scalar.copy(out=rb2[:], in_=ps_o.ap()[:]))
            b.op("scalar", lambda: nc.scalar.copy(out=mrb2[:], in_=ps_kv.ap()[:]))
            b.pub("scalar", s_rbe)

    if DBG == "scan":
        b.wait("scalar", s_dbga, 17 * NC0)
        b.wait("scalar", s_dbgy, 17 * NBT)

    for ci in range(NC0):
        g, cg = ci // 4, ci % 4
        if cg == 3:
            for j in range(1, KT, 2):
                nb = g * 8 + (j - 1) // 2
                sl_xoP.wait("gpsimd", nb % 2, 16 * (nb // 2 + 1))
                b.wait("gpsimd", s_rbe, g + 1)
                if nb >= 2:
                    sl_xwP.wait("gpsimd", nb % 2, 16 * (nb // 2))
                b.op("gpsimd", lambda j=j: nc.gpsimd.tensor_tensor(
                    out=ywo.ap()[:, j, :], in0=ywo.ap()[:, j, :], in1=rb2[:], op=ALU.mult))
                b.op("gpsimd", lambda j=j: nc.gpsimd.tensor_tensor(
                    out=ywo.ap()[:, j, :], in0=ywo.ap()[:, j, :], in1=mrb2[:], op=ALU.subtract))
                b.op("gpsimd", lambda j=j, nb=nb: nc.gpsimd.tensor_tensor(
                    out=xoldP[nb % 2][:], in0=ywo.ap()[:, j, :], in1=xoldP[nb % 2][:], op=ALU.add))
                b.pub("gpsimd", s_ap)

    for i in range(2):
        sl_xwV.wait("sync", i, 16 * (NBT * KT // 4))
        sl_xwP.wait("sync", i, 16 * (NBT * KT // 4))
    b.barrier()
    r.close()



DBG = None


def ph_l0a(b, c, lay, xin, xout):
    nc = b.nc
    pr = b.region()
    xn = emit_preln(b, "a1", xin, c, pr)
    if DBG == "preln":
        dbg = b.dout("dbg_xn", [KT, P, T], BF16)
        sd = b.sem("dbgs")
        for k in range(KT):
            b.op("sync", lambda k=k: nc.sync.dma_start(
                out=dbg[k], in_=xn[:, k * T:(k + 1) * T]), sd, 16)
        b.wait("sync", sd, 16 * KT)
        pr.close()
        return
    emit_l0a_qkv(b, c, lay, xn, pr)
    pr.close()
    if DBG == "qkv":
        return
    emit_l0a_scan(b, c, lay, xin, xout)


PHASES = {"l0a": ph_l0a}
GROUPS = [["l0a", "l0f", "l1a", "l1f"]]
STATE_NAMES = {"l0a": "x1", "l0f": "x2", "l1a": "x3", "l1f": "x4"}


def _register_phases():
    PHASES["l0f"] = ph_l0f
    for nm, fn in list(globals().items()):
        if nm.startswith("ph_") and callable(fn):
            PHASES[nm[3:]] = fn


def build_group(group, c, lay, xin_arrs):
    _register_phases()
    """group: list of phase names. xin_arrs: per-core list of [KT,P,T] f32."""
    b = Build()
    xcur = b.din("xin_" + group[0], xin_arrs)
    for i, ph in enumerate(group):
        last = i == len(group) - 1
        nm = STATE_NAMES[ph]
        if last:
            xnext = b.dout(nm, [KT, P, T], F32)
        else:
            xnext = b.dscratch(nm, [KT, P, T], F32)
        PHASES[ph](b, c, lay, xcur, xnext)
        if not last:
            b.phase_sems_reset()
        xcur = xnext
    b.finalize()
    return b, STATE_NAMES[group[-1]]


def kernel(**inputs):
    c = mk_consts()
    lay = mk_layouts(inputs)
    state = lay["x0"]
    for group in GROUPS:
        b, outname = build_group(group, c, lay, state)
        in_maps = []
        for core in range(N_CORES):
            m = {}
            for name, arr in b.host.items():
                m[name] = arr[core] if isinstance(arr, list) else arr
            in_maps.append(m)
        res = run_bass_kernel_spmd(b.nc, in_maps, core_ids=list(range(N_CORES)))
        state = [np.ascontiguousarray(res.results[core][outname]) for core in range(N_CORES)]
    out = np.empty((N_CORES, T, D), dtype=np.float32)
    for core in range(N_CORES):
        out[core] = state[core].reshape(D, T).T
    return out



def emit_postln_group(b, tag, E, g, gt, ysrc, xin, xout, bufs, nofj=KT, GW=512, yready=None):
    nc = b.nc
    ones_sb, oner_sb, cp_sb = bufs["ones"], bufs["oner"], bufs["cp"]
    yb, ysq, row2, rowt = bufs["yb"], bufs["ysq"], bufs["row2"], bufs["rowt"]
    rb2, mrb2, xoldV, xoldP = bufs["rb2"], bufs["mrb2"], bufs["xoldV"], bufs["xoldP"]
    pl_ps, rp_a, rp_b = bufs["pl_ps"], bufs["rp_a"], bufs["rp_b"]
    s_cb, s_sq2, s_plm = E["cb"], E["sq2"], E["plm"]
    s_r2, s_rp2, s_rbe = E["r2"], E["rp2"], E["rbe"]
    s_av, s_ap = E["av"], E["ap"]
    sl_xoV, sl_xoP, sl_xwV, sl_xwP = E["sl_xoV"], E["sl_xoP"], E["sl_xwV"], E["sl_xwP"]
    KTJ = nofj

    def t_part():
        for j in range(KTJ):
            b.wait("tensor", s_cb, g * KTJ + j + 1)
            b.wait("tensor", s_sq2, g * KTJ + j + 1)
            if j == 0 and g >= 1:
                b.wait("tensor", s_rp2, 2 * g)
                b.wait("tensor", s_r2, 3 * g)
            b.op("tensor", lambda j=j: nc.tensor.matmul(
                out=pl_ps.ap()[0:1, 0:GW], lhsT=ones_sb[:], rhs=yb[j % 2][:, 0:GW],
                start=(j == 0), stop=(j == KTJ - 1)))
            b.op("tensor", lambda j=j: nc.tensor.matmul(
                out=pl_ps.ap()[64:65, 0:GW], lhsT=ones_sb[:], rhs=ysq[j % 2][:, 0:GW],
                start=(j == 0), stop=(j == KTJ - 1)), s_plm)
        b.wait("tensor", s_r2, 3 * (g + 1))
        if g >= 1:
            b.wait("tensor", s_rbe, g)
        b.op("tensor", lambda: nc.tensor.matmul(
            out=rp_a.ap()[:, 0:GW], lhsT=oner_sb[:], rhs=row2[0:1, 0:GW], start=True, stop=True))
        b.op("tensor", lambda: nc.tensor.matmul(
            out=rp_b.ap()[:, 0:GW], lhsT=oner_sb[:], rhs=row2[0:1, GW:2 * GW], start=True, stop=True), s_rp2, inc=2)

    def v_part():
        for j in range(KTJ):
            if g * KTJ + j >= 2:
                b.wait("vector", s_plm, g * KTJ + j - 1)
            if yready is not None:
                yready(j)
            b.op("vector", lambda j=j: nc.vector.tensor_copy(out=yb[j % 2][:, 0:GW], in_=ysrc(j)))
            b.pub("vector", s_cb)
        b.wait("vector", s_plm, (g + 1) * KTJ)
        b.op("vector", lambda: nc.vector.tensor_scalar(
            out=rowt[0:1, 0:GW], in0=pl_ps.ap()[0:1, 0:GW], scalar1=1.0 / D, scalar2=None, op0=ALU.mult))
        b.op("vector", lambda: nc.vector.tensor_scalar(
            out=rowt[0:1, GW:2 * GW], in0=pl_ps.ap()[64:65, 0:GW], scalar1=1.0 / D, scalar2=None, op0=ALU.mult))
        b.op("vector", lambda: nc.vector.tensor_tensor(
            out=row2[0:1, GW:2 * GW], in0=rowt[0:1, 0:GW], in1=rowt[0:1, 0:GW], op=ALU.mult))
        b.op("vector", lambda: nc.vector.tensor_tensor(
            out=row2[0:1, GW:2 * GW], in0=rowt[0:1, GW:2 * GW], in1=row2[0:1, GW:2 * GW], op=ALU.subtract))
        b.pub("vector", s_r2)
        b.wait("vector", s_r2, 3 * g + 2)
        b.op("vector", lambda: nc.vector.tensor_tensor(
            out=row2[0:1, GW:2 * GW], in0=rowt[0:1, 0:GW], in1=row2[0:1, 0:GW], op=ALU.mult))
        b.pub("vector", s_r2)
        for j in range(0, KTJ, 2):
            nb = g * (KTJ // 2) + j // 2
            sl_xoV.wait("vector", nb % 2, 16 * (nb // 2 + 1))
            b.wait("vector", s_rbe, g + 1)
            if nb >= 2:
                sl_xwV.wait("vector", nb % 2, 16 * (nb // 2))
            b.op("vector", lambda j=j: nc.vector.tensor_tensor(
                out=ysrc(j), in0=ysrc(j), in1=rb2[:, 0:GW], op=ALU.mult))
            b.op("vector", lambda j=j: nc.vector.tensor_tensor(
                out=ysrc(j), in0=ysrc(j), in1=mrb2[:, 0:GW], op=ALU.subtract))
            b.op("vector", lambda j=j, nb=nb: nc.vector.tensor_tensor(
                out=xoldV[nb % 2][:, 0:GW], in0=ysrc(j), in1=xoldV[nb % 2][:, 0:GW], op=ALU.add))
            b.pub("vector", s_av)

    def s_part():
        for j in range(KTJ):
            if g * KTJ + j >= 2:
                b.wait("scalar", s_plm, g * KTJ + j - 1)
            b.wait("scalar", s_cb, g * KTJ + j + 1)
            b.op("scalar", lambda j=j: nc.scalar.activation(
                out=ysq[j % 2][:, 0:GW], in_=ysrc(j), func=AF.Square))
            b.pub("scalar", s_sq2)
        b.wait("scalar", s_r2, 3 * g + 1)
        b.op("scalar", lambda: nc.scalar.activation(
            out=row2[0:1, 0:GW], in_=row2[0:1, GW:2 * GW], func=AF.Ln, bias=cp_sb[0:1, 0:1]))
        b.op("scalar", lambda: nc.scalar.activation(
            out=row2[0:1, 0:GW], in_=row2[0:1, 0:GW], func=AF.Exp, scale=-0.5))
        b.pub("scalar", s_r2)
        b.wait("scalar", s_rp2, 2 * (g + 1))
        if g >= 1:
            b.wait("scalar", s_av, g * (KTJ // 2))
            b.wait("scalar", s_ap, g * (KTJ // 2))
        b.op("scalar", lambda: nc.scalar.copy(out=rb2[:, 0:GW], in_=rp_a.ap()[:, 0:GW]))
        b.op("scalar", lambda: nc.scalar.copy(out=mrb2[:, 0:GW], in_=rp_b.ap()[:, 0:GW]))
        b.pub("scalar", s_rbe)

    def g_part():
        for j in range(1, KTJ, 2):
            nb = g * (KTJ // 2) + (j - 1) // 2
            sl_xoP.wait("gpsimd", nb % 2, 16 * (nb // 2 + 1))
            b.wait("gpsimd", s_rbe, g + 1)
            if nb >= 2:
                sl_xwP.wait("gpsimd", nb % 2, 16 * (nb // 2))
            b.op("gpsimd", lambda j=j: nc.gpsimd.tensor_tensor(
                out=ysrc(j), in0=ysrc(j), in1=rb2[:, 0:GW], op=ALU.mult))
            b.op("gpsimd", lambda j=j: nc.gpsimd.tensor_tensor(
                out=ysrc(j), in0=ysrc(j), in1=mrb2[:, 0:GW], op=ALU.subtract))
            b.op("gpsimd", lambda j=j, nb=nb: nc.gpsimd.tensor_tensor(
                out=xoldP[nb % 2][:, 0:GW], in0=ysrc(j), in1=xoldP[nb % 2][:, 0:GW], op=ALU.add))
            b.pub("gpsimd", s_ap)

    def sp_part():
        def _ld(j):
            if j % 2 == 0:
                nb = g * (KTJ // 2) + j // 2
                if nb >= 2:
                    sl_xwV.wait("sync", nb % 2, 16 * (nb // 2))
                sl_xoV.load("sync", nb % 2, lambda j=j, nb=nb: nc.sync.dma_start(
                    out=xoldV[nb % 2][:, 0:GW], in_=xin[j][:, gt * GW:(gt + 1) * GW]))
            else:
                nb = g * (KTJ // 2) + (j - 1) // 2
                if nb >= 2:
                    sl_xwP.wait("sync", nb % 2, 16 * (nb // 2))
                sl_xoP.load("sync", nb % 2, lambda j=j, nb=nb: nc.sync.dma_start(
                    out=xoldP[nb % 2][:, 0:GW], in_=xin[j][:, gt * GW:(gt + 1) * GW]))

        def _st(j):
            if j % 2 == 0:
                nb = g * (KTJ // 2) + j // 2
                b.wait("sync", s_av, nb + 1)
                sl_xwV.load("sync", nb % 2, lambda j=j, nb=nb: nc.sync.dma_start(
                    out=xout[j][:, gt * GW:(gt + 1) * GW], in_=xoldV[nb % 2][:, 0:GW]))
            else:
                nb = g * (KTJ // 2) + (j - 1) // 2
                b.wait("sync", s_ap, nb + 1)
                sl_xwP.load("sync", nb % 2, lambda j=j, nb=nb: nc.sync.dma_start(
                    out=xout[j][:, gt * GW:(gt + 1) * GW], in_=xoldP[nb % 2][:, 0:GW]))

        for j in range(KTJ):
            if j >= 4:
                _st(j - 4)
            _ld(j)
        for j in range(KTJ - 4, KTJ):
            _st(j)

    return t_part, v_part, s_part, g_part, sp_part


def mk_postln_sems(b, tag):
    return {
        "cb": b.sem(tag + "cb"), "sq2": b.sem(tag + "sq2"), "plm": b.sem(tag + "plm"),
        "r2": b.sem(tag + "r2"), "rp2": b.sem(tag + "rp2"), "rbe": b.sem(tag + "rbe"),
        "av": b.sem(tag + "av"), "ap": b.sem(tag + "ap"),
        "sl_xoV": Slots(b, tag + "xoV", 2), "sl_xoP": Slots(b, tag + "xoP", 2),
        "sl_xwV": Slots(b, tag + "xwV", 2), "sl_xwP": Slots(b, tag + "xwP", 2),
    }


def mk_postln_bufs(b, reg, tag, c, ps_reg=None, ps_override=None):
    pr = ps_reg if ps_reg is not None else reg
    if ps_override is None:
        ps_override = {}
    return {
        "ones": b.sb(reg, tag + "ones", [P, 1], BF16),
        "oner": b.sb(reg, tag + "oner", [1, P], F32),
        "cp": b.sb(reg, tag + "cp", [P, 8], F32),
        "yb": [b.sb(reg, tag + f"yb{i}", [P, 512], BF16) for i in range(2)],
        "ysq": [b.sb(reg, tag + f"ysq{i}", [P, 512], BF16) for i in range(2)],
        "row2": b.sb(reg, tag + "row2", [1, 1024], F32),
        "rowt": b.sb(reg, tag + "rowt", [1, 1024], F32),
        "rb2": b.sb(reg, tag + "rb2", [P, 512], F32),
        "mrb2": b.sb(reg, tag + "mrb2", [P, 512], F32),
        "xoldV": [b.sb(reg, tag + f"xoV{i}", [P, 512], F32) for i in range(2)],
        "xoldP": [b.sb(reg, tag + f"xoP{i}", [P, 512], F32) for i in range(2)],
        "pl_ps": ps_override.get("pl") or b.ps(pr, tag + "plps", [P, 512]),
        "rp_a": ps_override.get("rpa") or b.ps(pr, tag + "rpa", [P, 512]),
        "rp_b": ps_override.get("rpb") or b.ps(pr, tag + "rpb", [P, 512]),
    }


def emit_postln_consts(b, bufs, c, sem):
    nc = b.nc
    b.op("sync", lambda: nc.sync.dma_start(out=bufs["ones"][:], in_=b.din("ones_bf", c["ones_bf"])), sem, 16)
    b.op("sync", lambda: nc.sync.dma_start(out=bufs["oner"][:], in_=b.din("ones_row_f32", c["ones_row_f32"])), sem, 16)
    b.op("sync", lambda: nc.sync.dma_start(out=bufs["cp"][:], in_=b.din("cpack", c["cpack"])), sem, 16)



def emit_gather(b, tag, eng, idxT_d, lut_sb, s_lut, out_sc, nb, W):
    nc = b.nc
    r = b.region()
    idx = [b.sb(r, f"{tag}i0", [P, W], I32)]
    idxf = b.sb(r, f"{tag}f", [P, W], BF16)
    msk = b.sb(r, f"{tag}m", [P, W], BF16)
    acc = [b.sb(r, f"{tag}a{i}", [P, W], BF16) for i in range(2)]
    sl_i = Slots(b, tag + "i", 1)
    s_d = b.sem(tag + "d")
    sl_o = Slots(b, tag + "o", 2)
    e = getattr(nc, eng)

    for k in range(nb):
        if k >= 1:
            b.wait("sync", s_d, k)
        sl_i.load("sync", 0, lambda k=k: nc.sync.dma_start(out=idx[0][:], in_=idxT_d[k]))

    for k in range(nb):
        sl_i.wait(eng, 0, 16 * (k + 1))
        if k == 0:
            b.wait(eng, s_lut, 16)
        if k >= 2:
            sl_o.wait(eng, k % 2, 16 * (k // 2))
        b.op(eng, lambda k=k: e.tensor_copy(out=idxf[:], in_=idx[0][:]))
        for j in range(16):
            if j == 0:
                b.op(eng, lambda k=k, j=j: e.tensor_scalar(
                    out=acc[k % 2][:], in0=idxf[:], scalar1=float(j), scalar2=lut_sb[:, j:j + 1],
                    op0=ALU.is_equal, op1=ALU.mult))
            else:
                b.op(eng, lambda k=k, j=j: e.tensor_scalar(
                    out=msk[:], in0=idxf[:], scalar1=float(j), scalar2=lut_sb[:, j:j + 1],
                    op0=ALU.is_equal, op1=ALU.mult))
                b.op(eng, lambda k=k: e.tensor_tensor(
                    out=acc[k % 2][:], in0=acc[k % 2][:], in1=msk[:], op=ALU.add))
        b.pub(eng, s_d)

    for k in range(nb):
        b.wait("scalar", s_d, k + 1)
        sl_o.load("scalar", k % 2, lambda k=k: nc.scalar.dma_start(out=out_sc[k], in_=acc[k % 2][:]))
    for i in range(2):
        sl_o.wait("scalar", i, 16 * ((nb + 1 - i) // 2))
    return r


def emit_l0f_gateup(b, c, lay, xn_res, pr):
    nc = b.nc
    r = b.region()
    wgt = [b.sb(r, f"f2wg{i}", [P, KT, P], BF16) for i in range(2)]
    wut = [b.sb(r, f"f2wu{i}", [P, KT, P], BF16) for i in range(2)]
    hrow = [b.sb(r, f"f2h{i}", [P, T], BF16) for i in range(2)]
    ps = [b.ps(r, f"f2ps{i}", [P, 512]) for i in range(6)]
    wg_sc, wu_sc = b._dram["wg_sc"], b._dram["wu_sc"]
    h_sc = b.dscratch("h_sc", [OT0, P, T], BF16)
    sl_w = Slots(b, "f2w", 2)
    s_pm = b.sem("f2pm")
    s_evg = b.sem("f2evg")
    s_evu = b.sem("f2evu")
    sl_h = Slots(b, "f2ho", 2, persistent=True)

    def bank_owner(g_all):
        o_, u_ = divmod(g_all, 16)
        return ("g", o_ * 8 + u_ + 1) if u_ < 8 else ("u", o_ * 8 + (u_ - 8) + 1)

    for o in range(OT0):
        if o >= 2:
            b.wait("sync", s_pm, (o - 1) * 16)
        sl_w.load("sync", o % 2, lambda o=o: nc.sync.dma_start(
            out=wgt[o % 2].ap(),
            in_=wg_sc[:, :, o * P:(o + 1) * P].rearrange("k p m -> p k m")))
        sl_w.uses[o % 2] -= 1
        sl_w.load("sync", o % 2, lambda o=o: nc.sync.dma_start(
            out=wut[o % 2].ap(),
            in_=wu_sc[:, :, o * P:(o + 1) * P].rearrange("k p m -> p k m")))

    for o in range(OT0):
        sl_w.wait("tensor", o % 2, 32 * (o // 2 + 1))
        for u in range(16):
            g_all = o * 16 + u
            wt = wgt if u < 8 else wut
            tb = u % 8
            bank = g_all % 6
            if g_all >= 6:
                own, cnt = bank_owner(g_all - 6)
                b.wait("tensor", s_evg if own == "g" else s_evu, cnt)
            for k in range(KT):
                st, sp = (k == 0), (k == KT - 1)
                b.op("tensor", lambda o=o, u=u, tb=tb, k=k, bank=bank, wt=wt, st=st, sp=sp: nc.tensor.matmul(
                    out=ps[bank].ap()[:], lhsT=wt[o % 2][:, k, :],
                    rhs=xn_res[:, k * T + tb * 512:k * T + (tb + 1) * 512],
                    start=st, stop=sp), s_pm if sp else None)

    for o in range(OT0):
        for tb in range(NBT):
            g_all = o * 16 + tb
            bank = g_all % 6
            b.wait("scalar", s_pm, g_all + 1)
            if tb == 0 and o >= 2:
                sl_h.wait("scalar", o % 2, 16 * (o // 2))
            b.op("scalar", lambda o=o, tb=tb, bank=bank: nc.scalar.activation(
                out=hrow[o % 2][:, tb * 512:(tb + 1) * 512], in_=ps[bank].ap()[:], func=AF.Silu))
            b.pub("scalar", s_evg)

    for o in range(OT0):
        for tb in range(NBT):
            g_all = o * 16 + 8 + tb
            bank = g_all % 6
            b.wait("vector", s_pm, g_all + 1)
            b.wait("vector", s_evg, o * 8 + tb + 1)
            b.op("vector", lambda o=o, tb=tb, bank=bank: nc.vector.tensor_tensor(
                out=hrow[o % 2][:, tb * 512:(tb + 1) * 512],
                in0=hrow[o % 2][:, tb * 512:(tb + 1) * 512],
                in1=ps[bank].ap()[:], op=ALU.mult))
            b.pub("vector", s_evu)

    for o in range(OT0):
        b.wait("gpsimd", s_evu, (o + 1) * 8)
        sl_h.load("gpsimd", o % 2, lambda o=o: nc.gpsimd.dma_start(out=h_sc[o], in_=hrow[o % 2][:]))
    for i in range(2):
        sl_h.wait("gpsimd", i, 16 * ((OT0 + 1 - i) // 2))
    b.barrier()
    r.close()


def emit_l0f_down(b, c, lay, xin, xout):
    nc = b.nc
    r = b.region()
    KOC = [8, 8, 8, 8, 8, 3]
    NTQ = 4
    ydown = b.sb(r, "f3y", [P, KT, 1024], F32)
    wdt = [b.sb(r, f"f3w{i}", [P, 8, D], BF16) for i in range(2)]
    hq = [b.sb(r, f"f3h{i}", [P, 8, 1024], BF16) for i in range(2)]
    ps = [b.ps(r, f"f3ps{i}", [P, 512]) for i in range(4)]
    pbufs = mk_postln_bufs(b, r, "f3p", c)
    E = mk_postln_sems(b, "f3p")
    s_cst = b.sem("f3cst")
    emit_postln_consts(b, pbufs, c, s_cst)
    wd_sc = b._dram["wd_sc"]
    h_sc = b._dram["h_sc"]
    sl_wd = Slots(b, "f3wd", 2)
    sl_hq = Slots(b, "f3hq", 2)
    s_mm = b.sem("f3mm")
    s_ev = b.sem("f3ev")

    tgt_wd = {}
    tgt_hq = {}
    wd_cnt = [0, 0]
    hq_cnt = [0, 0]

    def unit(tq, ci, j, tb):
        return ((tq * 6 + ci) * KT + j) * 2 + tb

    def pg(tq, gg):
        return tq * 2 + gg

    parts = {}
    for tq in range(NTQ):
        for ci, kosz in enumerate(KOC):
            ko0 = sum(KOC[:ci])
            idx = tq * 6 + ci
            slot = idx % 2
            if idx >= 2:
                b.wait("sync", s_mm, (idx - 1) * 32)
            for kk in range(kosz):
                sl_wd.load("sync", slot, lambda ko0=ko0, kk=kk, slot=slot: nc.sync.dma_start(
                    out=wdt[slot].ap()[:, kk, :], in_=wd_sc[ko0 + kk]))
                sl_hq.load("sync", slot, lambda ko0=ko0, kk=kk, slot=slot, tq=tq: nc.sync.dma_start(
                    out=hq[slot].ap()[:, kk, :], in_=h_sc[ko0 + kk][:, tq * 1024:(tq + 1) * 1024]))
            wd_cnt[slot] += 16 * kosz
            hq_cnt[slot] += 16 * kosz
            tgt_wd[idx] = wd_cnt[slot]
            tgt_hq[idx] = hq_cnt[slot]
        for gg in range(2):
            g = pg(tq, gg)
            parts[g] = emit_postln_group(
                b, "f3p", E, g, g, (lambda j, tq=tq, gg=gg: ydown.ap()[:, j, gg * 512:(gg + 1) * 512]),
                xin, xout, pbufs)
        for gg in range(2):
            parts[pg(tq, gg)][4]()

    b.wait("tensor", s_cst, 48)
    for tq in range(NTQ):
        for ci, kosz in enumerate(KOC):
            idx = tq * 6 + ci
            slot = idx % 2
            sl_wd.wait("tensor", slot, tgt_wd[idx])
            sl_hq.wait("tensor", slot, tgt_hq[idx])
            for j in range(KT):
                for kk in range(kosz):
                    for tb in range(2):
                        bank = (j % 2) * 2 + tb
                        u = unit(tq, ci, j, tb)
                        if kk == 0 and u >= 4:
                            b.wait("tensor", s_ev, u - 3)
                        st, sp = (kk == 0), (kk == kosz - 1)
                        b.op("tensor", lambda slot=slot, kk=kk, j=j, tb=tb, bank=bank, st=st, sp=sp: nc.tensor.matmul(
                            out=ps[bank].ap()[:], lhsT=wdt[slot][:, kk, j * P:(j + 1) * P],
                            rhs=hq[slot].ap()[:, kk, tb * 512:(tb + 1) * 512],
                            start=st, stop=sp), s_mm if sp else None)
        for gg in range(2):
            parts[pg(tq, gg)][0]()

    for tq in range(NTQ):
        if tq >= 1:
            b.wait("vector", E["ap"], tq * 16)
        for ci, kosz in enumerate(KOC):
            for j in range(KT):
                for tb in range(2):
                    u = unit(tq, ci, j, tb)
                    b.wait("vector", s_mm, u + 1)
                    if ci == 0:
                        b.op("vector", lambda j=j, tb=tb, bank=(j % 2) * 2 + tb: nc.vector.tensor_copy(
                            out=ydown.ap()[:, j, tb * 512:(tb + 1) * 512], in_=ps[bank].ap()[:]))
                    else:
                        b.op("vector", lambda j=j, tb=tb, bank=(j % 2) * 2 + tb: nc.vector.tensor_tensor(
                            out=ydown.ap()[:, j, tb * 512:(tb + 1) * 512],
                            in0=ydown.ap()[:, j, tb * 512:(tb + 1) * 512],
                            in1=ps[bank].ap()[:], op=ALU.add))
                    b.pub("vector", s_ev)
        for gg in range(2):
            parts[pg(tq, gg)][1]()

    for tq in range(NTQ):
        for gg in range(2):
            parts[pg(tq, gg)][2]()
    for tq in range(NTQ):
        for gg in range(2):
            parts[pg(tq, gg)][3]()

    for i in range(2):
        E["sl_xwV"].wait("sync", i, 16 * (NTQ * 2 * (KT // 2) // 2))
        E["sl_xwP"].wait("sync", i, 16 * (NTQ * 2 * (KT // 2) // 2))
    b.barrier()
    r.close()


def ph_l0f(b, c, lay, xin, xout):
    nc = b.nc
    r0 = b.region()
    lut_sb = b.sb(r0, "g0lut", [P, 16], F32)
    s_lut = b.sem("g0lut")
    b.op("sync", lambda: nc.sync.dma_start(out=lut_sb[:], in_=b.din("l0_lut", lay["l0_lut"])), s_lut, 16)
    wg_sc = b.dscratch("wg_sc", [KT, P, I0], BF16)
    wu_sc = b.dscratch("wu_sc", [KT, P, I0], BF16)
    wd_sc = b.dscratch("wd_sc", [OT0, P, D], BF16)
    rg = emit_gather(b, "g0g", "vector", b.din("l0_giT", lay["l0_giT"]), lut_sb, s_lut, wg_sc, KT, I0)
    ru = emit_gather(b, "g0u", "vector", b.din("l0_uiT", lay["l0_uiT"]), lut_sb, s_lut, wu_sc, KT, I0)
    rd = emit_gather(b, "g0d", "vector", b.din("l0_diT", lay["l0_diT"]), lut_sb, s_lut, wd_sc, OT0, D)
    b.barrier()
    rd.close(); ru.close(); rg.close(); r0.close()

    pr = b.region()
    xn = emit_preln(b, "f1", xin, c, pr)
    emit_l0f_gateup(b, c, lay, xn, pr)
    pr.close()
    emit_l0f_down(b, c, lay, xin, xout)



def emit_l1a_qkvg(b, c, lay, xn_res, pr):
    nc = b.nc
    r = b.region()
    wfp = [b.sb(r, f"b2wf{i}", [P, KT * P], F32) for i in range(2)]
    wb = [b.sb(r, f"b2wb{i}", [P, KT * P], BF16) for i in range(2)]
    orow = b.sb(r, "b2or", [P, T], BF16)
    ps = [b.ps(r, f"b2ps{i}", [P, 512]) for i in range(6)]
    q_sc = b.dscratch("q1_sc", [NB1, P, H, BLK1], BF16)
    k_sc = b.dscratch("k1_sc", [NB1, P, H, BLK1], BF16)
    v_sc = b.dscratch("v1_sc", [NB1, P, H, BLK1], BF16)
    g_sc = b.dscratch("g1_sc", [NB1, P, H, BLK1], BF16)
    wq_d = b.din("l1_wqkvT", lay["l1_wqkvT"])
    wg_d = b.din("l1_wgateT", lay["l1_wgateT"])
    sl_w = Slots(b, "b2w", 2)
    s_wc = b.sem("b2wc")
    s_pm = b.sem("b2pm")
    s_ev = b.sem("b2ev")
    sl_o = Slots(b, "b2o", 8, persistent=True)

    NJ = 48 + 16
    for jj in range(NJ):
        if jj >= 2:
            b.wait("sync", s_wc, jj - 1)
        src = (lambda jj=jj: wq_d[jj] if jj < 48 else wg_d[jj - 48])
        sl_w.load("sync", jj % 2, lambda jj=jj, src=src: nc.sync.dma_start(
            out=wfp[jj % 2][:], in_=src()))

    for jj in range(NJ):
        sl_w.wait("vector", jj % 2, 16 * (jj // 2 + 1))
        if jj >= 2:
            b.wait("vector", s_pm, (jj - 1) * 8)
        b.op("vector", lambda jj=jj: nc.vector.tensor_copy(out=wb[jj % 2][:], in_=wfp[jj % 2][:]))
        b.pub("vector", s_wc)

    for jj in range(NJ):
        b.wait("tensor", s_wc, jj + 1)
        for tb in range(NBT):
            g = jj * 8 + tb
            if g >= 6:
                b.wait("tensor", s_ev, g - 5)
            for k in range(KT):
                st, sp = (k == 0), (k == KT - 1)
                b.op("tensor", lambda jj=jj, tb=tb, k=k, g=g, st=st, sp=sp: nc.tensor.matmul(
                    out=ps[g % 6].ap()[:], lhsT=wb[jj % 2][:, k * P:(k + 1) * P],
                    rhs=xn_res[:, k * T + tb * 512:k * T + (tb + 1) * 512],
                    start=st, stop=sp), s_pm if sp else None)

    s_evd = b.sem("b2evd")
    for jj in range(NJ):
        func = AF.Silu if jj < 48 else AF.Sigmoid
        for tb in range(NBT):
            g = jj * 8 + tb
            b.wait("scalar", s_pm, g + 1)
            if jj >= 1:
                sl_o.wait("scalar", tb, 16 * jj)
            b.op("scalar", lambda g=g, tb=tb, func=func: nc.scalar.activation(
                out=orow[:, tb * 512:(tb + 1) * 512], in_=ps[g % 6].ap()[:], func=func), s_ev)
            b.pub("scalar", s_evd)

    for jj in range(NJ):
        if jj < 48:
            h, kind = jj // 3, jj % 3
            dst = [q_sc, k_sc, v_sc][kind]
        else:
            h, dst = jj - 48, g_sc
        for tb in range(NBT):
            g = jj * 8 + tb
            b.wait("gpsimd", s_evd, g + 1)
            sl_o.load("gpsimd", tb, lambda dst=dst, h=h, tb=tb: nc.gpsimd.dma_start(
                out=dst[tb * 2:(tb + 1) * 2, :, h, :].rearrange("cb p m -> p cb m"),
                in_=orow.ap()[:, tb * 512:(tb + 1) * 512].rearrange("p (cb m) -> p cb m", cb=2)))
    for tb in range(NBT):
        sl_o.wait("gpsimd", tb, 16 * NJ)
    b.barrier()
    r.close()


def emit_l1a_scan(b, c, lay, xin, xout):
    nc = b.nc
    r = b.region()
    wob = b.sb(r, "b3wo", [P, H * KT * P], BF16)
    kv = b.sb(r, "b3kv", [P, H * P], F32)
    qdec = b.sb(r, "b3qd", [P, H * BLK1], BF16)
    diag = b.sb(r, "b3dg", [P, H * 2 * BLK1], BF16)
    kdec = b.sb(r, "b3kd", [P, 2 * H], F32)
    bdec = b.sb(r, "b3bd", [P, H], F32)
    negs = b.sb(r, "b3ns", [P, H], F32)
    slope_sb = b.sb(r, "b3sl", [P, H], F32)
    arrb = b.sb(r, "b3ar", [P, BLK1], F32)
    revc = b.sb(r, "b3rv", [P, 2], F32)
    posb = b.sb(r, "b3po", [P, 2 * BLK1], F32)
    maskb = b.sb(r, "b3mk", [P, 2 * BLK1], F32)
    blkc = b.sb(r, "b3bc", [P, 1], F32)
    nwpc = b.sb(r, "b3nw", [P, KT], F32)
    ids = b.sb(r, "b3id", [P, P], BF16)
    etmp = b.sb(r, "b3et", [P, BLK1], F32)
    wfp = [b.sb(r, "b3wf0", [P, KT * P], F32)]
    qch = [b.sb(r, "b3q0", [P, H * BLK1], BF16)]
    kch = [b.sb(r, "b3k0", [P, H * BLK1], BF16)]
    vch = [b.sb(r, "b3v0", [P, H * BLK1], BF16)]
    gch = [b.sb(r, "b3g0", [P, H * BLK1], BF16)]
    ktd = b.sb(r, "b3ktd", [P, BLK1], BF16)
    vtok = b.sb(r, "b3vt", [P, BLK1], BF16)
    scm = b.sb(r, "b3sc", [P, 2 * BLK1], BF16)
    qd = b.sb(r, "b3qdd", [P, BLK1], F32)
    dkv = b.sb(r, "b3dk", [P, P], F32)
    afm = b.sb(r, "b3af", [P, H * BLK1], BF16)
    asq = b.sb(r, "b3as", [P, BLK1], BF16)
    rowb = b.sb(r, "b3rw", [1, BLK1], F32)
    rb_c = b.sb(r, "b3rb", [P, BLK1], F32)
    gtmp = b.sb(r, "b3gt", [P, BLK1], F32)
    grbuf = b.sb(r, "b3gr", [P, H, BLK1], BF16)
    ywo = b.sb(r, "b3yw", [P, KT, BLK1], F32)
    ps_tk = b.ps(r, "b3ptk", [P, 1024], BF16)
    ps_tv = b.ps(r, "b3ptv", [P, 1024], BF16)
    ps_sc = [b.ps(r, f"b3psc{i}", [P, 512]) for i in range(2)]
    ps_o = b.ps(r, "b3po2", [P, 512])
    ps_kv = b.ps(r, "b3pkv", [P, 512])
    ps_rm = b.ps(r, "b3prm", [P, 512])
    ps_wo = [b.ps(r, "b3pw0", [P, 512])]
    pbufs = mk_postln_bufs(b, r, "b3p", c, ps_reg=None,
                           ps_override={"pl": ps_sc[0], "rpa": ps_sc[1], "rpb": ps_o})
    E = mk_postln_sems(b, "b3p")
    s_cst = b.sem("b3cst")
    emit_postln_consts(b, pbufs, c, s_cst)

    q_sc, k_sc = b._dram["q1_sc"], b._dram["k1_sc"]
    v_sc, g_sc = b._dram["v1_sc"], b._dram["g1_sc"]
    wo_d = b.din("l1_woutT", lay["l1_woutT"])

    s_ldc = b.sem("b3ldc")
    sl_wol = Slots(b, "b3wol", 1)
    s_wc = b.sem("b3wc")
    sl_ch = Slots(b, "b3ch", 1)
    s_dec = b.sem("b3dec")
    s_t1 = b.sem("b3t1")
    s_t2 = b.sem("b3t2")
    s_m1 = b.sem("b3m1")
    s_o = b.sem("b3o")
    s_m5 = b.sem("b3m5")
    s_m6 = b.sem("b3m6")
    s_vt = b.sem("b3vt")
    s_dkv = b.sem("b3dkv")
    s_aq = b.sem("b3aq")
    s_scm = b.sem("b3scm")
    s_qd = b.sem("b3qd2")
    s_ktd = b.sem("b3ktd2")
    s_kv = b.sem("b3kv2")
    s_rr = b.sem("b3rr")
    s_rp = b.sem("b3rp")
    s_rb = b.sem("b3rb2")
    s_gr = b.sem("b3gr2")
    s_wo = b.sem("b3wo2")
    s_ye = b.sem("b3ye")

    for nm, dst, src in [("id", ids, ("ident", c["ident"])),
                         ("sl", slope_sb, ("l1_slope_b", lay["l1_slope_b"])),
                         ("ar", arrb, ("l1_arr_row_b", c["l1_arr_row_b"])),
                         ("rv", revc, ("l1_rev_pc", c["l1_rev_pc"])),
                         ("bc", blkc, ("l1_blk_const", c["l1_blk_const"])),
                         ("nw", nwpc, ("l1_normw", lay["l1_normw"]))]:
        b.op("sync", lambda dst=dst, src=src: nc.sync.dma_start(out=dst[:], in_=b.din(*src)), s_ldc, 16)
    b.op("sync", lambda: nc.sync.dma_start(
        out=posb.ap().rearrange("p (h m) -> p h m", h=2),
        in_=b.din("l1_posT", c["l1_posT"]).rearrange("h p m -> p h m")), s_ldc, 16)
    b.op("sync", lambda: nc.sync.dma_start(
        out=maskb.ap().rearrange("p (h m) -> p h m", h=2),
        in_=b.din("l1_maskT", c["l1_maskT"]).rearrange("h p m -> p h m")), s_ldc, 16)
    NLDC = 8
    for j in range(H):
        if j >= 1:
            b.wait("sync", s_wc, j)
        sl_wol.load("sync", 0, lambda j=j: nc.sync.dma_start(out=wfp[0][:], in_=wo_d[j]))

    s_dge = b.sem("b3dge")
    s_dgv = b.sem("b3dgv")
    b.wait("vector", s_ldc, 16 * NLDC)
    b.op("vector", lambda: nc.vector.tensor_scalar(
        out=negs[:], in0=slope_sb[:], scalar1=-1.0, scalar2=None, op0=ALU.mult))
    b.pub("vector", s_dec)
    b.wait("scalar", s_ldc, 16 * NLDC)
    b.wait("scalar", s_dec, 1)
    for h in range(H):
        b.op("scalar", lambda h=h: nc.scalar.activation(
            out=qdec[:, h * BLK1:(h + 1) * BLK1], in_=arrb[:], func=AF.Exp,
            scale=negs[:, h:h + 1]))
        for half in range(2):
            b.op("scalar", lambda h=h, half=half: nc.scalar.activation(
                out=kdec[:, 2 * h + half:2 * h + half + 1], in_=revc[:, half:half + 1],
                func=AF.Exp, scale=negs[:, h:h + 1]))
        b.op("scalar", lambda h=h: nc.scalar.activation(
            out=bdec[:, h:h + 1], in_=blkc[:], func=AF.Exp, scale=negs[:, h:h + 1]))
    for i in range(2 * H):
        h, half = divmod(i, 2)
        if i >= 1:
            b.wait("scalar", s_dgv, i)
        b.op("scalar", lambda h=h, half=half, i=i: nc.scalar.activation(
            out=etmp[:, (i % 2) * 0:BLK1] if False else etmp[:],
            in_=posb[:, half * BLK1:(half + 1) * BLK1], func=AF.Exp,
            scale=negs[:, h:h + 1]))
        b.pub("scalar", s_dge)
        b.wait("vector", s_dge, i + 1)
        b.op("vector", lambda h=h, half=half: nc.vector.tensor_tensor(
            out=diag[:, (h * 2 + half) * BLK1:(h * 2 + half + 1) * BLK1],
            in0=etmp[:], in1=maskb[:, half * BLK1:(half + 1) * BLK1], op=ALU.mult))
        b.pub("vector", s_dgv)
    b.pub("scalar", s_dec)

    for j in range(H):
        sl_wol.wait("vector", 0, 16 * (j + 1))
        for k in range(KT):
            b.op("vector", lambda j=j, k=k: nc.vector.tensor_scalar(
                out=wob[:, (j * KT + k) * P:(j * KT + k + 1) * P],
                in0=wfp[0][:, k * P:(k + 1) * P],
                scalar1=nwpc[:, k:k + 1], scalar2=None, op0=ALU.mult))
        b.pub("vector", s_wc)
    b.op("gpsimd", lambda: nc.gpsimd.memset(kv[:], 0.0))
    b.pub("gpsimd", s_kv)

    parts = {}
    for nb in range(NB1):
        if nb >= 1:
            b.wait("sync", s_m5, nb * H)
            b.wait("sync", s_ktd, nb * H)
            b.wait("sync", s_gr, nb * H)
        for src, dstb in [(q_sc, qch), (k_sc, kch), (v_sc, vch), (g_sc, gch)]:
            sl_ch.load("sync", 0, lambda src=src, dstb=dstb, nb=nb: nc.sync.dma_start(
                out=dstb[0].ap().rearrange("p (h m) -> p h m", h=H), in_=src[nb]))
            sl_ch.uses[0] -= 1
        sl_ch.uses[0] += 1
        parts[nb] = emit_postln_group(
            b, "b3p", E, nb, nb, (lambda j: ywo.ap()[:, j, :]),
            xin, xout, pbufs, GW=BLK1,
            yready=(lambda j, nb=nb: b.wait("vector", s_ye, nb * KT + j + 1)))
        parts[nb][4]()

    def hs2(h):
        return slice(h * BLK1, (h + 1) * BLK1)

    b.wait("tensor", s_ldc, 16 * NLDC)
    b.wait("tensor", s_wc, H)
    for nb in range(NB1):
        sl_ch.wait("tensor", 0, 64 * (nb + 1))
        if nb >= 1:
            b.wait("tensor", E["rbe"], nb)
            b.wait("tensor", E["r2"], 3 * nb)
        for h in range(H):
            n = nb * H + h
            if n >= 1:
                b.wait("tensor", s_ktd, n)
                b.wait("tensor", s_vt, n)
            for half in range(2):
                b.op("tensor", lambda nb=nb, h=h, half=half: nc.tensor.transpose(
                    out=ps_tk.ap()[:, half * P:(half + 1) * P],
                    in_=kch[0][:, h * BLK1 + half * P: h * BLK1 + (half + 1) * P],
                    identity=ids[:]), s_t1 if half == 1 else None)
            for half in range(2):
                b.op("tensor", lambda nb=nb, h=h, half=half: nc.tensor.transpose(
                    out=ps_tv.ap()[:, half * P:(half + 1) * P],
                    in_=vch[0][:, h * BLK1 + half * P: h * BLK1 + (half + 1) * P],
                    identity=ids[:]), s_t2 if half == 1 else None)
            if n >= 1:
                b.wait("tensor", s_ktd, n)
            for half in range(2):
                b.op("tensor", lambda nb=nb, h=h, half=half: nc.tensor.matmul(
                    out=ps_sc[half].ap()[:, 0:BLK1],
                    lhsT=kch[0][:, h * BLK1 + half * P: h * BLK1 + (half + 1) * P],
                    rhs=qch[0][:, hs2(h)], start=True, stop=True),
                    s_m1 if half == 1 else None)
            b.wait("tensor", s_kv, (n + 2 - H) if nb >= 1 else 1)
            b.wait("tensor", s_ktd, n + 1)
            if n >= 1:
                b.wait("tensor", s_aq, n)
            b.op("tensor", lambda h=h: nc.tensor.matmul(
                out=ps_o.ap()[:, 0:BLK1], lhsT=kv[:, h * P:(h + 1) * P], rhs=qd[:],
                start=True, stop=False))
            b.wait("tensor", s_vt, n + 1)
            for half in range(2):
                b.op("tensor", lambda half=half: nc.tensor.matmul(
                    out=ps_o.ap()[:, 0:BLK1], lhsT=vtok[:, half * P:(half + 1) * P],
                    rhs=scm[:, half * BLK1:(half + 1) * BLK1], start=False,
                    stop=(half == 1)), s_o if half == 1 else None)
            if n >= 1:
                b.wait("tensor", s_kv, n + 1)
            for half in range(2):
                b.op("tensor", lambda half=half: nc.tensor.matmul(
                    out=ps_kv.ap()[:, 0:P], lhsT=ktd[:, half * P:(half + 1) * P],
                    rhs=vtok[:, half * P:(half + 1) * P], start=(half == 0),
                    stop=(half == 1)), s_m5 if half == 1 else None)
            b.wait("tensor", s_aq, n + 1)
            if h == 0 and nb >= 1:
                b.wait("tensor", s_rp, nb)
                b.wait("tensor", s_rb, nb)
            b.op("tensor", lambda h=h: nc.tensor.matmul(
                out=ps_rm.ap()[0:1, 0:BLK1], lhsT=ones_dummy[:] if False else pbufs["ones"][:],
                rhs=asq[:], start=(h == 0), stop=(h == H - 1)), s_m6)
        b.wait("tensor", s_rr, nb + 1)
        b.op("tensor", lambda: nc.tensor.matmul(
            out=ps_rm.ap()[:, BLK1:2 * BLK1], lhsT=pbufs["oner"][:], rhs=rowb[0:1, :],
            start=True, stop=True), s_rp)
        b.wait("tensor", s_gr, (nb + 1) * H)
        for j in range(KT):
            if nb * KT + j >= 1:
                b.wait("tensor", s_ye, nb * KT + j)
            for h in range(H):
                b.op("tensor", lambda j=j, h=h, nb=nb: nc.tensor.matmul(
                    out=ps_wo[0].ap()[:, 0:BLK1], lhsT=wob[:, (j * KT + h) * P:(j * KT + h + 1) * P],
                    rhs=grbuf.ap()[:, h, :], start=(h == 0), stop=(h == H - 1)),
                    s_wo if h == H - 1 else None, inc=H)
        parts[nb][0]()

    b.wait("vector", s_dgv, 2 * H)
    b.wait("vector", s_dec, 2)
    for nb in range(NB1):
        sl_ch.wait("vector", 0, 64 * (nb + 1))
        for h in range(H):
            n = nb * H + h
            b.wait("vector", s_t1, n + 1)
            b.wait("vector", s_m5, n)
            for half in range(2):
                b.op("vector", lambda h=h, half=half: nc.vector.tensor_scalar(
                    out=ktd[:, half * P:(half + 1) * P], in0=ps_tk.ap()[:, half * P:(half + 1) * P],
                    scalar1=kdec[:, 2 * h + half:2 * h + half + 1], scalar2=None, op0=ALU.mult))
            b.wait("vector", s_m1, n + 1)
            b.wait("vector", s_o, n)
            for half in range(2):
                b.op("vector", lambda h=h, half=half: nc.vector.tensor_tensor(
                    out=scm[:, half * BLK1:(half + 1) * BLK1], in0=ps_sc[half].ap()[:, 0:BLK1],
                    in1=diag[:, (h * 2 + half) * BLK1:(h * 2 + half + 1) * BLK1], op=ALU.mult))
            b.op("vector", lambda nb=nb, h=h: nc.vector.tensor_tensor(
                out=qd[:], in0=qch[0][:, hs2(h)], in1=qdec[:, hs2(h)], op=ALU.mult))
            b.pub("vector", s_ktd)
            b.wait("vector", s_m5, n + 1)
            b.op("vector", lambda h=h: nc.vector.scalar_tensor_tensor(
                out=kv[:, h * P:(h + 1) * P], in0=kv[:, h * P:(h + 1) * P],
                scalar=bdec[:, h:h + 1], in1=ps_kv.ap()[:, 0:P], op0=ALU.mult, op1=ALU.add))
            b.pub("vector", s_kv)
        b.wait("vector", s_rb, nb + 1)
        b.wait("vector", s_aq, (nb + 1) * H)
        b.wait("vector", s_wo, (nb) * KT * H)
        for h in range(H):
            b.op("vector", lambda h=h: nc.vector.tensor_tensor(
                out=gtmp[:], in0=afm[:, hs2(h)], in1=rb_c[:], op=ALU.mult))
            b.op("vector", lambda h=h, nb=nb: nc.vector.tensor_tensor(
                out=grbuf.ap()[:, h, :], in0=gtmp[:], in1=gch[0][:, hs2(h)], op=ALU.mult))
        b.pub("vector", s_gr, inc=H)
        parts[nb][1]()

    b.wait("scalar", s_cst, 48)
    for nb in range(NB1):
        gt = nb // 2
        for h in range(H):
            n = nb * H + h
            b.wait("scalar", s_t2, n + 1)
            b.wait("scalar", s_m5, n)
            b.op("scalar", lambda: nc.scalar.copy(out=vtok[:], in_=ps_tv.ap()[:, 0:2 * P]))
            b.pub("scalar", s_vt)
            b.wait("scalar", s_o, n + 1)
            if nb >= 1:
                b.wait("scalar", s_gr, n - H + 1)
            b.wait("scalar", s_m6, n)
            b.op("scalar", lambda h=h: nc.scalar.copy(out=afm[:, hs2(h)], in_=ps_o.ap()[:, 0:BLK1]))
            b.op("scalar", lambda: nc.scalar.activation(out=asq[:], in_=ps_o.ap()[:, 0:BLK1], func=AF.Square))
            b.pub("scalar", s_aq)
        b.wait("scalar", s_m6, (nb + 1) * H)
        b.wait("scalar", s_rp, nb)
        b.op("scalar", lambda: nc.scalar.activation(
            out=rowb[0:1, :], in_=ps_rm.ap()[0:1, 0:BLK1], func=AF.Ln, scale=1.0 / D,
            bias=pbufs["cp"][0:1, 1:2]))
        b.op("scalar", lambda: nc.scalar.activation(
            out=rowb[0:1, :], in_=rowb[0:1, :], func=AF.Exp, scale=-0.5))
        b.pub("scalar", s_rr)
        b.wait("scalar", s_rp, nb + 1)
        b.wait("scalar", s_gr, nb * H)
        b.op("scalar", lambda: nc.scalar.copy(out=rb_c[:], in_=ps_rm.ap()[:, BLK1:2 * BLK1]))
        b.pub("scalar", s_rb)
        for j in range(KT):
            b.wait("scalar", s_wo, nb * KT * H + (j + 1) * H)
            if nb >= 1:
                b.wait("scalar", E["av"], nb * 8)
                b.wait("scalar", E["ap"], nb * 8)
            b.op("scalar", lambda j=j, nb=nb: nc.scalar.copy(
                out=ywo.ap()[:, j, :], in_=ps_wo[0].ap()[:, 0:BLK1]))
            b.pub("scalar", s_ye)
        parts[nb][2]()

    for nb in range(NB1):
        parts[nb][3]()

    for i in range(2):
        E["sl_xwV"].wait("sync", i, 16 * (NB1 * (KT // 2) // 2))
        E["sl_xwP"].wait("sync", i, 16 * (NB1 * (KT // 2) // 2))
    b.barrier()
    r.close()


def ph_l1a(b, c, lay, xin, xout):
    pr = b.region()
    xn = emit_preln(b, "b1", xin, c, pr)
    emit_l1a_qkvg(b, c, lay, xn, pr)
    pr.close()
    emit_l1a_scan(b, c, lay, xin, xout)



def emit_l1f_gateup(b, c, lay, xn_res, pr):
    nc = b.nc
    r = b.region()
    w8g = [b.sb(r, f"e2g{i}", [P, KT * P], I8) for i in range(2)]
    w8u = [b.sb(r, f"e2u{i}", [P, KT * P], I8) for i in range(2)]
    wbg = [b.sb(r, f"e2bg{i}", [P, KT * P], BF16) for i in range(2)]
    wbu = [b.sb(r, f"e2bu{i}", [P, KT * P], BF16) for i in range(2)]
    hrow = [b.sb(r, f"e2h{i}", [P, T], BF16) for i in range(2)]
    sg_sb = b.sb(r, "e2sg", [P, OT1], F32)
    su_sb = b.sb(r, "e2su", [P, OT1], F32)
    ps = [b.ps(r, f"e2ps{i}", [P, 512]) for i in range(6)]
    wg_d = b.din("l1_wg8", lay["l1_wg8"])
    wu_d = b.din("l1_wu8", lay["l1_wu8"])
    h_sc = b.dscratch("h2_sc", [OT1, P, T], BF16)
    sl_w = Slots(b, "e2w", 2)
    s_sld = b.sem("e2sld")
    s_wc = b.sem("e2wc")
    s_pm = b.sem("e2pm")
    s_evg = b.sem("e2evg")
    s_evu = b.sem("e2evu")
    sl_h = Slots(b, "e2ho", 2, persistent=True)

    b.op("sync", lambda: nc.sync.dma_start(out=sg_sb[:], in_=b.din("l1_sg", lay["l1_sg"])), s_sld, 16)
    b.op("sync", lambda: nc.sync.dma_start(out=su_sb[:], in_=b.din("l1_su", lay["l1_su"])), s_sld, 16)

    def bank_owner(g_all):
        o_, u_ = divmod(g_all, 16)
        return ("g", o_ * 8 + u_ + 1) if u_ < 8 else ("u", o_ * 8 + (u_ - 8) + 1)

    for o in range(OT1):
        if o >= 2:
            b.wait("sync", s_wc, o - 1)
        sl_w.load("sync", o % 2, lambda o=o: nc.sync.dma_start(out=w8g[o % 2][:], in_=wg_d[o]))
        sl_w.uses[o % 2] -= 1
        sl_w.load("sync", o % 2, lambda o=o: nc.sync.dma_start(out=w8u[o % 2][:], in_=wu_d[o]))

    def _upev(o):
        for tb in range(NBT):
            g_all = o * 16 + 8 + tb
            bank = g_all % 6
            b.wait("vector", s_pm, g_all + 1)
            b.wait("vector", s_evg, o * 8 + tb + 1)
            b.op("vector", lambda o=o, tb=tb, bank=bank: nc.vector.scalar_tensor_tensor(
                out=hrow[o % 2][:, tb * 512:(tb + 1) * 512], in0=ps[bank].ap()[:],
                scalar=su_sb[:, o:o + 1], in1=hrow[o % 2][:, tb * 512:(tb + 1) * 512],
                op0=ALU.mult, op1=ALU.mult))
            b.pub("vector", s_evu)

    for o in range(OT1 + 1):
        if o < OT1:
            sl_w.wait("vector", o % 2, 32 * (o // 2 + 1))
            if o >= 2:
                b.wait("vector", s_pm, (o - 1) * 16)
            b.op("vector", lambda o=o: nc.vector.tensor_copy(out=wbg[o % 2][:], in_=w8g[o % 2][:]))
            b.op("vector", lambda o=o: nc.vector.tensor_copy(out=wbu[o % 2][:], in_=w8u[o % 2][:]))
            b.pub("vector", s_wc)
        if o >= 1:
            _upev(o - 1)

    for o in range(OT1):
        b.wait("tensor", s_wc, o + 1)
        for u in range(16):
            g_all = o * 16 + u
            wt = wbg if u < 8 else wbu
            tb = u % 8
            bank = g_all % 6
            if g_all >= 6:
                own, cnt = bank_owner(g_all - 6)
                b.wait("tensor", s_evg if own == "g" else s_evu, cnt)
            for k in range(KT):
                st, sp = (k == 0), (k == KT - 1)
                b.op("tensor", lambda o=o, tb=tb, k=k, bank=bank, wt=wt, st=st, sp=sp: nc.tensor.matmul(
                    out=ps[bank].ap()[:], lhsT=wt[o % 2][:, k * P:(k + 1) * P],
                    rhs=xn_res[:, k * T + tb * 512:k * T + (tb + 1) * 512],
                    start=st, stop=sp), s_pm if sp else None)

    b.wait("scalar", s_sld, 32)
    for o in range(OT1):
        for tb in range(NBT):
            g_all = o * 16 + tb
            bank = g_all % 6
            b.wait("scalar", s_pm, g_all + 1)
            if tb == 0 and o >= 2:
                sl_h.wait("scalar", o % 2, 16 * (o // 2))
            b.op("scalar", lambda o=o, tb=tb, bank=bank: nc.scalar.activation(
                out=hrow[o % 2][:, tb * 512:(tb + 1) * 512], in_=ps[bank].ap()[:],
                func=AF.Silu, scale=sg_sb[:, o:o + 1]))
            b.pub("scalar", s_evg)

    for o in range(OT1):
        b.wait("gpsimd", s_evu, (o + 1) * 8)
        sl_h.load("gpsimd", o % 2, lambda o=o: nc.gpsimd.dma_start(out=h_sc[o], in_=hrow[o % 2][:]))
    for i in range(2):
        sl_h.wait("gpsimd", i, 16 * ((OT1 + 1 - i) // 2))
    b.barrier()
    r.close()


def emit_l1f_down(b, c, lay, xin, xout):
    nc = b.nc
    r = b.region()
    KOC = [8] * 8
    NTQ = 4
    ydown = b.sb(r, "e3y", [P, KT, 1024], F32)
    wd8 = [b.sb(r, f"e3w8{i}", [P, 8, KT * P], I8) for i in range(2)]
    wdc = b.sb(r, "e3wc", [P, 8, KT * P], BF16)
    hq = [b.sb(r, f"e3h{i}", [P, 8, 1024], BF16) for i in range(2)]
    sd_sb = b.sb(r, "e3sd", [P, KT], F32)
    ps = [b.ps(r, f"e3ps{i}", [P, 512]) for i in range(4)]
    pbufs = mk_postln_bufs(b, r, "e3p", c)
    E = mk_postln_sems(b, "e3p")
    s_cst = b.sem("e3cst")
    emit_postln_consts(b, pbufs, c, s_cst)
    b.op("sync", lambda: nc.sync.dma_start(out=sd_sb[:], in_=b.din("l1_sd", lay["l1_sd"])), s_cst, 16)
    wd_d = b.din("l1_wdk", lay["l1_wdk"])
    h_sc = b._dram["h2_sc"]
    sl_wd = Slots(b, "e3wd", 2)
    sl_hq = Slots(b, "e3hq", 2)
    s_mm = b.sem("e3mm")
    s_ev = b.sem("e3ev")
    s_wdc = b.sem("e3wdc")

    tgt_wd, tgt_hq = {}, {}
    wd_cnt, hq_cnt = [0, 0], [0, 0]

    def unit(tq, ci, j, tb):
        return ((tq * 8 + ci) * KT + j) * 2 + tb

    parts = {}
    for tq in range(NTQ):
        for ci, kosz in enumerate(KOC):
            ko0 = sum(KOC[:ci])
            idx = tq * 8 + ci
            slot = idx % 2
            if idx >= 2:
                b.wait("sync", s_mm, (idx - 1) * 32)
            for kk in range(kosz):
                sl_wd.load("sync", slot, lambda ko0=ko0, kk=kk, slot=slot: nc.sync.dma_start(
                    out=wd8[slot].ap()[:, kk, :], in_=wd_d[ko0 + kk].rearrange("p j m -> p (j m)")))
                sl_hq.load("sync", slot, lambda ko0=ko0, kk=kk, slot=slot, tq=tq: nc.sync.dma_start(
                    out=hq[slot].ap()[:, kk, :], in_=h_sc[ko0 + kk][:, tq * 1024:(tq + 1) * 1024]))
            wd_cnt[slot] += 16 * kosz
            hq_cnt[slot] += 16 * kosz
            tgt_wd[idx] = wd_cnt[slot]
            tgt_hq[idx] = hq_cnt[slot]
        for gg in range(2):
            g = tq * 2 + gg
            parts[g] = emit_postln_group(
                b, "e3p", E, g, g, (lambda j, gg=gg: ydown.ap()[:, j, gg * 512:(gg + 1) * 512]),
                xin, xout, pbufs)
        for gg in range(2):
            parts[tq * 2 + gg][4]()

    for tq in range(NTQ):
        if tq >= 1:
            b.wait("vector", E["ap"], tq * 16)
        for ci, kosz in enumerate(KOC):
            idx = tq * 8 + ci
            slot = idx % 2
            sl_wd.wait("vector", slot, tgt_wd[idx])
            if idx >= 1:
                b.wait("vector", s_mm, idx * 32)
            for kk in range(kosz):
                b.op("vector", lambda slot=slot, kk=kk: nc.vector.tensor_copy(
                    out=wdc.ap()[:, kk, :], in_=wd8[slot].ap()[:, kk, :]))
            b.pub("vector", s_wdc)
            for j in range(KT):
                for tb in range(2):
                    u = unit(tq, ci, j, tb)
                    b.wait("vector", s_mm, u + 1)
                    if ci == 0:
                        b.op("vector", lambda j=j, tb=tb, bank=(j % 2) * 2 + tb: nc.vector.tensor_scalar(
                            out=ydown.ap()[:, j, tb * 512:(tb + 1) * 512], in0=ps[bank].ap()[:],
                            scalar1=sd_sb[:, j:j + 1], scalar2=None, op0=ALU.mult))
                    else:
                        b.op("vector", lambda j=j, tb=tb, bank=(j % 2) * 2 + tb: nc.vector.scalar_tensor_tensor(
                            out=ydown.ap()[:, j, tb * 512:(tb + 1) * 512], in0=ps[bank].ap()[:],
                            scalar=sd_sb[:, j:j + 1],
                            in1=ydown.ap()[:, j, tb * 512:(tb + 1) * 512],
                            op0=ALU.mult, op1=ALU.add))
                    b.pub("vector", s_ev)
        for gg in range(2):
            parts[tq * 2 + gg][1]()

    b.wait("tensor", s_cst, 64)
    for tq in range(NTQ):
        for ci, kosz in enumerate(KOC):
            idx = tq * 8 + ci
            slot = idx % 2
            sl_hq.wait("tensor", slot, tgt_hq[idx])
            b.wait("tensor", s_wdc, idx + 1)
            for j in range(KT):
                for kk in range(kosz):
                    for tb in range(2):
                        bank = (j % 2) * 2 + tb
                        u = unit(tq, ci, j, tb)
                        if kk == 0 and u >= 4:
                            b.wait("tensor", s_ev, u - 3)
                        st, sp = (kk == 0), (kk == kosz - 1)
                        b.op("tensor", lambda slot=slot, kk=kk, j=j, tb=tb, bank=bank, st=st, sp=sp: nc.tensor.matmul(
                            out=ps[bank].ap()[:], lhsT=wdc[:, kk, j * P:(j + 1) * P],
                            rhs=hq[slot].ap()[:, kk, tb * 512:(tb + 1) * 512],
                            start=st, stop=sp), s_mm if sp else None)
        for gg in range(2):
            parts[tq * 2 + gg][0]()

    for tq in range(NTQ):
        for gg in range(2):
            parts[tq * 2 + gg][2]()
    for tq in range(NTQ):
        for gg in range(2):
            parts[tq * 2 + gg][3]()

    for i in range(2):
        E["sl_xwV"].wait("sync", i, 16 * (NTQ * 2 * (KT // 2) // 2))
        E["sl_xwP"].wait("sync", i, 16 * (NTQ * 2 * (KT // 2) // 2))
    b.barrier()
    r.close()


def ph_l1f(b, c, lay, xin, xout):
    pr = b.region()
    xn = emit_preln(b, "e1", xin, c, pr)
    emit_l1f_gateup(b, c, lay, xn, pr)
    pr.close()
    emit_l1f_down(b, c, lay, xin, xout)


# version 07
# speedup vs baseline: 1.0242x; 1.0242x over previous
import math
from contextlib import ExitStack

import numpy as np
import ml_dtypes

import concourse.bass as bass
import concourse.mybir as mybir
from concourse.bass_utils import run_bass_kernel_spmd

F32 = mybir.dt.float32
BF16 = mybir.dt.bfloat16
I32 = mybir.dt.int32
I8 = mybir.dt.int8
AF = mybir.ActivationFunctionType
ALU = mybir.AluOpType
BF = ml_dtypes.bfloat16

P = 128
T = 4096
D = 2048
KT = D // P
NBT = T // 512
H = 16
DH = 128
C0 = 128
NC0 = T // C0
BLK1 = 256
NB1 = T // BLK1
I0 = 5504
OT0 = I0 // P
I1 = 8192
OT1 = I1 // P
N_CORES = 8

ENGINES = ("sync", "scalar", "vector", "tensor", "gpsimd")



def blkW(W):
    K, M = W.shape
    kt, mt = K // P, M // P
    return np.ascontiguousarray(
        W.reshape(kt, P, mt, P).transpose(2, 1, 0, 3).reshape(mt, P, kt * P))


def fm(x):
    return np.ascontiguousarray(x.T.reshape(KT, P, T))


def col_blocks(v):
    return np.ascontiguousarray(np.asarray(v).reshape(-1, P, 1))


def mk_consts():
    c = {}
    c["ones_bf"] = np.ones((P, 1), dtype=BF)
    c["ones_row_f32"] = np.ones((1, P), dtype=np.float32)
    c["ident"] = np.eye(P, dtype=BF)

    freq = (1.0 / 1024.0) ** np.linspace(0.0, 1.0, DH // 4, dtype=np.float32)
    freq = np.concatenate([freq, np.zeros(DH // 4, dtype=np.float32)])
    t = np.arange(T, dtype=np.float32)
    theta = (t[:, None] * freq[None, :]).astype(np.float32).astype(np.float64)
    cosT = np.cos(theta).astype(np.float32).T
    sinT = np.sin(theta).astype(np.float32).T
    c["cos2"] = np.ascontiguousarray(np.concatenate([cosT, cosT], axis=0))
    c["sin2pm"] = np.ascontiguousarray(np.concatenate([sinT, -sinT], axis=0))
    sw = np.zeros((P, P), dtype=BF)
    for m in range(P):
        sw[(m + 64) % P, m] = 1
    c["swapmat"] = sw

    start = 2.0 ** (-2.0 ** (-(math.log2(H) - 3)))
    slopes = np.array([start * start ** i for i in range(H)], dtype=np.float64)
    r = np.exp(-slopes)
    i_ = np.arange(C0, dtype=np.float64)
    qdec = r[:, None] ** (i_[None, :] + 1.0)
    kdec = r[:, None] ** (C0 - 1.0 - i_[None, :])
    diff = i_[:, None] - i_[None, :]
    dmask = np.where(diff >= 0, r[:, None, None] ** np.maximum(diff, 0.0)[None], 0.0)
    c["qdec_b"] = np.ascontiguousarray(
        np.broadcast_to(qdec[:, None, :], (H, P, C0)).astype(np.float32))
    c["kdec_pc"] = np.ascontiguousarray(kdec.astype(np.float32).T)
    c["dmaskT"] = np.ascontiguousarray(dmask.transpose(0, 2, 1).astype(np.float32))
    c["rC_pc"] = np.ascontiguousarray(
        np.broadcast_to((r ** float(C0)).astype(np.float32)[None, :], (P, H)))

    arr = np.arange(1, BLK1 + 1, dtype=np.float32)
    c["l1_arr_row_b"] = np.ascontiguousarray(np.broadcast_to(arr[None, :], (P, BLK1)))
    c["l1_rev_pc"] = np.ascontiguousarray((float(BLK1) - arr).reshape(2, P).T)
    idx = arr[:, None] - arr[None, :]
    c["l1_posT"] = np.ascontiguousarray(
        np.maximum(idx, 0.0).T.reshape(2, P, BLK1).astype(np.float32))
    c["l1_maskT"] = np.ascontiguousarray(
        (idx >= 0.0).T.astype(np.float32).reshape(2, P, BLK1))
    c["l1_blk_const"] = np.full((P, 1), float(BLK1), dtype=np.float32)
    cp = np.zeros((P, 8), dtype=np.float32)
    cp[:, 0] = 1e-5
    cp[:, 1] = 1e-6
    cp[:, 2] = 1.0 / D
    cp[:, 3] = -0.5
    cp[:, 4] = float(BLK1)
    c["cpack"] = cp
    return c


def mk_layouts(inp):
    d = {}
    x = np.asarray(inp["L0_attn_x"])
    d["x0"] = [fm(x[b]) for b in range(N_CORES)]

    d["l0_wq"] = blkW(np.asarray(inp["L0_attn_Wq"]))
    d["l0_wk"] = blkW(np.asarray(inp["L0_attn_Wk"]))
    d["l0_wv"] = blkW(np.asarray(inp["L0_attn_Wv"]))
    d["l0_wo"] = blkW(np.asarray(inp["L0_attn_Wo"]))
    d["l0_rmsw"] = np.ascontiguousarray(
        np.asarray(inp["L0_attn_rms_w"]).reshape(KT, P).T.astype(np.float32))

    lut = np.asarray(inp["L0_ffn_lut"])
    d["l0_lut"] = np.ascontiguousarray(np.broadcast_to(lut[None, :], (P, 16)).astype(np.float32))
    d["l0_giT"] = np.ascontiguousarray(np.asarray(inp["L0_ffn_gate_idx"]).T.reshape(KT, P, I0))
    d["l0_uiT"] = np.ascontiguousarray(np.asarray(inp["L0_ffn_up_idx"]).T.reshape(KT, P, I0))
    d["l0_diT"] = np.ascontiguousarray(np.asarray(inp["L0_ffn_down_idx"]).T.reshape(OT0, P, D))

    sl = np.asarray(inp["L1_attn_slope_rate"]).reshape(H)
    d["l1_slope_b"] = np.ascontiguousarray(np.broadcast_to(sl[None, :], (P, H)).astype(np.float32))
    d["l1_wqkvT"] = blkW(np.ascontiguousarray(np.asarray(inp["L1_attn_w_qkv"]).T))
    d["l1_wgateT"] = blkW(np.ascontiguousarray(np.asarray(inp["L1_attn_w_gate"]).T))
    d["l1_woutT"] = blkW(np.ascontiguousarray(np.asarray(inp["L1_attn_w_out"]).T))
    d["l1_normw"] = np.ascontiguousarray(
        np.asarray(inp["L1_attn_norm_w"]).reshape(KT, P).T.astype(np.float32))

    d["l1_wg8"] = blkW(np.asarray(inp["L1_ffn_w_gate"]))
    d["l1_wu8"] = blkW(np.asarray(inp["L1_ffn_w_up"]))
    d["l1_wdk"] = np.ascontiguousarray(
        np.asarray(inp["L1_ffn_w_down"]).reshape(OT1, P, KT, P))
    d["l1_sg"] = np.ascontiguousarray(np.asarray(inp["L1_ffn_s_gate"]).reshape(OT1, P).T.astype(np.float32))
    d["l1_su"] = np.ascontiguousarray(np.asarray(inp["L1_ffn_s_up"]).reshape(OT1, P).T.astype(np.float32))
    d["l1_sd"] = np.ascontiguousarray(np.asarray(inp["L1_ffn_s_down"]).reshape(KT, P).T.astype(np.float32))
    return d


class Slots:

    def __init__(self, b, name, n, persistent=False):
        self.b = b
        self.sems = [b.sem(f"{name}_{i}", persistent=persistent) for i in range(n)]
        self.uses = [0] * n

    def load(self, eng, slot, fn):
        self.uses[slot] += 1
        self.b.op(eng, fn, self.sems[slot], 16)
        return 16 * self.uses[slot]

    def wait(self, eng, slot, count):
        self.b.wait(eng, self.sems[slot], count)


def np_dt(dt):
    dt = np.dtype(dt)
    if dt == np.float32:
        return F32
    if dt == np.int32:
        return I32
    if dt == np.int8:
        return I8
    if dt == np.dtype(BF):
        return BF16
    raise ValueError(dt)



class Build:
    def __init__(self):
        self.nc = bass.Bass("TRN2", target_bir_lowering=False)
        self.prog = {e: [] for e in ENGINES}
        self.stack = ExitStack()
        self.sem_stack = ExitStack()
        self._sems = {}
        self._bar = None
        self.host = {}
        self._dram = {}
        self.bar_n = 0

    def sem(self, name, persistent=False):
        if name == "bar":
            if self._bar is None:
                self._bar = self.stack.enter_context(self.nc.semaphore(name))
            return self._bar
        if not hasattr(self, "_psems"):
            self._psems = {}
        if persistent or name in self._psems:
            if name not in self._psems:
                self._psems[name] = self.stack.enter_context(self.nc.semaphore(name))
            return self._psems[name]
        if name not in self._sems:
            self._sems[name] = self.sem_stack.enter_context(self.nc.semaphore(name))
        return self._sems[name]

    def phase_sems_reset(self):
        self.barrier()
        e = self.nc.sync
        for h in list(self._sems.values()):
            self.prog["sync"].append(lambda h=h: e.sem_clear(h))
        self.barrier()
        self.sem_stack.close()
        self.sem_stack = ExitStack()
        self._sems = {}

    def din(self, name, arr):
        if name not in self._dram:
            a = arr[0] if isinstance(arr, list) else arr
            tn = self.nc.dram_tensor(name, list(a.shape), np_dt(a.dtype), kind="ExternalInput")
            self._dram[name] = tn.ap()
            self.host[name] = arr
        return self._dram[name]

    def dout(self, name, shape, dtype):
        if name not in self._dram:
            self._dram[name] = self.nc.dram_tensor(name, list(shape), dtype, kind="ExternalOutput").ap()
        return self._dram[name]

    def dscratch(self, name, shape, dtype):
        if name not in self._dram:
            self._dram[name] = self.nc.dram_tensor(name, list(shape), dtype).ap()
        return self._dram[name]

    def region(self):
        return ExitStack()

    def sb(self, reg, name, shape, dtype, **kw):
        return reg.enter_context(self.nc.sbuf_tensor(name, list(shape), dtype, **kw))

    def ps(self, reg, name, shape, dtype=F32):
        return reg.enter_context(self.nc.psum_tensor(name, list(shape), dtype))

    def ei(self, eng, fn):
        self.prog[eng].append(fn)

    def wait(self, eng, sem, val):
        s = self.sem(sem) if isinstance(sem, str) else sem
        e = getattr(self.nc, eng)
        self.prog[eng].append(lambda: e.wait_ge(s, val))

    def op(self, eng, fn, sem=None, inc=1):
        s = self.sem(sem) if isinstance(sem, str) else sem
        if s is None:
            self.prog[eng].append(fn)
        else:
            self.prog[eng].append(lambda: fn().then_inc(s, inc))

    def pub(self, eng, sem, inc=1):
        s = self.sem(sem) if isinstance(sem, str) else sem
        e = getattr(self.nc, eng)
        self.prog[eng].append(lambda: e.drain().then_inc(s, inc))

    def barrier(self):
        s = self.sem("bar")
        self.bar_n += len(ENGINES)
        tgt = self.bar_n
        for eng in ENGINES:
            e = getattr(self.nc, eng)
            if eng == "sync":
                self.prog[eng].append(lambda e=e: e.nop().then_inc(s, 1))
            else:
                self.prog[eng].append(lambda e=e: e.drain().then_inc(s, 1))
        for eng in ENGINES:
            e = getattr(self.nc, eng)
            self.prog[eng].append(lambda e=e: e.wait_ge(s, tgt))

    def finalize(self, final_waits=()):
        for sem, val in final_waits:
            self.wait("sync", sem, val)
        nc = self.nc
        with self.stack:
            with nc.Block() as block:
                def mk(eng):
                    def run(_):
                        for f in self.prog[eng]:
                            f()
                    return run
                block.sync(mk("sync"))
                block.scalar(mk("scalar"))
                block.vector(mk("vector"))
                block.tensor(mk("tensor"))
                block.gpsimd(mk("gpsimd"))
        return nc



def emit_preln(b, tag, xin, c, pr):
    nc = b.nc
    r0 = b.region()
    ones_sb = b.sb(r0, f"{tag}ones", [P, 1], BF16)
    oner_sb = b.sb(r0, f"{tag}oner", [1, P], F32)
    cp_sb = b.sb(r0, f"{tag}cp", [P, 8], F32)
    rowr = [b.sb(r0, f"{tag}rr{i}", [1, 512], F32) for i in range(2)]
    rowmr = [b.sb(r0, f"{tag}rm{i}", [1, 512], F32) for i in range(2)]
    stps = b.ps(r0, f"{tag}stps", [P, 4096])

    sl_x = Slots(b, tag + "x", 2)
    s_c = b.sem(tag + "c")
    s_q = b.sem(tag + "q")
    s_m = b.sem(tag + "m")
    s_1 = b.sem(tag + "1")

    r1 = b.region()
    xa = [b.sb(r1, f"{tag}xa{i}", [P, T], F32) for i in range(2)]
    xb = [b.sb(r1, f"{tag}xb{i}", [P, T], BF16) for i in range(2)]
    xq = [b.sb(r1, f"{tag}xq{i}", [P, T], BF16) for i in range(2)]

    b.op("sync", lambda: nc.sync.dma_start(out=ones_sb[:], in_=b.din("ones_bf", c["ones_bf"])), s_1, 16)
    b.op("sync", lambda: nc.sync.dma_start(out=oner_sb[:], in_=b.din("ones_row_f32", c["ones_row_f32"])), s_1, 16)
    b.op("sync", lambda: nc.sync.dma_start(out=cp_sb[:], in_=b.din("cpack", c["cpack"])), s_1, 16)
    for k in range(KT):
        if k >= 2:
            b.wait("sync", s_c, k - 1)
            b.wait("sync", s_q, k - 1)
        sl_x.load("sync", k % 2, lambda k=k: nc.sync.dma_start(out=xa[k % 2][:], in_=xin[k]))

    for k in range(KT):
        sl_x.wait("vector", k % 2, 16 * (k // 2 + 1))
        if k >= 2:
            b.wait("vector", s_m, (k - 1) * NBT)
        b.op("vector", lambda k=k: nc.vector.tensor_copy(out=xb[k % 2][:], in_=xa[k % 2][:]))
        b.pub("vector", s_c)

    for k in range(KT):
        sl_x.wait("scalar", k % 2, 16 * (k // 2 + 1))
        if k >= 2:
            b.wait("scalar", s_m, (k - 1) * NBT)
        b.op("scalar", lambda k=k: nc.scalar.activation(out=xq[k % 2][:], in_=xa[k % 2][:], func=AF.Square))
        b.pub("scalar", s_q)

    b.wait("tensor", s_1, 48)
    for k in range(KT):
        b.wait("tensor", s_c, k + 1)
        b.wait("tensor", s_q, k + 1)
        for tb in range(NBT):
            st, sp = (k == 0), (k == KT - 1)
            b.op("tensor", lambda k=k, tb=tb, st=st, sp=sp: nc.tensor.matmul(
                out=stps.ap()[0:1, tb * 512:(tb + 1) * 512], lhsT=ones_sb[:],
                rhs=xb[k % 2][:, tb * 512:(tb + 1) * 512], start=st, stop=sp))
            b.op("tensor", lambda k=k, tb=tb, st=st, sp=sp: nc.tensor.matmul(
                out=stps.ap()[64:65, tb * 512:(tb + 1) * 512], lhsT=ones_sb[:],
                rhs=xq[k % 2][:, tb * 512:(tb + 1) * 512], start=st, stop=sp), s_m)

    b.barrier()
    r1.close()
    xn_res = b.sb(pr, tag + "xn", [P, KT * T], BF16, side="right")

    r2 = b.region()
    r_b = [b.sb(r2, f"{tag}rb{i}", [P, 512], F32) for i in range(2)]
    mr_b = [b.sb(r2, f"{tag}mrb{i}", [P, 512], F32) for i in range(2)]
    xaV = [b.sb(r2, f"{tag}xaV{i}", [P, 512], F32) for i in range(2)]
    xaP = [b.sb(r2, f"{tag}xaP{i}", [P, 512], F32) for i in range(2)]
    t1V = b.sb(r2, f"{tag}t1V", [P, 512], F32)
    t1P = b.sb(r2, f"{tag}t1P", [P, 512], F32)
    mq = b.sb(r2, f"{tag}mq", [1, NBT * 1024], F32)
    msq = b.sb(r2, f"{tag}msq", [1, 512], F32)

    sl_xv = Slots(b, tag + "xv", 2)
    sl_xp = Slots(b, tag + "xp", 2)
    s_mq = b.sem(tag + "mq")
    s_rmv = b.sem(tag + "rmv")
    s_rl = b.sem(tag + "rl")
    s_row = b.sem(tag + "row")
    s_rp = b.sem(tag + "rp")
    s_rev = b.sem(tag + "rev")
    s_av = b.sem(tag + "av")
    s_ap_ = b.sem(tag + "ap")

    def _v_apply(tb):
        for k in range(0, KT, 2):
            nv = tb * 8 + k // 2 + 1
            sl_xv.wait("vector", (nv - 1) % 2, 16 * ((nv - 1) // 2 + 1))
            b.wait("vector", s_rev, tb + 1)
            b.op("vector", lambda tb=tb, k=k, nv=nv: nc.vector.tensor_tensor(
                out=t1V[:], in0=xaV[(nv - 1) % 2][:], in1=r_b[tb % 2][:], op=ALU.mult))
            b.op("vector", lambda tb=tb, k=k: nc.vector.tensor_tensor(
                out=xn_res[:, k * T + tb * 512: k * T + (tb + 1) * 512],
                in0=t1V[:], in1=mr_b[tb % 2][:], op=ALU.subtract), s_av)

    for tb in range(NBT):
        sl = (tb * 512, (tb + 1) * 512)
        b.op("vector", lambda tb=tb, sl=sl: nc.vector.tensor_scalar(
            out=mq[0:1, tb * 1024:tb * 1024 + 512], in0=stps.ap()[0:1, sl[0]:sl[1]],
            scalar1=1.0 / D, scalar2=None, op0=ALU.mult))
        b.op("vector", lambda tb=tb, sl=sl: nc.vector.tensor_scalar(
            out=mq[0:1, tb * 1024 + 512:tb * 1024 + 1024], in0=stps.ap()[64:65, sl[0]:sl[1]],
            scalar1=1.0 / D, scalar2=None, op0=ALU.mult))
    b.pub("vector", s_mq)
    for tb in range(NBT + 1):
        if tb < NBT:
            if tb >= 2:
                b.wait("vector", s_rp, tb - 1)
            b.op("vector", lambda tb=tb: nc.vector.tensor_tensor(
                out=msq[0:1, :], in0=mq[0:1, tb * 1024:tb * 1024 + 512],
                in1=mq[0:1, tb * 1024:tb * 1024 + 512], op=ALU.mult))
            b.op("vector", lambda tb=tb: nc.vector.tensor_tensor(
                out=mq[0:1, tb * 1024 + 512:tb * 1024 + 1024],
                in0=mq[0:1, tb * 1024 + 512:tb * 1024 + 1024], in1=msq[0:1, :], op=ALU.subtract))
            b.pub("vector", s_rmv)
            b.wait("vector", s_rl, tb + 1)
            b.op("vector", lambda tb=tb: nc.vector.tensor_tensor(
                out=rowmr[tb % 2][0:1, :], in0=mq[0:1, tb * 1024:tb * 1024 + 512],
                in1=rowr[tb % 2][0:1, :], op=ALU.mult))
            b.pub("vector", s_row)
        if tb >= 1:
            _v_apply(tb - 1)

    b.wait("scalar", s_1, 48)
    for tb in range(NBT):
        b.wait("scalar", s_rmv, tb + 1)
        if tb >= 2:
            b.wait("scalar", s_rp, tb - 1)
        b.op("scalar", lambda tb=tb: nc.scalar.activation(
            out=rowr[tb % 2][0:1, :], in_=mq[0:1, tb * 1024 + 512:tb * 1024 + 1024],
            func=AF.Ln, bias=cp_sb[0:1, 0:1]))
        b.op("scalar", lambda tb=tb: nc.scalar.activation(
            out=rowr[tb % 2][0:1, :], in_=rowr[tb % 2][0:1, :], func=AF.Exp, scale=-0.5))
        b.pub("scalar", s_rl)
        b.wait("scalar", s_rp, tb + 1)
        if tb >= 2:
            b.wait("scalar", s_av, (tb - 1) * NBT)
            b.wait("scalar", s_ap_, (tb - 1) * NBT)
        b1, b2 = (2 * tb) % 8, (2 * tb + 1) % 8
        b.op("scalar", lambda tb=tb, b1=b1: nc.scalar.copy(
            out=r_b[tb % 2][:], in_=stps.ap()[:, b1 * 512:(b1 + 1) * 512]))
        b.op("scalar", lambda tb=tb, b2=b2: nc.scalar.copy(
            out=mr_b[tb % 2][:], in_=stps.ap()[:, b2 * 512:(b2 + 1) * 512]))
        b.pub("scalar", s_rev)

    b.wait("tensor", s_mq, 1)
    for tb in range(NBT):
        b.wait("tensor", s_row, tb + 1)
        if tb >= 4:
            b.wait("tensor", s_rev, tb - 3)
        b1, b2 = (2 * tb) % 8, (2 * tb + 1) % 8
        b.op("tensor", lambda tb=tb, b1=b1: nc.tensor.matmul(
            out=stps.ap()[:, b1 * 512:(b1 + 1) * 512], lhsT=oner_sb[:],
            rhs=rowr[tb % 2][0:1, :], start=True, stop=True))
        b.op("tensor", lambda tb=tb, b2=b2: nc.tensor.matmul(
            out=stps.ap()[:, b2 * 512:(b2 + 1) * 512], lhsT=oner_sb[:],
            rhs=rowmr[tb % 2][0:1, :], start=True, stop=True), s_rp)

    nv = np_ = 0
    for tb in range(NBT):
        for k in range(KT):
            if k % 2 == 0:
                nv += 1
                if nv > 2:
                    b.wait("sync", s_av, nv - 2)
                sl_xv.load("sync", (nv - 1) % 2, lambda tb=tb, k=k, nv=nv: nc.sync.dma_start(
                    out=xaV[(nv - 1) % 2][:], in_=xin[k][:, tb * 512:(tb + 1) * 512]))
            else:
                np_ += 1
                if np_ > 2:
                    b.wait("sync", s_ap_, np_ - 2)
                sl_xp.load("sync", (np_ - 1) % 2, lambda tb=tb, k=k, np_=np_: nc.sync.dma_start(
                    out=xaP[(np_ - 1) % 2][:], in_=xin[k][:, tb * 512:(tb + 1) * 512]))

    for tb in range(NBT):
        for k in range(1, KT, 2):
            np2 = tb * 8 + (k - 1) // 2 + 1
            sl_xp.wait("gpsimd", (np2 - 1) % 2, 16 * ((np2 - 1) // 2 + 1))
            b.wait("gpsimd", s_rev, tb + 1)
            b.op("gpsimd", lambda tb=tb, k=k, np2=np2: nc.gpsimd.tensor_tensor(
                out=t1P[:], in0=xaP[(np2 - 1) % 2][:], in1=r_b[tb % 2][:], op=ALU.mult))
            b.op("gpsimd", lambda tb=tb, k=k: nc.gpsimd.tensor_tensor(
                out=xn_res[:, k * T + tb * 512: k * T + (tb + 1) * 512],
                in0=t1P[:], in1=mr_b[tb % 2][:], op=ALU.subtract), s_ap_)

    b.barrier()
    r2.close()
    r0.close()
    return xn_res


def emit_l0a_qkv(b, c, lay, xn_res, phase_reg):
    nc = b.nc
    r = b.region()
    wfp = [b.sb(r, f"a2wf{i}", [P, KT * P], F32) for i in range(2)]
    wb = [b.sb(r, f"a2wb{i}", [P, KT * P], BF16) for i in range(2)]
    qtmp = [b.sb(r, f"a2qt{i}", [P, 512], BF16) for i in range(2)]
    rt1 = b.sb(r, "a2rt1", [P, 512], F32)
    rt2 = b.sb(r, "a2rt2", [P, 512], F32)
    orow = b.sb(r, "a2orow", [P, T], BF16)
    cos2 = b.sb(r, "a2cos2", [P, T], F32)
    sin2 = b.sb(r, "a2sin2", [P, T], F32)
    swp = b.sb(r, "a2swp", [P, P], BF16)
    ps = [b.ps(r, f"a2ps{i}", [P, 512]) for i in range(6)]
    psw = [b.ps(r, f"a2psw{i}", [P, 512]) for i in range(2)]

    sl_wl = Slots(b, "a2wl", 2)
    sl_out = Slots(b, "a2out", NBT, persistent=True)
    s_wc = b.sem("a2wc")
    s_pm = b.sem("a2pm")
    s_sw = b.sem("a2sw")
    s_ev = b.sem("a2ev")
    s_evd = b.sem("a2evd")
    s_rotq = b.sem("a2rotq")
    s_rotv = b.sem("a2rotv")
    s_cst = b.sem("a2cst")

    mk = b.dout if DBG == "qkv" else b.dscratch
    q_sc = mk("q_sc", [NC0, P, H * P], BF16)
    k_sc = mk("k_sc", [NC0, P, H * P], BF16)
    v_sc = mk("v_sc", [NC0, P, H * P], BF16)
    mats = [(b.din("l0_wq", lay["l0_wq"]), q_sc, True),
            (b.din("l0_wk", lay["l0_wk"]), k_sc, True),
            (b.din("l0_wv", lay["l0_wv"]), v_sc, False)]

    b.op("sync", lambda: nc.sync.dma_start(out=cos2[:], in_=b.din("cos2", c["cos2"])), s_cst, 16)
    b.op("sync", lambda: nc.sync.dma_start(out=sin2[:], in_=b.din("sin2pm", c["sin2pm"])), s_cst, 16)
    b.op("sync", lambda: nc.sync.dma_start(out=swp[:], in_=b.din("swapmat", c["swapmat"])), s_cst, 16)

    for jj in range(3 * H):
        if jj >= 2:
            b.wait("sync", s_wc, jj - 1)
        wa, _, _ = mats[jj // H]
        sl_wl.load("sync", jj % 2, lambda jj=jj, wa=wa: nc.sync.dma_start(
            out=wfp[jj % 2][:], in_=wa[jj % H]))

    b.wait("vector", s_cst, 48)
    for jj in range(3 * H):
        rot = mats[jj // H][2]
        sl_wl.wait("vector", jj % 2, 16 * (jj // 2 + 1))
        if jj >= 2:
            b.wait("vector", s_pm, (jj - 1) * 8)
        b.op("vector", lambda jj=jj: nc.vector.tensor_copy(out=wb[jj % 2][:], in_=wfp[jj % 2][:]))
        b.pub("vector", s_wc)
        if rot:
            for tb in range(NBT):
                g = jj * 8 + tb
                b.wait("vector", s_evd, g + 1)
                b.wait("vector", s_sw, g + 1)
                if jj >= 1:
                    sl_out.wait("vector", tb, 16 * jj)
                sl = (tb * 512, (tb + 1) * 512)
                b.op("vector", lambda g=g, sl=sl: nc.vector.tensor_tensor(
                    out=rt1[:], in0=qtmp[g % 2][:], in1=cos2[:, sl[0]:sl[1]], op=ALU.mult))
                b.op("vector", lambda g=g, sl=sl: nc.vector.tensor_tensor(
                    out=rt2[:], in0=psw[g % 2].ap()[:], in1=sin2[:, sl[0]:sl[1]], op=ALU.mult))
                b.op("vector", lambda sl=sl: nc.vector.tensor_tensor(
                    out=orow[:, sl[0]:sl[1]], in0=rt1[:], in1=rt2[:], op=ALU.add))
                b.pub("vector", s_rotq)

    for jj in range(3 * H):
        rot = mats[jj // H][2]
        for tb in range(NBT):
            g = jj * 8 + tb
            b.wait("scalar", s_pm, g + 1)
            if rot:
                if g >= 2:
                    b.wait("scalar", s_rotq, g - 1)
                    b.wait("scalar", s_sw, g - 1)
                b.op("scalar", lambda g=g: nc.scalar.copy(
                    out=qtmp[g % 2][:], in_=ps[g % 6].ap()[:]), s_ev)
                b.pub("scalar", s_evd)
            else:
                if jj >= 1:
                    sl_out.wait("scalar", tb, 16 * jj)
                b.op("scalar", lambda g=g, tb=tb: nc.scalar.copy(
                    out=orow[:, tb * 512:(tb + 1) * 512], in_=ps[g % 6].ap()[:]), s_ev)
                b.pub("scalar", s_rotv)

    b.wait("tensor", s_cst, 48)
    pend_swaps = []
    for jj in range(3 * H):
        rot = mats[jj // H][2]
        b.wait("tensor", s_wc, jj + 1)
        for tb in range(NBT):
            g = jj * 8 + tb
            if g >= 6:
                b.wait("tensor", s_ev, g - 5)
            for k in range(KT):
                st, sp = (k == 0), (k == KT - 1)
                b.op("tensor", lambda jj=jj, tb=tb, k=k, g=g, st=st, sp=sp: nc.tensor.matmul(
                    out=ps[g % 6].ap()[:], lhsT=wb[jj % 2][:, k * P:(k + 1) * P],
                    rhs=xn_res[:, k * T + tb * 512:k * T + (tb + 1) * 512],
                    start=st, stop=sp), s_pm if sp else None)
            if rot:
                pend_swaps.append(g)
            while pend_swaps and (pend_swaps[0] <= g - 2 or tb == NBT - 1):
                gs = pend_swaps.pop(0)
                b.wait("tensor", s_evd, gs + 1)
                if gs >= 2:
                    b.wait("tensor", s_rotq, gs - 1)
                b.op("tensor", lambda gs=gs: nc.tensor.matmul(
                    out=psw[gs % 2].ap()[:], lhsT=swp[:], rhs=qtmp[gs % 2][:],
                    start=True, stop=True), s_sw)
    assert not pend_swaps

    for jj in range(3 * H):
        _, dst, rot = mats[jj // H]
        j = jj % H
        for tb in range(NBT):
            g = jj * 8 + tb
            if rot:
                b.wait("gpsimd", s_rotq, g + 1)
            else:
                b.wait("gpsimd", s_rotv, (jj - 2 * H) * 8 + tb + 1)
            sl_out.load("gpsimd", tb, lambda dst=dst, j=j, tb=tb: nc.gpsimd.dma_start(
                out=dst[tb * 4:(tb + 1) * 4, :, j * P:(j + 1) * P].rearrange("c p m -> p c m"),
                in_=orow.ap()[:, tb * 512:(tb + 1) * 512].rearrange("p (c m) -> p c m", c=4)))
    for tb in range(NBT):
        sl_out.wait("gpsimd", tb, 16 * 3 * H)

    b.barrier()
    r.close()


def emit_l0a_scan(b, c, lay, xin, xout):
    nc = b.nc
    r = b.region()
    wob = b.sb(r, "a3wob", [P, H * KT * P], BF16)
    kv = b.sb(r, "a3kv", [P, H * P], F32)
    dm_sb = b.sb(r, "a3dm", [P, H * P], F32)
    qdb_sb = b.sb(r, "a3qdb", [P, H * P], F32)
    kdc_sb = b.sb(r, "a3kdc", [P, H], F32)
    rc_sb = b.sb(r, "a3rc", [P, H], F32)
    rmsw_sb = b.sb(r, "a3rmsw", [P, KT], F32)
    ids = b.sb(r, "a3ids", [P, P], BF16)
    ones_sb = b.sb(r, "a3ones", [P, 1], BF16)
    oner_sb = b.sb(r, "a3oner", [1, P], F32)
    cp_sb = b.sb(r, "a3cp", [P, 8], F32)
    wfp = [b.sb(r, "a3wf0", [P, KT * P], F32)]
    qch = [b.sb(r, f"a3q{i}", [P, H * P], BF16) for i in range(2)]
    kch = [b.sb(r, f"a3k{i}", [P, H * P], BF16) for i in range(2)]
    vch = [b.sb(r, f"a3v{i}", [P, H * P], BF16) for i in range(2)]
    ktd = b.sb(r, "a3ktd", [P, P], BF16)
    vtok = b.sb(r, "a3vtok", [P, P], BF16)
    scm = b.sb(r, "a3scm", [P, P], BF16)
    qd = b.sb(r, "a3qd", [P, P], F32)
    dkv = b.sb(r, "a3dkv", [P, P], F32)
    afm = b.sb(r, "a3afm", [P, H * P], F32)
    asq = b.sb(r, "a3asq", [P, P], BF16)
    rowb = b.sb(r, "a3rowb", [1, P], F32)
    rb_c = b.sb(r, "a3rbc", [P, P], F32)
    grbuf = b.sb(r, "a3gr", [P, H, 512], BF16)
    ywo = b.sb(r, "a3ywo", [P, KT, 512], F32)
    yb = [b.sb(r, f"a3yb{i}", [P, 512], BF16) for i in range(2)]
    ysq = [b.sb(r, f"a3ysq{i}", [P, 512], BF16) for i in range(2)]
    row2 = b.sb(r, "a3row2", [1, 2 * 512], F32)
    rowt = b.sb(r, "a3rowt", [1, 2 * 512], F32)
    rb2 = b.sb(r, "a3rb2", [P, 512], F32)
    mrb2 = b.sb(r, "a3mrb2", [P, 512], F32)
    xoldV = [b.sb(r, f"a3xoV{i}", [P, 512], F32) for i in range(2)]
    xoldP = [b.sb(r, f"a3xoP{i}", [P, 512], F32) for i in range(2)]
    ps_tk = b.ps(r, "a3ptk", [P, 1024], BF16)
    ps_tv = b.ps(r, "a3ptv", [P, 1024], BF16)
    ps_sc = b.ps(r, "a3psc", [P, 512], F32)
    ps_o = b.ps(r, "a3po", [P, 512], F32)
    ps_kv = b.ps(r, "a3pkv", [P, 512], F32)
    ps_rm = b.ps(r, "a3prm", [P, 512], F32)
    ps_wo = [b.ps(r, f"a3pw{i}", [P, 512], F32) for i in range(2)]

    q_sc, k_sc, v_sc = b._dram["q_sc"], b._dram["k_sc"], b._dram["v_sc"]

    s_ldc = b.sem("a3ldc")
    sl_wol = Slots(b, "a3wol", 1)
    s_wc = b.sem("a3wc")
    sl_ch = Slots(b, "a3ch", 2)
    s_t1 = b.sem("a3t1")
    s_t2 = b.sem("a3t2")
    s_m1 = b.sem("a3m1")
    s_o = b.sem("a3o")
    s_m5 = b.sem("a3m5")
    s_m6 = b.sem("a3m6")
    s_vt = b.sem("a3vt")
    s_dkv = b.sem("a3dkv")
    s_aq = b.sem("a3aq")
    s_scm = b.sem("a3scm")
    s_qd = b.sem("a3qd")
    s_ktd = b.sem("a3ktd")
    s_kv = b.sem("a3kv")
    s_rr = b.sem("a3rr")
    s_rp = b.sem("a3rp")
    s_rb = b.sem("a3rb")
    s_gr = b.sem("a3gr")
    s_wo = b.sem("a3wo")
    s_ye = b.sem("a3ye")
    s_cb = b.sem("a3cb")
    s_sq2 = b.sem("a3sq2")
    s_plm = b.sem("a3plm")
    s_r2 = b.sem("a3r2")
    s_rp2 = b.sem("a3rp2")
    s_rbe = b.sem("a3rbe")
    sl_xoV = Slots(b, "a3xoV", 2)
    sl_xoP = Slots(b, "a3xoP", 2)
    sl_xwV = Slots(b, "a3xwV", 2)
    sl_xwP = Slots(b, "a3xwP", 2)
    s_av = b.sem("a3av")
    s_ap = b.sem("a3ap")

    b.op("sync", lambda: nc.sync.dma_start(out=ids[:], in_=b.din("ident", c["ident"])), s_ldc, 16)
    b.op("sync", lambda: nc.sync.dma_start(out=ones_sb[:], in_=b.din("ones_bf", c["ones_bf"])), s_ldc, 16)
    b.op("sync", lambda: nc.sync.dma_start(out=oner_sb[:], in_=b.din("ones_row_f32", c["ones_row_f32"])), s_ldc, 16)
    b.op("sync", lambda: nc.sync.dma_start(out=cp_sb[:], in_=b.din("cpack", c["cpack"])), s_ldc, 16)
    b.op("sync", lambda: nc.sync.dma_start(
        out=dm_sb.ap().rearrange("p (h m) -> p h m", h=H),
        in_=b.din("dmaskT", c["dmaskT"]).rearrange("h p m -> p h m")), s_ldc, 16)
    b.op("sync", lambda: nc.sync.dma_start(
        out=qdb_sb.ap().rearrange("p (h m) -> p h m", h=H),
        in_=b.din("qdec_b", c["qdec_b"]).rearrange("h p m -> p h m")), s_ldc, 16)
    b.op("sync", lambda: nc.sync.dma_start(
        out=kdc_sb[:], in_=b.din("kdec_pc", c["kdec_pc"])), s_ldc, 16)
    b.op("sync", lambda: nc.sync.dma_start(
        out=rc_sb[:], in_=b.din("rC_pc", c["rC_pc"])), s_ldc, 16)
    b.op("sync", lambda: nc.sync.dma_start(
        out=rmsw_sb[:], in_=b.din("l0_rmsw", lay["l0_rmsw"])), s_ldc, 16)
    wo_d = b.din("l0_wo", lay["l0_wo"])
    for j in range(H):
        if j >= 1:
            b.wait("sync", s_wc, j)
        sl_wol.load("sync", 0, lambda j=j: nc.sync.dma_start(out=wfp[0][:], in_=wo_d[j]))
    b.wait("vector", s_ldc, 16 * 9)
    for j in range(H):
        sl_wol.wait("vector", 0, 16 * (j + 1))
        for k in range(KT):
            b.op("vector", lambda j=j, k=k: nc.vector.tensor_scalar(
                out=wob[:, (j * KT + k) * P:(j * KT + k + 1) * P],
                in0=wfp[0][:, k * P:(k + 1) * P],
                scalar1=rmsw_sb[:, k:k + 1], scalar2=None, op0=ALU.mult))
        b.pub("vector", s_wc)
    b.op("gpsimd", lambda: nc.gpsimd.memset(kv[:], 0.0))
    b.pub("gpsimd", s_kv)

    for ci in range(NC0):
        if ci >= 2:
            b.wait("sync", s_m5, (ci - 1) * H)
            b.wait("sync", s_ktd, (ci - 1) * H)
        sl_ch.load("sync", ci % 2, lambda ci=ci: nc.sync.dma_start(out=qch[ci % 2][:], in_=q_sc[ci]))
        sl_ch.uses[ci % 2] -= 1
        sl_ch.load("sync", ci % 2, lambda ci=ci: nc.sync.dma_start(out=kch[ci % 2][:], in_=k_sc[ci]))
        sl_ch.uses[ci % 2] -= 1
        sl_ch.load("sync", ci % 2, lambda ci=ci: nc.sync.dma_start(out=vch[ci % 2][:], in_=v_sc[ci]))
        if ci % 4 == 3:
            g = ci // 4

            def _ld(j, g=g):
                if j % 2 == 0:
                    nb = g * 8 + j // 2
                    if nb >= 2:
                        sl_xwV.wait("sync", nb % 2, 16 * (nb // 2))
                    sl_xoV.load("sync", nb % 2, lambda g=g, j=j, nb=nb: nc.sync.dma_start(
                        out=xoldV[nb % 2][:], in_=xin[j][:, g * 512:(g + 1) * 512]))
                else:
                    nb = g * 8 + (j - 1) // 2
                    if nb >= 2:
                        sl_xwP.wait("sync", nb % 2, 16 * (nb // 2))
                    sl_xoP.load("sync", nb % 2, lambda g=g, j=j, nb=nb: nc.sync.dma_start(
                        out=xoldP[nb % 2][:], in_=xin[j][:, g * 512:(g + 1) * 512]))

            def _st(j, g=g):
                if j % 2 == 0:
                    nb = g * 8 + j // 2
                    b.wait("sync", s_av, nb + 1)
                    sl_xwV.load("sync", nb % 2, lambda g=g, j=j, nb=nb: nc.sync.dma_start(
                        out=xout[j][:, g * 512:(g + 1) * 512], in_=xoldV[nb % 2][:]))
                else:
                    nb = g * 8 + (j - 1) // 2
                    b.wait("sync", s_ap, nb + 1)
                    sl_xwP.load("sync", nb % 2, lambda g=g, j=j, nb=nb: nc.sync.dma_start(
                        out=xout[j][:, g * 512:(g + 1) * 512], in_=xoldP[nb % 2][:]))

            for j in range(KT):
                if j >= 4:
                    _st(j - 4)
                _ld(j)
            for j in range(KT - 4, KT):
                _st(j)

    def hs(h):
        return slice(h * P, (h + 1) * P)

    b.wait("tensor", s_ldc, 16 * 9)
    b.wait("tensor", s_wc, H)
    for ci in range(NC0):
        g, cg = ci // 4, ci % 4
        sl_ch.wait("tensor", ci % 2, 48 * (ci // 2 + 1))
        if cg == 0 and ci >= 4:
            b.wait("tensor", s_rbe, ci // 4)
            b.wait("tensor", s_r2, 3 * (ci // 4))
        for h in range(H):
            n = ci * H + h
            if n >= 1:
                b.wait("tensor", s_ktd, n)
                b.wait("tensor", s_vt, n)
            b.op("tensor", lambda ci=ci, h=h: nc.tensor.transpose(
                out=ps_tk.ap()[:, 0:P], in_=kch[ci % 2][:, hs(h)], identity=ids[:]), s_t1)
            b.op("tensor", lambda ci=ci, h=h: nc.tensor.transpose(
                out=ps_tv.ap()[:, 0:P], in_=vch[ci % 2][:, hs(h)], identity=ids[:]), s_t2)
            if n >= 1:
                b.wait("tensor", s_ktd, n)
            b.op("tensor", lambda ci=ci, h=h: nc.tensor.matmul(
                out=ps_sc.ap()[:, 0:P], lhsT=kch[ci % 2][:, hs(h)],
                rhs=qch[ci % 2][:, hs(h)], start=True, stop=True), s_m1)
            b.wait("tensor", s_kv, (n + 2 - H) if ci >= 1 else 1)
            b.wait("tensor", s_ktd, n + 1)
            if n >= 1:
                b.wait("tensor", s_aq, n)
            b.op("tensor", lambda h=h: nc.tensor.matmul(
                out=ps_o.ap()[:, 0:P], lhsT=kv[:, hs(h)], rhs=qd[:], start=True, stop=False))
            b.wait("tensor", s_vt, n + 1)
            b.op("tensor", lambda: nc.tensor.matmul(
                out=ps_o.ap()[:, 0:P], lhsT=vtok[:], rhs=scm[:], start=False, stop=True), s_o)
            if n >= 1:
                b.wait("tensor", s_kv, n + 1)
            b.op("tensor", lambda: nc.tensor.matmul(
                out=ps_kv.ap()[:, 0:P], lhsT=ktd[:], rhs=vtok[:], start=True, stop=True), s_m5)
            b.wait("tensor", s_aq, n + 1)
            if h == 0 and ci >= 1:
                b.wait("tensor", s_rp, ci)
                b.wait("tensor", s_rb, ci)
            b.op("tensor", lambda h=h: nc.tensor.matmul(
                out=ps_rm.ap()[0:1, 0:P], lhsT=ones_sb[:], rhs=asq[:],
                start=(h == 0), stop=(h == H - 1)), s_m6)
        b.wait("tensor", s_rr, ci + 1)
        b.op("tensor", lambda: nc.tensor.matmul(
            out=ps_rm.ap()[:, P:2 * P], lhsT=oner_sb[:], rhs=rowb[0:1, :],
            start=True, stop=True), s_rp)
        if cg == 3:
            b.wait("tensor", s_gr, (ci + 1) * H)
            for j in range(KT):
                if g >= 1 or j >= 2:
                    b.wait("tensor", s_ye, g * KT + j - 1)
                for h in range(H):
                    b.op("tensor", lambda j=j, h=h, g=g: nc.tensor.matmul(
                        out=ps_wo[j % 2].ap()[:], lhsT=wob[:, (j * KT + h) * P:(j * KT + h + 1) * P],
                        rhs=grbuf.ap()[:, h, :], start=(h == 0), stop=(h == H - 1)),
                        s_wo if h == H - 1 else None, inc=H if h == H - 1 else 1)
            for j in range(KT):
                b.wait("tensor", s_cb, g * KT + j + 1)
                b.wait("tensor", s_sq2, g * KT + j + 1)
                b.op("tensor", lambda j=j: nc.tensor.matmul(
                    out=ps_sc.ap()[0:1, 0:512], lhsT=ones_sb[:], rhs=yb[j % 2][:],
                    start=(j == 0), stop=(j == KT - 1)))
                b.op("tensor", lambda j=j: nc.tensor.matmul(
                    out=ps_sc.ap()[64:65, 0:512], lhsT=ones_sb[:], rhs=ysq[j % 2][:],
                    start=(j == 0), stop=(j == KT - 1)), s_plm)
            b.wait("tensor", s_aq, (ci + 1) * H)
            b.wait("tensor", s_kv, (ci + 1) * H + 1)
            b.wait("tensor", s_r2, 3 * g + 3)
            if g >= 1:
                b.wait("tensor", s_rbe, g)
            b.op("tensor", lambda: nc.tensor.matmul(
                out=ps_o.ap()[:], lhsT=oner_sb[:], rhs=row2[0:1, 0:512], start=True, stop=True))
            b.op("tensor", lambda: nc.tensor.matmul(
                out=ps_kv.ap()[:], lhsT=oner_sb[:], rhs=row2[0:1, 512:1024], start=True, stop=True), s_rp2, inc=2)

    b.wait("vector", s_ldc, 16 * 9)
    for ci in range(NC0):
        g, cg = ci // 4, ci % 4
        sl_ch.wait("vector", ci % 2, 48 * (ci // 2 + 1))
        for h in range(H):
            n = ci * H + h
            b.wait("vector", s_t1, n + 1)
            b.wait("vector", s_m5, n)
            b.op("vector", lambda h=h: nc.vector.tensor_scalar(
                out=ktd[:], in0=ps_tk.ap()[:, 0:P], scalar1=kdc_sb[:, h:h + 1],
                scalar2=None, op0=ALU.mult))
            b.wait("vector", s_m1, n + 1)
            b.wait("vector", s_o, n)
            b.op("vector", lambda h=h: nc.vector.tensor_tensor(
                out=scm[:], in0=ps_sc.ap()[:, 0:P], in1=dm_sb[:, hs(h)], op=ALU.mult))
            b.op("vector", lambda ci=ci, h=h: nc.vector.tensor_tensor(
                out=qd[:], in0=qch[ci % 2][:, hs(h)], in1=qdb_sb[:, hs(h)], op=ALU.mult))
            b.pub("vector", s_ktd)
            b.wait("vector", s_m5, n + 1)
            b.op("vector", lambda h=h: nc.vector.scalar_tensor_tensor(
                out=kv[:, hs(h)], in0=kv[:, hs(h)], scalar=rc_sb[:, h:h + 1],
                in1=ps_kv.ap()[:, 0:P], op0=ALU.mult, op1=ALU.add))
            b.pub("vector", s_kv)
        b.wait("vector", s_rb, ci + 1)
        b.wait("vector", s_aq, (ci + 1) * H)
        if ci >= 4:
            b.wait("vector", s_wo, (ci // 4) * KT * H)
        for h in range(H):
            b.op("vector", lambda h=h, cg=cg: nc.vector.tensor_tensor(
                out=grbuf.ap()[:, h, cg * P:(cg + 1) * P], in0=afm[:, hs(h)], in1=rb_c[:], op=ALU.mult))
        b.pub("vector", s_gr, inc=H)
        if cg == 3:
            for j in range(KT):
                b.wait("vector", s_ye, g * KT + j + 1)
                if g * KT + j >= 2:
                    b.wait("vector", s_plm, g * KT + j - 1)
                b.op("vector", lambda j=j: nc.vector.tensor_copy(out=yb[j % 2][:], in_=ywo.ap()[:, j, :]))
                b.pub("vector", s_cb)
            b.wait("vector", s_plm, (g + 1) * KT)
            b.op("vector", lambda: nc.vector.tensor_scalar(
                out=rowt[0:1, 0:512], in0=ps_sc.ap()[0:1, 0:512], scalar1=1.0 / D, scalar2=None, op0=ALU.mult))
            b.op("vector", lambda: nc.vector.tensor_scalar(
                out=rowt[0:1, 512:1024], in0=ps_sc.ap()[64:65, 0:512], scalar1=1.0 / D, scalar2=None, op0=ALU.mult))
            b.op("vector", lambda: nc.vector.tensor_tensor(
                out=row2[0:1, 512:1024], in0=rowt[0:1, 0:512], in1=rowt[0:1, 0:512], op=ALU.mult))
            b.op("vector", lambda: nc.vector.tensor_tensor(
                out=row2[0:1, 512:1024], in0=rowt[0:1, 512:1024], in1=row2[0:1, 512:1024], op=ALU.subtract))
            b.pub("vector", s_r2)
            b.wait("vector", s_r2, 3 * g + 2)
            b.op("vector", lambda: nc.vector.tensor_tensor(
                out=row2[0:1, 512:1024], in0=rowt[0:1, 0:512], in1=row2[0:1, 0:512], op=ALU.mult))
            b.pub("vector", s_r2)
            for j in range(0, KT, 2):
                nb = g * 8 + j // 2
                sl_xoV.wait("vector", nb % 2, 16 * (nb // 2 + 1))
                b.wait("vector", s_rbe, g + 1)
                if nb >= 2:
                    sl_xwV.wait("vector", nb % 2, 16 * (nb // 2))
                b.op("vector", lambda j=j: nc.vector.tensor_tensor(
                    out=ywo.ap()[:, j, :], in0=ywo.ap()[:, j, :], in1=rb2[:], op=ALU.mult))
                b.op("vector", lambda j=j: nc.vector.tensor_tensor(
                    out=ywo.ap()[:, j, :], in0=ywo.ap()[:, j, :], in1=mrb2[:], op=ALU.subtract))
                b.op("vector", lambda j=j, nb=nb: nc.vector.tensor_tensor(
                    out=xoldV[nb % 2][:], in0=ywo.ap()[:, j, :], in1=xoldV[nb % 2][:], op=ALU.add))
                b.pub("vector", s_av)

    if DBG == "scan":
        dbg_attn = b.dout("dbg_attn", [NC0, P, H * P], F32)
        s_dbga = b.sem("a3dbga")
        dbg_ywo = b.dout("dbg_ywo", [NBT, P, KT * 512], F32)
        s_dbgy = b.sem("a3dbgy")
    b.wait("scalar", s_ldc, 16 * 9)
    for ci in range(NC0):
        g, cg = ci // 4, ci % 4
        for h in range(H):
            n = ci * H + h
            b.wait("scalar", s_t2, n + 1)
            b.wait("scalar", s_m5, n)
            b.op("scalar", lambda: nc.scalar.copy(out=vtok[:], in_=ps_tv.ap()[:, 0:P]))
            b.pub("scalar", s_vt)
            b.wait("scalar", s_o, n + 1)
            if DBG == "scan" and ci >= 1 and h == 0:
                b.wait("scalar", s_dbga, 17 * ci)
            if ci >= 1:
                b.wait("scalar", s_gr, n - H + 1)
            b.wait("scalar", s_m6, n)
            b.op("scalar", lambda h=h: nc.scalar.copy(out=afm[:, hs(h)], in_=ps_o.ap()[:, 0:P]))
            b.op("scalar", lambda: nc.scalar.activation(out=asq[:], in_=ps_o.ap()[:, 0:P], func=AF.Square))
            b.pub("scalar", s_aq)
        if DBG == "scan":
            b.pub("scalar", s_dbga)
            b.op("scalar", lambda ci=ci: nc.scalar.dma_start(
                out=dbg_attn[ci], in_=afm[:]), s_dbga, 16)
        b.wait("scalar", s_m6, (ci + 1) * H)
        b.wait("scalar", s_rp, ci)
        b.op("scalar", lambda: nc.scalar.activation(
            out=rowb[0:1, :], in_=ps_rm.ap()[0:1, 0:P], func=AF.Ln, scale=1.0 / D, bias=cp_sb[0:1, 0:1]))
        b.op("scalar", lambda: nc.scalar.activation(
            out=rowb[0:1, :], in_=rowb[0:1, :], func=AF.Exp, scale=-0.5))
        b.pub("scalar", s_rr)
        b.wait("scalar", s_rp, ci + 1)
        b.wait("scalar", s_gr, ci * H)
        b.op("scalar", lambda: nc.scalar.copy(out=rb_c[:], in_=ps_rm.ap()[:, P:2 * P]))
        b.pub("scalar", s_rb)
        if cg == 3:
            for j in range(KT):
                b.wait("scalar", s_wo, g * KT * H + (j + 1) * H)
                if g >= 1:
                    b.wait("scalar", s_av, g * 8)
                    b.wait("scalar", s_ap, g * 8)
                b.op("scalar", lambda j=j: nc.scalar.copy(
                    out=ywo.ap()[:, j, :], in_=ps_wo[j % 2].ap()[:]))
                b.pub("scalar", s_ye)
            for j in range(KT):
                if g * KT + j >= 2:
                    b.wait("scalar", s_plm, g * KT + j - 1)
                b.op("scalar", lambda j=j: nc.scalar.activation(
                    out=ysq[j % 2][:], in_=ywo.ap()[:, j, :], func=AF.Square))
                b.pub("scalar", s_sq2)
            if DBG == "scan":
                b.pub("scalar", s_dbgy)
                b.op("scalar", lambda g=g: nc.scalar.dma_start(
                    out=dbg_ywo[g], in_=ywo.ap().rearrange("p j m -> p (j m)")), s_dbgy, 16)
            b.wait("scalar", s_r2, 3 * g + 1)
            b.op("scalar", lambda: nc.scalar.activation(
                out=row2[0:1, 0:512], in_=row2[0:1, 512:1024], func=AF.Ln, bias=cp_sb[0:1, 0:1]))
            b.op("scalar", lambda: nc.scalar.activation(
                out=row2[0:1, 0:512], in_=row2[0:1, 0:512], func=AF.Exp, scale=-0.5))
            b.pub("scalar", s_r2)
            b.wait("scalar", s_rp2, 2 * (g + 1))
            b.op("scalar", lambda: nc.scalar.copy(out=rb2[:], in_=ps_o.ap()[:]))
            b.op("scalar", lambda: nc.scalar.copy(out=mrb2[:], in_=ps_kv.ap()[:]))
            b.pub("scalar", s_rbe)

    if DBG == "scan":
        b.wait("scalar", s_dbga, 17 * NC0)
        b.wait("scalar", s_dbgy, 17 * NBT)

    for ci in range(NC0):
        g, cg = ci // 4, ci % 4
        if cg == 3:
            for j in range(1, KT, 2):
                nb = g * 8 + (j - 1) // 2
                sl_xoP.wait("gpsimd", nb % 2, 16 * (nb // 2 + 1))
                b.wait("gpsimd", s_rbe, g + 1)
                if nb >= 2:
                    sl_xwP.wait("gpsimd", nb % 2, 16 * (nb // 2))
                b.op("gpsimd", lambda j=j: nc.gpsimd.tensor_tensor(
                    out=ywo.ap()[:, j, :], in0=ywo.ap()[:, j, :], in1=rb2[:], op=ALU.mult))
                b.op("gpsimd", lambda j=j: nc.gpsimd.tensor_tensor(
                    out=ywo.ap()[:, j, :], in0=ywo.ap()[:, j, :], in1=mrb2[:], op=ALU.subtract))
                b.op("gpsimd", lambda j=j, nb=nb: nc.gpsimd.tensor_tensor(
                    out=xoldP[nb % 2][:], in0=ywo.ap()[:, j, :], in1=xoldP[nb % 2][:], op=ALU.add))
                b.pub("gpsimd", s_ap)

    for i in range(2):
        sl_xwV.wait("sync", i, 16 * (NBT * KT // 4))
        sl_xwP.wait("sync", i, 16 * (NBT * KT // 4))
    b.barrier()
    r.close()



DBG = None


def ph_l0a(b, c, lay, xin, xout):
    nc = b.nc
    pr = b.region()
    xn = emit_preln(b, "a1", xin, c, pr)
    if DBG == "preln":
        dbg = b.dout("dbg_xn", [KT, P, T], BF16)
        sd = b.sem("dbgs")
        for k in range(KT):
            b.op("sync", lambda k=k: nc.sync.dma_start(
                out=dbg[k], in_=xn[:, k * T:(k + 1) * T]), sd, 16)
        b.wait("sync", sd, 16 * KT)
        pr.close()
        return
    emit_l0a_qkv(b, c, lay, xn, pr)
    pr.close()
    if DBG == "qkv":
        return
    emit_l0a_scan(b, c, lay, xin, xout)


PHASES = {"l0a": ph_l0a}
GROUPS = [["l0a", "l0f", "l1a", "l1f"]]
STATE_NAMES = {"l0a": "x1", "l0f": "x2", "l1a": "x3", "l1f": "x4"}


def _register_phases():
    PHASES["l0f"] = ph_l0f
    for nm, fn in list(globals().items()):
        if nm.startswith("ph_") and callable(fn):
            PHASES[nm[3:]] = fn


def build_group(group, c, lay, xin_arrs):
    _register_phases()
    """group: list of phase names. xin_arrs: per-core list of [KT,P,T] f32."""
    b = Build()
    xcur = b.din("xin_" + group[0], xin_arrs)
    for i, ph in enumerate(group):
        last = i == len(group) - 1
        nm = STATE_NAMES[ph]
        if last:
            xnext = b.dout(nm, [KT, P, T], F32)
        else:
            xnext = b.dscratch(nm, [KT, P, T], F32)
        PHASES[ph](b, c, lay, xcur, xnext)
        if not last:
            b.phase_sems_reset()
        xcur = xnext
    b.finalize()
    return b, STATE_NAMES[group[-1]]


def kernel(**inputs):
    c = mk_consts()
    lay = mk_layouts(inputs)
    state = lay["x0"]
    for group in GROUPS:
        b, outname = build_group(group, c, lay, state)
        in_maps = []
        for core in range(N_CORES):
            m = {}
            for name, arr in b.host.items():
                m[name] = arr[core] if isinstance(arr, list) else arr
            in_maps.append(m)
        res = run_bass_kernel_spmd(b.nc, in_maps, core_ids=list(range(N_CORES)))
        state = [np.ascontiguousarray(res.results[core][outname]) for core in range(N_CORES)]
    out = np.empty((N_CORES, T, D), dtype=np.float32)
    for core in range(N_CORES):
        out[core] = state[core].reshape(D, T).T
    return out



def emit_postln_group(b, tag, E, g, gt, ysrc, xin, xout, bufs, nofj=KT, GW=512, yready=None):
    nc = b.nc
    ones_sb, oner_sb, cp_sb = bufs["ones"], bufs["oner"], bufs["cp"]
    yb, ysq, row2, rowt = bufs["yb"], bufs["ysq"], bufs["row2"], bufs["rowt"]
    rb2, mrb2, xoldV, xoldP = bufs["rb2"], bufs["mrb2"], bufs["xoldV"], bufs["xoldP"]
    pl_ps, rp_a, rp_b = bufs["pl_ps"], bufs["rp_a"], bufs["rp_b"]
    s_cb, s_sq2, s_plm = E["cb"], E["sq2"], E["plm"]
    s_r2, s_rp2, s_rbe = E["r2"], E["rp2"], E["rbe"]
    s_av, s_ap = E["av"], E["ap"]
    sl_xoV, sl_xoP, sl_xwV, sl_xwP = E["sl_xoV"], E["sl_xoP"], E["sl_xwV"], E["sl_xwP"]
    KTJ = nofj

    def t_part():
        for j in range(KTJ):
            b.wait("tensor", s_cb, g * KTJ + j + 1)
            b.wait("tensor", s_sq2, g * KTJ + j + 1)
            if j == 0 and g >= 1:
                b.wait("tensor", s_rp2, 2 * g)
                b.wait("tensor", s_r2, 3 * g)
            b.op("tensor", lambda j=j: nc.tensor.matmul(
                out=pl_ps.ap()[0:1, 0:GW], lhsT=ones_sb[:], rhs=yb[j % 2][:, 0:GW],
                start=(j == 0), stop=(j == KTJ - 1)))
            b.op("tensor", lambda j=j: nc.tensor.matmul(
                out=pl_ps.ap()[64:65, 0:GW], lhsT=ones_sb[:], rhs=ysq[j % 2][:, 0:GW],
                start=(j == 0), stop=(j == KTJ - 1)), s_plm)
        b.wait("tensor", s_r2, 3 * (g + 1))
        if g >= 1:
            b.wait("tensor", s_rbe, g)
        b.op("tensor", lambda: nc.tensor.matmul(
            out=rp_a.ap()[:, 0:GW], lhsT=oner_sb[:], rhs=row2[0:1, 0:GW], start=True, stop=True))
        b.op("tensor", lambda: nc.tensor.matmul(
            out=rp_b.ap()[:, 0:GW], lhsT=oner_sb[:], rhs=row2[0:1, GW:2 * GW], start=True, stop=True), s_rp2, inc=2)

    def v_part():
        for j in range(KTJ):
            if g * KTJ + j >= 2:
                b.wait("vector", s_plm, g * KTJ + j - 1)
            if yready is not None:
                yready(j)
            b.op("vector", lambda j=j: nc.vector.tensor_copy(out=yb[j % 2][:, 0:GW], in_=ysrc(j)))
            b.pub("vector", s_cb)
        b.wait("vector", s_plm, (g + 1) * KTJ)
        b.op("vector", lambda: nc.vector.tensor_scalar(
            out=rowt[0:1, 0:GW], in0=pl_ps.ap()[0:1, 0:GW], scalar1=1.0 / D, scalar2=None, op0=ALU.mult))
        b.op("vector", lambda: nc.vector.tensor_scalar(
            out=rowt[0:1, GW:2 * GW], in0=pl_ps.ap()[64:65, 0:GW], scalar1=1.0 / D, scalar2=None, op0=ALU.mult))
        b.op("vector", lambda: nc.vector.tensor_tensor(
            out=row2[0:1, GW:2 * GW], in0=rowt[0:1, 0:GW], in1=rowt[0:1, 0:GW], op=ALU.mult))
        b.op("vector", lambda: nc.vector.tensor_tensor(
            out=row2[0:1, GW:2 * GW], in0=rowt[0:1, GW:2 * GW], in1=row2[0:1, GW:2 * GW], op=ALU.subtract))
        b.pub("vector", s_r2)
        b.wait("vector", s_r2, 3 * g + 2)
        b.op("vector", lambda: nc.vector.tensor_tensor(
            out=row2[0:1, GW:2 * GW], in0=rowt[0:1, 0:GW], in1=row2[0:1, 0:GW], op=ALU.mult))
        b.pub("vector", s_r2)
        for j in range(0, KTJ, 2):
            nb = g * (KTJ // 2) + j // 2
            sl_xoV.wait("vector", nb % 2, 16 * (nb // 2 + 1))
            b.wait("vector", s_rbe, g + 1)
            if nb >= 2:
                sl_xwV.wait("vector", nb % 2, 16 * (nb // 2))
            b.op("vector", lambda j=j: nc.vector.tensor_tensor(
                out=ysrc(j), in0=ysrc(j), in1=rb2[:, 0:GW], op=ALU.mult))
            b.op("vector", lambda j=j: nc.vector.tensor_tensor(
                out=ysrc(j), in0=ysrc(j), in1=mrb2[:, 0:GW], op=ALU.subtract))
            b.op("vector", lambda j=j, nb=nb: nc.vector.tensor_tensor(
                out=xoldV[nb % 2][:, 0:GW], in0=ysrc(j), in1=xoldV[nb % 2][:, 0:GW], op=ALU.add))
            b.pub("vector", s_av)

    def s_part():
        for j in range(KTJ):
            if g * KTJ + j >= 2:
                b.wait("scalar", s_plm, g * KTJ + j - 1)
            b.wait("scalar", s_cb, g * KTJ + j + 1)
            b.op("scalar", lambda j=j: nc.scalar.activation(
                out=ysq[j % 2][:, 0:GW], in_=ysrc(j), func=AF.Square))
            b.pub("scalar", s_sq2)
        b.wait("scalar", s_r2, 3 * g + 1)
        b.op("scalar", lambda: nc.scalar.activation(
            out=row2[0:1, 0:GW], in_=row2[0:1, GW:2 * GW], func=AF.Ln, bias=cp_sb[0:1, 0:1]))
        b.op("scalar", lambda: nc.scalar.activation(
            out=row2[0:1, 0:GW], in_=row2[0:1, 0:GW], func=AF.Exp, scale=-0.5))
        b.pub("scalar", s_r2)
        b.wait("scalar", s_rp2, 2 * (g + 1))
        if g >= 1:
            b.wait("scalar", s_av, g * (KTJ // 2))
            b.wait("scalar", s_ap, g * (KTJ // 2))
        b.op("scalar", lambda: nc.scalar.copy(out=rb2[:, 0:GW], in_=rp_a.ap()[:, 0:GW]))
        b.op("scalar", lambda: nc.scalar.copy(out=mrb2[:, 0:GW], in_=rp_b.ap()[:, 0:GW]))
        b.pub("scalar", s_rbe)

    def g_part():
        for j in range(1, KTJ, 2):
            nb = g * (KTJ // 2) + (j - 1) // 2
            sl_xoP.wait("gpsimd", nb % 2, 16 * (nb // 2 + 1))
            b.wait("gpsimd", s_rbe, g + 1)
            if nb >= 2:
                sl_xwP.wait("gpsimd", nb % 2, 16 * (nb // 2))
            b.op("gpsimd", lambda j=j: nc.gpsimd.tensor_tensor(
                out=ysrc(j), in0=ysrc(j), in1=rb2[:, 0:GW], op=ALU.mult))
            b.op("gpsimd", lambda j=j: nc.gpsimd.tensor_tensor(
                out=ysrc(j), in0=ysrc(j), in1=mrb2[:, 0:GW], op=ALU.subtract))
            b.op("gpsimd", lambda j=j, nb=nb: nc.gpsimd.tensor_tensor(
                out=xoldP[nb % 2][:, 0:GW], in0=ysrc(j), in1=xoldP[nb % 2][:, 0:GW], op=ALU.add))
            b.pub("gpsimd", s_ap)

    def sp_part():
        def _ld(j):
            if j % 2 == 0:
                nb = g * (KTJ // 2) + j // 2
                if nb >= 2:
                    sl_xwV.wait("sync", nb % 2, 16 * (nb // 2))
                sl_xoV.load("sync", nb % 2, lambda j=j, nb=nb: nc.sync.dma_start(
                    out=xoldV[nb % 2][:, 0:GW], in_=xin[j][:, gt * GW:(gt + 1) * GW]))
            else:
                nb = g * (KTJ // 2) + (j - 1) // 2
                if nb >= 2:
                    sl_xwP.wait("sync", nb % 2, 16 * (nb // 2))
                sl_xoP.load("sync", nb % 2, lambda j=j, nb=nb: nc.sync.dma_start(
                    out=xoldP[nb % 2][:, 0:GW], in_=xin[j][:, gt * GW:(gt + 1) * GW]))

        def _st(j):
            if j % 2 == 0:
                nb = g * (KTJ // 2) + j // 2
                b.wait("sync", s_av, nb + 1)
                sl_xwV.load("sync", nb % 2, lambda j=j, nb=nb: nc.sync.dma_start(
                    out=xout[j][:, gt * GW:(gt + 1) * GW], in_=xoldV[nb % 2][:, 0:GW]))
            else:
                nb = g * (KTJ // 2) + (j - 1) // 2
                b.wait("sync", s_ap, nb + 1)
                sl_xwP.load("sync", nb % 2, lambda j=j, nb=nb: nc.sync.dma_start(
                    out=xout[j][:, gt * GW:(gt + 1) * GW], in_=xoldP[nb % 2][:, 0:GW]))

        for j in range(KTJ):
            if j >= 4:
                _st(j - 4)
            _ld(j)
        for j in range(KTJ - 4, KTJ):
            _st(j)

    return t_part, v_part, s_part, g_part, sp_part


def mk_postln_sems(b, tag):
    return {
        "cb": b.sem(tag + "cb"), "sq2": b.sem(tag + "sq2"), "plm": b.sem(tag + "plm"),
        "r2": b.sem(tag + "r2"), "rp2": b.sem(tag + "rp2"), "rbe": b.sem(tag + "rbe"),
        "av": b.sem(tag + "av"), "ap": b.sem(tag + "ap"),
        "sl_xoV": Slots(b, tag + "xoV", 2), "sl_xoP": Slots(b, tag + "xoP", 2),
        "sl_xwV": Slots(b, tag + "xwV", 2), "sl_xwP": Slots(b, tag + "xwP", 2),
    }


def mk_postln_bufs(b, reg, tag, c, ps_reg=None, ps_override=None):
    pr = ps_reg if ps_reg is not None else reg
    if ps_override is None:
        ps_override = {}
    return {
        "ones": b.sb(reg, tag + "ones", [P, 1], BF16),
        "oner": b.sb(reg, tag + "oner", [1, P], F32),
        "cp": b.sb(reg, tag + "cp", [P, 8], F32),
        "yb": [b.sb(reg, tag + f"yb{i}", [P, 512], BF16) for i in range(2)],
        "ysq": [b.sb(reg, tag + f"ysq{i}", [P, 512], BF16) for i in range(2)],
        "row2": b.sb(reg, tag + "row2", [1, 1024], F32),
        "rowt": b.sb(reg, tag + "rowt", [1, 1024], F32),
        "rb2": b.sb(reg, tag + "rb2", [P, 512], F32),
        "mrb2": b.sb(reg, tag + "mrb2", [P, 512], F32),
        "xoldV": [b.sb(reg, tag + f"xoV{i}", [P, 512], F32) for i in range(2)],
        "xoldP": [b.sb(reg, tag + f"xoP{i}", [P, 512], F32) for i in range(2)],
        "pl_ps": ps_override.get("pl") or b.ps(pr, tag + "plps", [P, 512]),
        "rp_a": ps_override.get("rpa") or b.ps(pr, tag + "rpa", [P, 512]),
        "rp_b": ps_override.get("rpb") or b.ps(pr, tag + "rpb", [P, 512]),
    }


def emit_postln_consts(b, bufs, c, sem):
    nc = b.nc
    b.op("sync", lambda: nc.sync.dma_start(out=bufs["ones"][:], in_=b.din("ones_bf", c["ones_bf"])), sem, 16)
    b.op("sync", lambda: nc.sync.dma_start(out=bufs["oner"][:], in_=b.din("ones_row_f32", c["ones_row_f32"])), sem, 16)
    b.op("sync", lambda: nc.sync.dma_start(out=bufs["cp"][:], in_=b.din("cpack", c["cpack"])), sem, 16)



def emit_gather(b, tag, eng, idxT_d, lut_sb, s_lut, out_sc, nb, W):
    nc = b.nc
    r = b.region()
    idx = [b.sb(r, f"{tag}i0", [P, W], I32)]
    idxf = b.sb(r, f"{tag}f", [P, W], BF16)
    msk = b.sb(r, f"{tag}m", [P, W], BF16)
    acc = [b.sb(r, f"{tag}a{i}", [P, W], BF16) for i in range(2)]
    sl_i = Slots(b, tag + "i", 1)
    s_d = b.sem(tag + "d")
    sl_o = Slots(b, tag + "o", 2)
    e = getattr(nc, eng)

    for k in range(nb):
        if k >= 1:
            b.wait("sync", s_d, k)
        sl_i.load("sync", 0, lambda k=k: nc.sync.dma_start(out=idx[0][:], in_=idxT_d[k]))

    for k in range(nb):
        sl_i.wait(eng, 0, 16 * (k + 1))
        if k == 0:
            b.wait(eng, s_lut, 16)
        if k >= 2:
            sl_o.wait(eng, k % 2, 16 * (k // 2))
        b.op(eng, lambda k=k: e.tensor_copy(out=idxf[:], in_=idx[0][:]))
        for j in range(16):
            if j == 0:
                b.op(eng, lambda k=k, j=j: e.tensor_scalar(
                    out=acc[k % 2][:], in0=idxf[:], scalar1=float(j), scalar2=lut_sb[:, j:j + 1],
                    op0=ALU.is_equal, op1=ALU.mult))
            else:
                b.op(eng, lambda k=k, j=j: e.tensor_scalar(
                    out=msk[:], in0=idxf[:], scalar1=float(j), scalar2=lut_sb[:, j:j + 1],
                    op0=ALU.is_equal, op1=ALU.mult))
                b.op(eng, lambda k=k: e.tensor_tensor(
                    out=acc[k % 2][:], in0=acc[k % 2][:], in1=msk[:], op=ALU.add))
        b.pub(eng, s_d)

    for k in range(nb):
        b.wait("scalar", s_d, k + 1)
        sl_o.load("scalar", k % 2, lambda k=k: nc.scalar.dma_start(out=out_sc[k], in_=acc[k % 2][:]))
    for i in range(2):
        sl_o.wait("scalar", i, 16 * ((nb + 1 - i) // 2))
    return r


def emit_l0f_gateup(b, c, lay, xn_res, pr):
    nc = b.nc
    r = b.region()
    wgt = [b.sb(r, f"f2wg{i}", [P, KT, P], BF16) for i in range(2)]
    wut = [b.sb(r, f"f2wu{i}", [P, KT, P], BF16) for i in range(2)]
    hrow = [b.sb(r, f"f2h{i}", [P, T], BF16) for i in range(2)]
    ps = [b.ps(r, f"f2ps{i}", [P, 512]) for i in range(6)]
    wg_sc, wu_sc = b._dram["wg_sc"], b._dram["wu_sc"]
    h_sc = b.dscratch("h_sc", [OT0, P, T], BF16)
    sl_w = Slots(b, "f2w", 2)
    s_pm = b.sem("f2pm")
    s_evg = b.sem("f2evg")
    s_evu = b.sem("f2evu")
    sl_h = Slots(b, "f2ho", 2, persistent=True)

    def bank_owner(g_all):
        o_, u_ = divmod(g_all, 16)
        return ("g", o_ * 8 + u_ + 1) if u_ < 8 else ("u", o_ * 8 + (u_ - 8) + 1)

    for o in range(OT0):
        if o >= 2:
            b.wait("sync", s_pm, (o - 1) * 16)
        sl_w.load("sync", o % 2, lambda o=o: nc.sync.dma_start(
            out=wgt[o % 2].ap(),
            in_=wg_sc[:, :, o * P:(o + 1) * P].rearrange("k p m -> p k m")))
        sl_w.uses[o % 2] -= 1
        sl_w.load("sync", o % 2, lambda o=o: nc.sync.dma_start(
            out=wut[o % 2].ap(),
            in_=wu_sc[:, :, o * P:(o + 1) * P].rearrange("k p m -> p k m")))

    for o in range(OT0):
        sl_w.wait("tensor", o % 2, 32 * (o // 2 + 1))
        for u in range(16):
            g_all = o * 16 + u
            wt = wgt if u < 8 else wut
            tb = u % 8
            bank = g_all % 6
            if g_all >= 6:
                own, cnt = bank_owner(g_all - 6)
                b.wait("tensor", s_evg if own == "g" else s_evu, cnt)
            for k in range(KT):
                st, sp = (k == 0), (k == KT - 1)
                b.op("tensor", lambda o=o, u=u, tb=tb, k=k, bank=bank, wt=wt, st=st, sp=sp: nc.tensor.matmul(
                    out=ps[bank].ap()[:], lhsT=wt[o % 2][:, k, :],
                    rhs=xn_res[:, k * T + tb * 512:k * T + (tb + 1) * 512],
                    start=st, stop=sp), s_pm if sp else None)

    for o in range(OT0):
        for tb in range(NBT):
            g_all = o * 16 + tb
            bank = g_all % 6
            b.wait("scalar", s_pm, g_all + 1)
            if tb == 0 and o >= 2:
                sl_h.wait("scalar", o % 2, 16 * (o // 2))
            b.op("scalar", lambda o=o, tb=tb, bank=bank: nc.scalar.activation(
                out=hrow[o % 2][:, tb * 512:(tb + 1) * 512], in_=ps[bank].ap()[:], func=AF.Silu))
            b.pub("scalar", s_evg)

    for o in range(OT0):
        for tb in range(NBT):
            g_all = o * 16 + 8 + tb
            bank = g_all % 6
            b.wait("vector", s_pm, g_all + 1)
            b.wait("vector", s_evg, o * 8 + tb + 1)
            b.op("vector", lambda o=o, tb=tb, bank=bank: nc.vector.tensor_tensor(
                out=hrow[o % 2][:, tb * 512:(tb + 1) * 512],
                in0=hrow[o % 2][:, tb * 512:(tb + 1) * 512],
                in1=ps[bank].ap()[:], op=ALU.mult))
            b.pub("vector", s_evu)

    for o in range(OT0):
        b.wait("gpsimd", s_evu, (o + 1) * 8)
        sl_h.load("gpsimd", o % 2, lambda o=o: nc.gpsimd.dma_start(out=h_sc[o], in_=hrow[o % 2][:]))
    for i in range(2):
        sl_h.wait("gpsimd", i, 16 * ((OT0 + 1 - i) // 2))
    b.barrier()
    r.close()


def emit_l0f_down(b, c, lay, xin, xout):
    nc = b.nc
    r = b.region()
    KOC = [8, 8, 8, 8, 8, 3]
    NTQ = 4
    ydown = b.sb(r, "f3y", [P, KT, 1024], F32)
    wdt = [b.sb(r, f"f3w{i}", [P, 8, D], BF16) for i in range(2)]
    hq = [b.sb(r, f"f3h{i}", [P, 8, 1024], BF16) for i in range(2)]
    ps = [b.ps(r, f"f3ps{i}", [P, 512]) for i in range(4)]
    pbufs = mk_postln_bufs(b, r, "f3p", c)
    E = mk_postln_sems(b, "f3p")
    s_cst = b.sem("f3cst")
    emit_postln_consts(b, pbufs, c, s_cst)
    wd_sc = b._dram["wd_sc"]
    h_sc = b._dram["h_sc"]
    sl_wd = Slots(b, "f3wd", 2)
    sl_hq = Slots(b, "f3hq", 2)
    s_mm = b.sem("f3mm")
    s_ev = b.sem("f3ev")

    tgt_wd = {}
    tgt_hq = {}
    wd_cnt = [0, 0]
    hq_cnt = [0, 0]

    def unit(tq, ci, j, tb):
        return ((tq * 6 + ci) * KT + j) * 2 + tb

    def pg(tq, gg):
        return tq * 2 + gg

    parts = {}
    for tq in range(NTQ):
        for ci, kosz in enumerate(KOC):
            ko0 = sum(KOC[:ci])
            idx = tq * 6 + ci
            slot = idx % 2
            if idx >= 2:
                b.wait("sync", s_mm, (idx - 1) * 32)
            for kk in range(kosz):
                sl_wd.load("sync", slot, lambda ko0=ko0, kk=kk, slot=slot: nc.sync.dma_start(
                    out=wdt[slot].ap()[:, kk, :], in_=wd_sc[ko0 + kk]))
                sl_hq.load("sync", slot, lambda ko0=ko0, kk=kk, slot=slot, tq=tq: nc.sync.dma_start(
                    out=hq[slot].ap()[:, kk, :], in_=h_sc[ko0 + kk][:, tq * 1024:(tq + 1) * 1024]))
            wd_cnt[slot] += 16 * kosz
            hq_cnt[slot] += 16 * kosz
            tgt_wd[idx] = wd_cnt[slot]
            tgt_hq[idx] = hq_cnt[slot]
        for gg in range(2):
            g = pg(tq, gg)
            parts[g] = emit_postln_group(
                b, "f3p", E, g, g, (lambda j, tq=tq, gg=gg: ydown.ap()[:, j, gg * 512:(gg + 1) * 512]),
                xin, xout, pbufs)
        for gg in range(2):
            parts[pg(tq, gg)][4]()

    b.wait("tensor", s_cst, 48)
    for tq in range(NTQ):
        for ci, kosz in enumerate(KOC):
            idx = tq * 6 + ci
            slot = idx % 2
            sl_wd.wait("tensor", slot, tgt_wd[idx])
            sl_hq.wait("tensor", slot, tgt_hq[idx])
            for j in range(KT):
                for kk in range(kosz):
                    for tb in range(2):
                        bank = (j % 2) * 2 + tb
                        u = unit(tq, ci, j, tb)
                        if kk == 0 and u >= 4:
                            b.wait("tensor", s_ev, u - 3)
                        st, sp = (kk == 0), (kk == kosz - 1)
                        b.op("tensor", lambda slot=slot, kk=kk, j=j, tb=tb, bank=bank, st=st, sp=sp: nc.tensor.matmul(
                            out=ps[bank].ap()[:], lhsT=wdt[slot][:, kk, j * P:(j + 1) * P],
                            rhs=hq[slot].ap()[:, kk, tb * 512:(tb + 1) * 512],
                            start=st, stop=sp), s_mm if sp else None)
        for gg in range(2):
            parts[pg(tq, gg)][0]()

    for tq in range(NTQ):
        if tq >= 1:
            b.wait("vector", E["ap"], tq * 16)
        for ci, kosz in enumerate(KOC):
            for j in range(KT):
                for tb in range(2):
                    u = unit(tq, ci, j, tb)
                    b.wait("vector", s_mm, u + 1)
                    if ci == 0:
                        b.op("vector", lambda j=j, tb=tb, bank=(j % 2) * 2 + tb: nc.vector.tensor_copy(
                            out=ydown.ap()[:, j, tb * 512:(tb + 1) * 512], in_=ps[bank].ap()[:]))
                    else:
                        b.op("vector", lambda j=j, tb=tb, bank=(j % 2) * 2 + tb: nc.vector.tensor_tensor(
                            out=ydown.ap()[:, j, tb * 512:(tb + 1) * 512],
                            in0=ydown.ap()[:, j, tb * 512:(tb + 1) * 512],
                            in1=ps[bank].ap()[:], op=ALU.add))
                    b.pub("vector", s_ev)
        for gg in range(2):
            parts[pg(tq, gg)][1]()

    for tq in range(NTQ):
        for gg in range(2):
            parts[pg(tq, gg)][2]()
    for tq in range(NTQ):
        for gg in range(2):
            parts[pg(tq, gg)][3]()

    for i in range(2):
        E["sl_xwV"].wait("sync", i, 16 * (NTQ * 2 * (KT // 2) // 2))
        E["sl_xwP"].wait("sync", i, 16 * (NTQ * 2 * (KT // 2) // 2))
    b.barrier()
    r.close()


def ph_l0f(b, c, lay, xin, xout):
    nc = b.nc
    r0 = b.region()
    lut_sb = b.sb(r0, "g0lut", [P, 16], F32)
    s_lut = b.sem("g0lut")
    b.op("sync", lambda: nc.sync.dma_start(out=lut_sb[:], in_=b.din("l0_lut", lay["l0_lut"])), s_lut, 16)
    wg_sc = b.dscratch("wg_sc", [KT, P, I0], BF16)
    wu_sc = b.dscratch("wu_sc", [KT, P, I0], BF16)
    wd_sc = b.dscratch("wd_sc", [OT0, P, D], BF16)
    rg = emit_gather(b, "g0g", "vector", b.din("l0_giT", lay["l0_giT"]), lut_sb, s_lut, wg_sc, KT, I0)
    ru = emit_gather(b, "g0u", "vector", b.din("l0_uiT", lay["l0_uiT"]), lut_sb, s_lut, wu_sc, KT, I0)
    rd = emit_gather(b, "g0d", "vector", b.din("l0_diT", lay["l0_diT"]), lut_sb, s_lut, wd_sc, OT0, D)
    b.barrier()
    rd.close(); ru.close(); rg.close(); r0.close()

    pr = b.region()
    xn = emit_preln(b, "f1", xin, c, pr)
    emit_l0f_gateup(b, c, lay, xn, pr)
    pr.close()
    emit_l0f_down(b, c, lay, xin, xout)



def emit_l1a_qkvg(b, c, lay, xn_res, pr):
    nc = b.nc
    r = b.region()
    wfp = [b.sb(r, f"b2wf{i}", [P, KT * P], F32) for i in range(2)]
    wb = [b.sb(r, f"b2wb{i}", [P, KT * P], BF16) for i in range(2)]
    orow = b.sb(r, "b2or", [P, T], BF16)
    ps = [b.ps(r, f"b2ps{i}", [P, 512]) for i in range(6)]
    q_sc = b.dscratch("q1_sc", [NB1, P, H, BLK1], BF16)
    k_sc = b.dscratch("k1_sc", [NB1, P, H, BLK1], BF16)
    v_sc = b.dscratch("v1_sc", [NB1, P, H, BLK1], BF16)
    g_sc = b.dscratch("g1_sc", [NB1, P, H, BLK1], BF16)
    wq_d = b.din("l1_wqkvT", lay["l1_wqkvT"])
    wg_d = b.din("l1_wgateT", lay["l1_wgateT"])
    sl_w = Slots(b, "b2w", 2)
    s_wc = b.sem("b2wc")
    s_pm = b.sem("b2pm")
    s_ev = b.sem("b2ev")
    sl_o = Slots(b, "b2o", 8, persistent=True)

    NJ = 48 + 16
    for jj in range(NJ):
        if jj >= 2:
            b.wait("sync", s_wc, jj - 1)
        src = (lambda jj=jj: wq_d[jj] if jj < 48 else wg_d[jj - 48])
        sl_w.load("sync", jj % 2, lambda jj=jj, src=src: nc.sync.dma_start(
            out=wfp[jj % 2][:], in_=src()))

    for jj in range(NJ):
        sl_w.wait("vector", jj % 2, 16 * (jj // 2 + 1))
        if jj >= 2:
            b.wait("vector", s_pm, (jj - 1) * 8)
        b.op("vector", lambda jj=jj: nc.vector.tensor_copy(out=wb[jj % 2][:], in_=wfp[jj % 2][:]))
        b.pub("vector", s_wc)

    for jj in range(NJ):
        b.wait("tensor", s_wc, jj + 1)
        for tb in range(NBT):
            g = jj * 8 + tb
            if g >= 6:
                b.wait("tensor", s_ev, g - 5)
            for k in range(KT):
                st, sp = (k == 0), (k == KT - 1)
                b.op("tensor", lambda jj=jj, tb=tb, k=k, g=g, st=st, sp=sp: nc.tensor.matmul(
                    out=ps[g % 6].ap()[:], lhsT=wb[jj % 2][:, k * P:(k + 1) * P],
                    rhs=xn_res[:, k * T + tb * 512:k * T + (tb + 1) * 512],
                    start=st, stop=sp), s_pm if sp else None)

    s_evd = b.sem("b2evd")
    for jj in range(NJ):
        func = AF.Silu if jj < 48 else AF.Sigmoid
        for tb in range(NBT):
            g = jj * 8 + tb
            b.wait("scalar", s_pm, g + 1)
            if jj >= 1:
                sl_o.wait("scalar", tb, 16 * jj)
            b.op("scalar", lambda g=g, tb=tb, func=func: nc.scalar.activation(
                out=orow[:, tb * 512:(tb + 1) * 512], in_=ps[g % 6].ap()[:], func=func), s_ev)
            b.pub("scalar", s_evd)

    for jj in range(NJ):
        if jj < 48:
            h, kind = jj // 3, jj % 3
            dst = [q_sc, k_sc, v_sc][kind]
        else:
            h, dst = jj - 48, g_sc
        for tb in range(NBT):
            g = jj * 8 + tb
            b.wait("gpsimd", s_evd, g + 1)
            sl_o.load("gpsimd", tb, lambda dst=dst, h=h, tb=tb: nc.gpsimd.dma_start(
                out=dst[tb * 2:(tb + 1) * 2, :, h, :].rearrange("cb p m -> p cb m"),
                in_=orow.ap()[:, tb * 512:(tb + 1) * 512].rearrange("p (cb m) -> p cb m", cb=2)))
    for tb in range(NBT):
        sl_o.wait("gpsimd", tb, 16 * NJ)
    b.barrier()
    r.close()


def emit_l1a_scan(b, c, lay, xin, xout):
    nc = b.nc
    r = b.region()
    wob = b.sb(r, "b3wo", [P, H * KT * P], BF16)
    kv = b.sb(r, "b3kv", [P, H * P], F32)
    qdec = b.sb(r, "b3qd", [P, H * BLK1], BF16)
    diag = b.sb(r, "b3dg", [P, H * 2 * BLK1], BF16)
    kdec = b.sb(r, "b3kd", [P, 2 * H], F32)
    bdec = b.sb(r, "b3bd", [P, H], F32)
    negs = b.sb(r, "b3ns", [P, H], F32)
    slope_sb = b.sb(r, "b3sl", [P, H], F32)
    arrb = b.sb(r, "b3ar", [P, BLK1], F32)
    revc = b.sb(r, "b3rv", [P, 2], F32)
    posb = b.sb(r, "b3po", [P, 2 * BLK1], F32)
    maskb = b.sb(r, "b3mk", [P, 2 * BLK1], F32)
    blkc = b.sb(r, "b3bc", [P, 1], F32)
    nwpc = b.sb(r, "b3nw", [P, KT], F32)
    ids = b.sb(r, "b3id", [P, P], BF16)
    etmp = b.sb(r, "b3et", [P, BLK1], F32)
    wfp = [b.sb(r, "b3wf0", [P, KT * P], F32)]
    qch = [b.sb(r, "b3q0", [P, H * BLK1], BF16)]
    kch = [b.sb(r, "b3k0", [P, H * BLK1], BF16)]
    vch = [b.sb(r, "b3v0", [P, H * BLK1], BF16)]
    gch = [b.sb(r, "b3g0", [P, H * BLK1], BF16)]
    ktd = b.sb(r, "b3ktd", [P, BLK1], BF16)
    vtok = b.sb(r, "b3vt", [P, BLK1], BF16)
    scm = b.sb(r, "b3sc", [P, 2 * BLK1], BF16)
    qd = b.sb(r, "b3qdd", [P, BLK1], F32)
    dkv = b.sb(r, "b3dk", [P, P], F32)
    afm = b.sb(r, "b3af", [P, H * BLK1], BF16)
    asq = b.sb(r, "b3as", [P, BLK1], BF16)
    rowb = b.sb(r, "b3rw", [1, BLK1], F32)
    rb_c = b.sb(r, "b3rb", [P, BLK1], F32)
    gtmp = b.sb(r, "b3gt", [P, BLK1], F32)
    grbuf = b.sb(r, "b3gr", [P, H, BLK1], BF16)
    ywo = b.sb(r, "b3yw", [P, KT, BLK1], F32)
    ps_tk = b.ps(r, "b3ptk", [P, 1024], BF16)
    ps_tv = b.ps(r, "b3ptv", [P, 1024], BF16)
    ps_sc = [b.ps(r, f"b3psc{i}", [P, 512]) for i in range(2)]
    ps_o = b.ps(r, "b3po2", [P, 512])
    ps_kv = b.ps(r, "b3pkv", [P, 512])
    ps_rm = b.ps(r, "b3prm", [P, 512])
    ps_wo = [b.ps(r, "b3pw0", [P, 512])]
    pbufs = mk_postln_bufs(b, r, "b3p", c, ps_reg=None,
                           ps_override={"pl": ps_sc[0], "rpa": ps_sc[1], "rpb": ps_o})
    E = mk_postln_sems(b, "b3p")
    s_cst = b.sem("b3cst")
    emit_postln_consts(b, pbufs, c, s_cst)

    q_sc, k_sc = b._dram["q1_sc"], b._dram["k1_sc"]
    v_sc, g_sc = b._dram["v1_sc"], b._dram["g1_sc"]
    wo_d = b.din("l1_woutT", lay["l1_woutT"])

    s_ldc = b.sem("b3ldc")
    sl_wol = Slots(b, "b3wol", 1)
    s_wc = b.sem("b3wc")
    sl_ch = Slots(b, "b3ch", 1)
    s_dec = b.sem("b3dec")
    s_t1 = b.sem("b3t1")
    s_t2 = b.sem("b3t2")
    s_m1 = b.sem("b3m1")
    s_o = b.sem("b3o")
    s_m5 = b.sem("b3m5")
    s_m6 = b.sem("b3m6")
    s_vt = b.sem("b3vt")
    s_dkv = b.sem("b3dkv")
    s_aq = b.sem("b3aq")
    s_scm = b.sem("b3scm")
    s_qd = b.sem("b3qd2")
    s_ktd = b.sem("b3ktd2")
    s_kv = b.sem("b3kv2")
    s_rr = b.sem("b3rr")
    s_rp = b.sem("b3rp")
    s_rb = b.sem("b3rb2")
    s_gr = b.sem("b3gr2")
    s_wo = b.sem("b3wo2")
    s_ye = b.sem("b3ye")

    for nm, dst, src in [("id", ids, ("ident", c["ident"])),
                         ("sl", slope_sb, ("l1_slope_b", lay["l1_slope_b"])),
                         ("ar", arrb, ("l1_arr_row_b", c["l1_arr_row_b"])),
                         ("rv", revc, ("l1_rev_pc", c["l1_rev_pc"])),
                         ("bc", blkc, ("l1_blk_const", c["l1_blk_const"])),
                         ("nw", nwpc, ("l1_normw", lay["l1_normw"]))]:
        b.op("sync", lambda dst=dst, src=src: nc.sync.dma_start(out=dst[:], in_=b.din(*src)), s_ldc, 16)
    b.op("sync", lambda: nc.sync.dma_start(
        out=posb.ap().rearrange("p (h m) -> p h m", h=2),
        in_=b.din("l1_posT", c["l1_posT"]).rearrange("h p m -> p h m")), s_ldc, 16)
    b.op("sync", lambda: nc.sync.dma_start(
        out=maskb.ap().rearrange("p (h m) -> p h m", h=2),
        in_=b.din("l1_maskT", c["l1_maskT"]).rearrange("h p m -> p h m")), s_ldc, 16)
    NLDC = 8
    for j in range(H):
        if j >= 1:
            b.wait("sync", s_wc, j)
        sl_wol.load("sync", 0, lambda j=j: nc.sync.dma_start(out=wfp[0][:], in_=wo_d[j]))

    s_dge = b.sem("b3dge")
    s_dgv = b.sem("b3dgv")
    b.wait("vector", s_ldc, 16 * NLDC)
    b.op("vector", lambda: nc.vector.tensor_scalar(
        out=negs[:], in0=slope_sb[:], scalar1=-1.0, scalar2=None, op0=ALU.mult))
    b.pub("vector", s_dec)
    b.wait("scalar", s_ldc, 16 * NLDC)
    b.wait("scalar", s_dec, 1)
    for h in range(H):
        b.op("scalar", lambda h=h: nc.scalar.activation(
            out=qdec[:, h * BLK1:(h + 1) * BLK1], in_=arrb[:], func=AF.Exp,
            scale=negs[:, h:h + 1]))
        for half in range(2):
            b.op("scalar", lambda h=h, half=half: nc.scalar.activation(
                out=kdec[:, 2 * h + half:2 * h + half + 1], in_=revc[:, half:half + 1],
                func=AF.Exp, scale=negs[:, h:h + 1]))
        b.op("scalar", lambda h=h: nc.scalar.activation(
            out=bdec[:, h:h + 1], in_=blkc[:], func=AF.Exp, scale=negs[:, h:h + 1]))
    for i in range(2 * H):
        h, half = divmod(i, 2)
        if i >= 1:
            b.wait("scalar", s_dgv, i)
        b.op("scalar", lambda h=h, half=half, i=i: nc.scalar.activation(
            out=etmp[:, (i % 2) * 0:BLK1] if False else etmp[:],
            in_=posb[:, half * BLK1:(half + 1) * BLK1], func=AF.Exp,
            scale=negs[:, h:h + 1]))
        b.pub("scalar", s_dge)
        b.wait("vector", s_dge, i + 1)
        b.op("vector", lambda h=h, half=half: nc.vector.tensor_tensor(
            out=diag[:, (h * 2 + half) * BLK1:(h * 2 + half + 1) * BLK1],
            in0=etmp[:], in1=maskb[:, half * BLK1:(half + 1) * BLK1], op=ALU.mult))
        b.pub("vector", s_dgv)
    b.pub("scalar", s_dec)

    for j in range(H):
        sl_wol.wait("vector", 0, 16 * (j + 1))
        for k in range(KT):
            b.op("vector", lambda j=j, k=k: nc.vector.tensor_scalar(
                out=wob[:, (j * KT + k) * P:(j * KT + k + 1) * P],
                in0=wfp[0][:, k * P:(k + 1) * P],
                scalar1=nwpc[:, k:k + 1], scalar2=None, op0=ALU.mult))
        b.pub("vector", s_wc)
    b.op("gpsimd", lambda: nc.gpsimd.memset(kv[:], 0.0))
    b.pub("gpsimd", s_kv)

    parts = {}
    for nb in range(NB1):
        if nb >= 1:
            b.wait("sync", s_m5, nb * H)
            b.wait("sync", s_ktd, nb * H)
            b.wait("sync", s_gr, nb * H)
        for src, dstb in [(q_sc, qch), (k_sc, kch), (v_sc, vch), (g_sc, gch)]:
            sl_ch.load("sync", 0, lambda src=src, dstb=dstb, nb=nb: nc.sync.dma_start(
                out=dstb[0].ap().rearrange("p (h m) -> p h m", h=H), in_=src[nb]))
            sl_ch.uses[0] -= 1
        sl_ch.uses[0] += 1
        parts[nb] = emit_postln_group(
            b, "b3p", E, nb, nb, (lambda j: ywo.ap()[:, j, :]),
            xin, xout, pbufs, GW=BLK1,
            yready=(lambda j, nb=nb: b.wait("vector", s_ye, nb * KT + j + 1)))
        parts[nb][4]()

    def hs2(h):
        return slice(h * BLK1, (h + 1) * BLK1)

    b.wait("tensor", s_ldc, 16 * NLDC)
    b.wait("tensor", s_wc, H)
    for nb in range(NB1):
        sl_ch.wait("tensor", 0, 64 * (nb + 1))
        if nb >= 1:
            b.wait("tensor", E["rbe"], nb)
            b.wait("tensor", E["r2"], 3 * nb)
        for h in range(H):
            n = nb * H + h
            if n >= 1:
                b.wait("tensor", s_ktd, n)
                b.wait("tensor", s_vt, n)
            for half in range(2):
                b.op("tensor", lambda nb=nb, h=h, half=half: nc.tensor.transpose(
                    out=ps_tk.ap()[:, half * P:(half + 1) * P],
                    in_=kch[0][:, h * BLK1 + half * P: h * BLK1 + (half + 1) * P],
                    identity=ids[:]), s_t1 if half == 1 else None)
            for half in range(2):
                b.op("tensor", lambda nb=nb, h=h, half=half: nc.tensor.transpose(
                    out=ps_tv.ap()[:, half * P:(half + 1) * P],
                    in_=vch[0][:, h * BLK1 + half * P: h * BLK1 + (half + 1) * P],
                    identity=ids[:]), s_t2 if half == 1 else None)
            if n >= 1:
                b.wait("tensor", s_ktd, n)
            for half in range(2):
                b.op("tensor", lambda nb=nb, h=h, half=half: nc.tensor.matmul(
                    out=ps_sc[half].ap()[:, 0:BLK1],
                    lhsT=kch[0][:, h * BLK1 + half * P: h * BLK1 + (half + 1) * P],
                    rhs=qch[0][:, hs2(h)], start=True, stop=True),
                    s_m1 if half == 1 else None)
            b.wait("tensor", s_kv, (n + 2 - H) if nb >= 1 else 1)
            b.wait("tensor", s_ktd, n + 1)
            if n >= 1:
                b.wait("tensor", s_aq, n)
            b.op("tensor", lambda h=h: nc.tensor.matmul(
                out=ps_o.ap()[:, 0:BLK1], lhsT=kv[:, h * P:(h + 1) * P], rhs=qd[:],
                start=True, stop=False))
            b.wait("tensor", s_vt, n + 1)
            for half in range(2):
                b.op("tensor", lambda half=half: nc.tensor.matmul(
                    out=ps_o.ap()[:, 0:BLK1], lhsT=vtok[:, half * P:(half + 1) * P],
                    rhs=scm[:, half * BLK1:(half + 1) * BLK1], start=False,
                    stop=(half == 1)), s_o if half == 1 else None)
            if n >= 1:
                b.wait("tensor", s_kv, n + 1)
            for half in range(2):
                b.op("tensor", lambda half=half: nc.tensor.matmul(
                    out=ps_kv.ap()[:, 0:P], lhsT=ktd[:, half * P:(half + 1) * P],
                    rhs=vtok[:, half * P:(half + 1) * P], start=(half == 0),
                    stop=(half == 1)), s_m5 if half == 1 else None)
            b.wait("tensor", s_aq, n + 1)
            if h == 0 and nb >= 1:
                b.wait("tensor", s_rp, nb)
                b.wait("tensor", s_rb, nb)
            b.op("tensor", lambda h=h: nc.tensor.matmul(
                out=ps_rm.ap()[0:1, 0:BLK1], lhsT=ones_dummy[:] if False else pbufs["ones"][:],
                rhs=asq[:], start=(h == 0), stop=(h == H - 1)), s_m6)
        b.wait("tensor", s_rr, nb + 1)
        b.op("tensor", lambda: nc.tensor.matmul(
            out=ps_rm.ap()[:, BLK1:2 * BLK1], lhsT=pbufs["oner"][:], rhs=rowb[0:1, :],
            start=True, stop=True), s_rp)
        b.wait("tensor", s_gr, (nb + 1) * H)
        for j in range(KT):
            if nb * KT + j >= 1:
                b.wait("tensor", s_ye, nb * KT + j)
            for h in range(H):
                b.op("tensor", lambda j=j, h=h, nb=nb: nc.tensor.matmul(
                    out=ps_wo[0].ap()[:, 0:BLK1], lhsT=wob[:, (j * KT + h) * P:(j * KT + h + 1) * P],
                    rhs=grbuf.ap()[:, h, :], start=(h == 0), stop=(h == H - 1)),
                    s_wo if h == H - 1 else None, inc=H)
        parts[nb][0]()

    b.wait("vector", s_dgv, 2 * H)
    b.wait("vector", s_dec, 2)
    for nb in range(NB1):
        sl_ch.wait("vector", 0, 64 * (nb + 1))
        for h in range(H):
            n = nb * H + h
            b.wait("vector", s_t1, n + 1)
            b.wait("vector", s_m5, n)
            for half in range(2):
                b.op("vector", lambda h=h, half=half: nc.vector.tensor_scalar(
                    out=ktd[:, half * P:(half + 1) * P], in0=ps_tk.ap()[:, half * P:(half + 1) * P],
                    scalar1=kdec[:, 2 * h + half:2 * h + half + 1], scalar2=None, op0=ALU.mult))
            b.wait("vector", s_m1, n + 1)
            b.wait("vector", s_o, n)
            for half in range(2):
                b.op("vector", lambda h=h, half=half: nc.vector.tensor_tensor(
                    out=scm[:, half * BLK1:(half + 1) * BLK1], in0=ps_sc[half].ap()[:, 0:BLK1],
                    in1=diag[:, (h * 2 + half) * BLK1:(h * 2 + half + 1) * BLK1], op=ALU.mult))
            b.op("vector", lambda nb=nb, h=h: nc.vector.tensor_tensor(
                out=qd[:], in0=qch[0][:, hs2(h)], in1=qdec[:, hs2(h)], op=ALU.mult))
            b.pub("vector", s_ktd)
            b.wait("vector", s_m5, n + 1)
            b.op("vector", lambda h=h: nc.vector.scalar_tensor_tensor(
                out=kv[:, h * P:(h + 1) * P], in0=kv[:, h * P:(h + 1) * P],
                scalar=bdec[:, h:h + 1], in1=ps_kv.ap()[:, 0:P], op0=ALU.mult, op1=ALU.add))
            b.pub("vector", s_kv)
        b.wait("vector", s_rb, nb + 1)
        b.wait("vector", s_aq, (nb + 1) * H)
        b.wait("vector", s_wo, (nb) * KT * H)
        for h in range(H):
            b.op("vector", lambda h=h: nc.vector.tensor_tensor(
                out=gtmp[:], in0=afm[:, hs2(h)], in1=rb_c[:], op=ALU.mult))
            b.op("vector", lambda h=h, nb=nb: nc.vector.tensor_tensor(
                out=grbuf.ap()[:, h, :], in0=gtmp[:], in1=gch[0][:, hs2(h)], op=ALU.mult))
        b.pub("vector", s_gr, inc=H)
        parts[nb][1]()

    b.wait("scalar", s_cst, 48)
    for nb in range(NB1):
        gt = nb // 2
        for h in range(H):
            n = nb * H + h
            b.wait("scalar", s_t2, n + 1)
            b.wait("scalar", s_m5, n)
            b.op("scalar", lambda: nc.scalar.copy(out=vtok[:], in_=ps_tv.ap()[:, 0:2 * P]))
            b.pub("scalar", s_vt)
            b.wait("scalar", s_o, n + 1)
            if nb >= 1:
                b.wait("scalar", s_gr, n - H + 1)
            b.wait("scalar", s_m6, n)
            b.op("scalar", lambda h=h: nc.scalar.copy(out=afm[:, hs2(h)], in_=ps_o.ap()[:, 0:BLK1]))
            b.op("scalar", lambda: nc.scalar.activation(out=asq[:], in_=ps_o.ap()[:, 0:BLK1], func=AF.Square))
            b.pub("scalar", s_aq)
        b.wait("scalar", s_m6, (nb + 1) * H)
        b.wait("scalar", s_rp, nb)
        b.op("scalar", lambda: nc.scalar.activation(
            out=rowb[0:1, :], in_=ps_rm.ap()[0:1, 0:BLK1], func=AF.Ln, scale=1.0 / D,
            bias=pbufs["cp"][0:1, 1:2]))
        b.op("scalar", lambda: nc.scalar.activation(
            out=rowb[0:1, :], in_=rowb[0:1, :], func=AF.Exp, scale=-0.5))
        b.pub("scalar", s_rr)
        b.wait("scalar", s_rp, nb + 1)
        b.wait("scalar", s_gr, nb * H)
        b.op("scalar", lambda: nc.scalar.copy(out=rb_c[:], in_=ps_rm.ap()[:, BLK1:2 * BLK1]))
        b.pub("scalar", s_rb)
        for j in range(KT):
            b.wait("scalar", s_wo, nb * KT * H + (j + 1) * H)
            if nb >= 1:
                b.wait("scalar", E["av"], nb * 8)
                b.wait("scalar", E["ap"], nb * 8)
            b.op("scalar", lambda j=j, nb=nb: nc.scalar.copy(
                out=ywo.ap()[:, j, :], in_=ps_wo[0].ap()[:, 0:BLK1]))
            b.pub("scalar", s_ye)
        parts[nb][2]()

    for nb in range(NB1):
        parts[nb][3]()

    for i in range(2):
        E["sl_xwV"].wait("sync", i, 16 * (NB1 * (KT // 2) // 2))
        E["sl_xwP"].wait("sync", i, 16 * (NB1 * (KT // 2) // 2))
    b.barrier()
    r.close()


def ph_l1a(b, c, lay, xin, xout):
    pr = b.region()
    xn = emit_preln(b, "b1", xin, c, pr)
    emit_l1a_qkvg(b, c, lay, xn, pr)
    pr.close()
    emit_l1a_scan(b, c, lay, xin, xout)



def emit_l1f_gateup(b, c, lay, xn_res, pr):
    nc = b.nc
    r = b.region()
    w8g = [b.sb(r, f"e2g{i}", [P, KT * P], I8) for i in range(2)]
    w8u = [b.sb(r, f"e2u{i}", [P, KT * P], I8) for i in range(2)]
    wbg = [b.sb(r, f"e2bg{i}", [P, KT * P], BF16) for i in range(2)]
    wbu = [b.sb(r, f"e2bu{i}", [P, KT * P], BF16) for i in range(2)]
    hrow = [b.sb(r, f"e2h{i}", [P, T], BF16) for i in range(2)]
    sg_sb = b.sb(r, "e2sg", [P, OT1], F32)
    su_sb = b.sb(r, "e2su", [P, OT1], F32)
    ps = [b.ps(r, f"e2ps{i}", [P, 512]) for i in range(6)]
    wg_d = b.din("l1_wg8", lay["l1_wg8"])
    wu_d = b.din("l1_wu8", lay["l1_wu8"])
    h_sc = b.dscratch("h2_sc", [OT1, P, T], BF16)
    sl_w = Slots(b, "e2w", 2)
    s_sld = b.sem("e2sld")
    s_wc = b.sem("e2wc")
    s_pm = b.sem("e2pm")
    s_evg = b.sem("e2evg")
    s_evu = b.sem("e2evu")
    sl_h = Slots(b, "e2ho", 2, persistent=True)

    b.op("sync", lambda: nc.sync.dma_start(out=sg_sb[:], in_=b.din("l1_sg", lay["l1_sg"])), s_sld, 16)
    b.op("sync", lambda: nc.sync.dma_start(out=su_sb[:], in_=b.din("l1_su", lay["l1_su"])), s_sld, 16)

    def bank_owner(g_all):
        o_, u_ = divmod(g_all, 16)
        return ("g", o_ * 8 + u_ + 1) if u_ < 8 else ("u", o_ * 8 + (u_ - 8) + 1)

    for o in range(OT1):
        if o >= 2:
            b.wait("sync", s_wc, o - 1)
        sl_w.load("sync", o % 2, lambda o=o: nc.sync.dma_start(out=w8g[o % 2][:], in_=wg_d[o]))
        sl_w.uses[o % 2] -= 1
        sl_w.load("sync", o % 2, lambda o=o: nc.sync.dma_start(out=w8u[o % 2][:], in_=wu_d[o]))

    def _upev(o):
        for tb in range(NBT):
            g_all = o * 16 + 8 + tb
            bank = g_all % 6
            b.wait("vector", s_pm, g_all + 1)
            b.wait("vector", s_evg, o * 8 + tb + 1)
            b.op("vector", lambda o=o, tb=tb, bank=bank: nc.vector.scalar_tensor_tensor(
                out=hrow[o % 2][:, tb * 512:(tb + 1) * 512], in0=ps[bank].ap()[:],
                scalar=su_sb[:, o:o + 1], in1=hrow[o % 2][:, tb * 512:(tb + 1) * 512],
                op0=ALU.mult, op1=ALU.mult))
            b.pub("vector", s_evu)

    for o in range(OT1 + 1):
        if o < OT1:
            sl_w.wait("vector", o % 2, 32 * (o // 2 + 1))
            if o >= 2:
                b.wait("vector", s_pm, (o - 1) * 16)
            b.op("vector", lambda o=o: nc.vector.tensor_copy(out=wbg[o % 2][:], in_=w8g[o % 2][:]))
            b.op("vector", lambda o=o: nc.vector.tensor_copy(out=wbu[o % 2][:], in_=w8u[o % 2][:]))
            b.pub("vector", s_wc)
        if o >= 1:
            _upev(o - 1)

    for o in range(OT1):
        b.wait("tensor", s_wc, o + 1)
        for u in range(16):
            g_all = o * 16 + u
            wt = wbg if u < 8 else wbu
            tb = u % 8
            bank = g_all % 6
            if g_all >= 6:
                own, cnt = bank_owner(g_all - 6)
                b.wait("tensor", s_evg if own == "g" else s_evu, cnt)
            for k in range(KT):
                st, sp = (k == 0), (k == KT - 1)
                b.op("tensor", lambda o=o, tb=tb, k=k, bank=bank, wt=wt, st=st, sp=sp: nc.tensor.matmul(
                    out=ps[bank].ap()[:], lhsT=wt[o % 2][:, k * P:(k + 1) * P],
                    rhs=xn_res[:, k * T + tb * 512:k * T + (tb + 1) * 512],
                    start=st, stop=sp), s_pm if sp else None)

    b.wait("scalar", s_sld, 32)
    for o in range(OT1):
        for tb in range(NBT):
            g_all = o * 16 + tb
            bank = g_all % 6
            b.wait("scalar", s_pm, g_all + 1)
            if tb == 0 and o >= 2:
                sl_h.wait("scalar", o % 2, 16 * (o // 2))
            b.op("scalar", lambda o=o, tb=tb, bank=bank: nc.scalar.activation(
                out=hrow[o % 2][:, tb * 512:(tb + 1) * 512], in_=ps[bank].ap()[:],
                func=AF.Silu, scale=sg_sb[:, o:o + 1]))
            b.pub("scalar", s_evg)

    for o in range(OT1):
        b.wait("gpsimd", s_evu, (o + 1) * 8)
        sl_h.load("gpsimd", o % 2, lambda o=o: nc.gpsimd.dma_start(out=h_sc[o], in_=hrow[o % 2][:]))
    for i in range(2):
        sl_h.wait("gpsimd", i, 16 * ((OT1 + 1 - i) // 2))
    b.barrier()
    r.close()


def emit_l1f_wdcast(b, c, lay):
    nc = b.nc
    r = b.region()
    w8 = [b.sb(r, f"ewc8{i}", [P, KT * P], I8) for i in range(2)]
    wb = [b.sb(r, f"ewcb{i}", [P, KT * P], BF16) for i in range(2)]
    wd_d = b.din("l1_wdk", lay["l1_wdk"])
    wdb_sc = b.dscratch("wdb_sc", [OT1, P, KT * P], BF16)
    sl_i = Slots(b, "ewci", 2)
    s_c = b.sem("ewcc")
    sl_o = Slots(b, "ewco", 2)
    for ko in range(OT1):
        if ko >= 2:
            b.wait("sync", s_c, ko - 1)
        sl_i.load("sync", ko % 2, lambda ko=ko: nc.sync.dma_start(
            out=w8[ko % 2][:], in_=wd_d[ko].rearrange("p j m -> p (j m)")))
    for ko in range(OT1):
        sl_i.wait("vector", ko % 2, 16 * (ko // 2 + 1))
        if ko >= 2:
            sl_o.wait("vector", ko % 2, 16 * (ko // 2))
        b.op("vector", lambda ko=ko: nc.vector.tensor_copy(out=wb[ko % 2][:], in_=w8[ko % 2][:]))
        b.pub("vector", s_c)
    for ko in range(OT1):
        b.wait("scalar", s_c, ko + 1)
        sl_o.load("scalar", ko % 2, lambda ko=ko: nc.scalar.dma_start(
            out=wdb_sc[ko], in_=wb[ko % 2][:]))
    for i in range(2):
        sl_o.wait("scalar", i, 16 * ((OT1 + 1 - i) // 2))
    return r


def emit_l1f_down(b, c, lay, xin, xout):
    nc = b.nc
    r = b.region()
    KOC = [8] * 8
    NTQ = 4
    ydown = b.sb(r, "e3y", [P, KT, 1024], F32)
    wdc2 = [b.sb(r, f"e3w{i}", [P, 8, KT * P], BF16) for i in range(2)]
    hq = [b.sb(r, f"e3h{i}", [P, 8, 1024], BF16) for i in range(2)]
    sd_sb = b.sb(r, "e3sd", [P, KT], F32)
    ps = [b.ps(r, f"e3ps{i}", [P, 512]) for i in range(4)]
    pbufs = mk_postln_bufs(b, r, "e3p", c)
    E = mk_postln_sems(b, "e3p")
    s_cst = b.sem("e3cst")
    emit_postln_consts(b, pbufs, c, s_cst)
    b.op("sync", lambda: nc.sync.dma_start(out=sd_sb[:], in_=b.din("l1_sd", lay["l1_sd"])), s_cst, 16)
    wdb_sc = b._dram["wdb_sc"]
    h_sc = b._dram["h2_sc"]
    sl_wd = Slots(b, "e3wd", 2)
    sl_hq = Slots(b, "e3hq", 2)
    s_mm = b.sem("e3mm")
    s_ev = b.sem("e3ev")
    s_wdc = b.sem("e3wdc")

    tgt_wd, tgt_hq = {}, {}
    wd_cnt, hq_cnt = [0, 0], [0, 0]

    def unit(tq, ci, j, tb):
        return ((tq * 8 + ci) * KT + j) * 2 + tb

    parts = {}
    for tq in range(NTQ):
        for ci, kosz in enumerate(KOC):
            ko0 = sum(KOC[:ci])
            idx = tq * 8 + ci
            slot = idx % 2
            if idx >= 2:
                b.wait("sync", s_mm, (idx - 1) * 32)
            for kk in range(kosz):
                sl_wd.load("sync", slot, lambda ko0=ko0, kk=kk, slot=slot: nc.sync.dma_start(
                    out=wdc2[slot].ap()[:, kk, :], in_=wdb_sc[ko0 + kk]))
                sl_hq.load("sync", slot, lambda ko0=ko0, kk=kk, slot=slot, tq=tq: nc.sync.dma_start(
                    out=hq[slot].ap()[:, kk, :], in_=h_sc[ko0 + kk][:, tq * 1024:(tq + 1) * 1024]))
            wd_cnt[slot] += 16 * kosz
            hq_cnt[slot] += 16 * kosz
            tgt_wd[idx] = wd_cnt[slot]
            tgt_hq[idx] = hq_cnt[slot]
        for gg in range(2):
            g = tq * 2 + gg
            parts[g] = emit_postln_group(
                b, "e3p", E, g, g, (lambda j, gg=gg: ydown.ap()[:, j, gg * 512:(gg + 1) * 512]),
                xin, xout, pbufs)
        for gg in range(2):
            parts[tq * 2 + gg][4]()

    for tq in range(NTQ):
        if tq >= 1:
            b.wait("vector", E["ap"], tq * 16)
        for ci, kosz in enumerate(KOC):
            idx = tq * 8 + ci
            slot = idx % 2
            for j in range(KT):
                for tb in range(2):
                    u = unit(tq, ci, j, tb)
                    b.wait("vector", s_mm, u + 1)
                    if ci == 0:
                        b.op("vector", lambda j=j, tb=tb, bank=(j % 2) * 2 + tb: nc.vector.tensor_scalar(
                            out=ydown.ap()[:, j, tb * 512:(tb + 1) * 512], in0=ps[bank].ap()[:],
                            scalar1=sd_sb[:, j:j + 1], scalar2=None, op0=ALU.mult))
                    else:
                        b.op("vector", lambda j=j, tb=tb, bank=(j % 2) * 2 + tb: nc.vector.scalar_tensor_tensor(
                            out=ydown.ap()[:, j, tb * 512:(tb + 1) * 512], in0=ps[bank].ap()[:],
                            scalar=sd_sb[:, j:j + 1],
                            in1=ydown.ap()[:, j, tb * 512:(tb + 1) * 512],
                            op0=ALU.mult, op1=ALU.add))
                    b.pub("vector", s_ev)
        for gg in range(2):
            parts[tq * 2 + gg][1]()

    b.wait("tensor", s_cst, 64)
    for tq in range(NTQ):
        for ci, kosz in enumerate(KOC):
            idx = tq * 8 + ci
            slot = idx % 2
            sl_hq.wait("tensor", slot, tgt_hq[idx])
            sl_wd.wait("tensor", slot, tgt_wd[idx])
            for j in range(KT):
                for kk in range(kosz):
                    for tb in range(2):
                        bank = (j % 2) * 2 + tb
                        u = unit(tq, ci, j, tb)
                        if kk == 0 and u >= 4:
                            b.wait("tensor", s_ev, u - 3)
                        st, sp = (kk == 0), (kk == kosz - 1)
                        b.op("tensor", lambda slot=slot, kk=kk, j=j, tb=tb, bank=bank, st=st, sp=sp: nc.tensor.matmul(
                            out=ps[bank].ap()[:], lhsT=wdc2[slot][:, kk, j * P:(j + 1) * P],
                            rhs=hq[slot].ap()[:, kk, tb * 512:(tb + 1) * 512],
                            start=st, stop=sp), s_mm if sp else None)
        for gg in range(2):
            parts[tq * 2 + gg][0]()

    for tq in range(NTQ):
        for gg in range(2):
            parts[tq * 2 + gg][2]()
    for tq in range(NTQ):
        for gg in range(2):
            parts[tq * 2 + gg][3]()

    for i in range(2):
        E["sl_xwV"].wait("sync", i, 16 * (NTQ * 2 * (KT // 2) // 2))
        E["sl_xwP"].wait("sync", i, 16 * (NTQ * 2 * (KT // 2) // 2))
    b.barrier()
    r.close()


def ph_l1f(b, c, lay, xin, xout):
    rw = emit_l1f_wdcast(b, c, lay)
    pr = b.region()
    xn = emit_preln(b, "e1", xin, c, pr)
    emit_l1f_gateup(b, c, lay, xn, pr)
    pr.close()
    rw.close()
    emit_l1f_down(b, c, lay, xin, xout)


# version 08
# speedup vs baseline: 1.0851x; 1.0595x over previous
import math
from contextlib import ExitStack

import numpy as np
import ml_dtypes

import concourse.bass as bass
import concourse.mybir as mybir
from concourse.bass_utils import run_bass_kernel_spmd

F32 = mybir.dt.float32
BF16 = mybir.dt.bfloat16
I32 = mybir.dt.int32
I8 = mybir.dt.int8
AF = mybir.ActivationFunctionType
ALU = mybir.AluOpType
BF = ml_dtypes.bfloat16

P = 128
T = 4096
D = 2048
KT = D // P
NBT = T // 512
H = 16
DH = 128
C0 = 128
NC0 = T // C0
BLK1 = 256
NB1 = T // BLK1
I0 = 5504
OT0 = I0 // P
I1 = 8192
OT1 = I1 // P
N_CORES = 8

ENGINES = ("sync", "scalar", "vector", "tensor", "gpsimd")



def blkW(W):
    K, M = W.shape
    kt, mt = K // P, M // P
    return np.ascontiguousarray(
        W.reshape(kt, P, mt, P).transpose(2, 1, 0, 3).reshape(mt, P, kt * P))


def fm(x):
    return np.ascontiguousarray(x.T.reshape(KT, P, T))


def col_blocks(v):
    return np.ascontiguousarray(np.asarray(v).reshape(-1, P, 1))


def mk_consts():
    c = {}
    c["ones_bf"] = np.ones((P, 1), dtype=BF)
    c["ones_row_f32"] = np.ones((1, P), dtype=np.float32)
    c["ident"] = np.eye(P, dtype=BF)

    freq = (1.0 / 1024.0) ** np.linspace(0.0, 1.0, DH // 4, dtype=np.float32)
    freq = np.concatenate([freq, np.zeros(DH // 4, dtype=np.float32)])
    t = np.arange(T, dtype=np.float32)
    theta = (t[:, None] * freq[None, :]).astype(np.float32).astype(np.float64)
    cosT = np.cos(theta).astype(np.float32).T
    sinT = np.sin(theta).astype(np.float32).T
    c["cos2"] = np.ascontiguousarray(np.concatenate([cosT, cosT], axis=0))
    c["sin2pm"] = np.ascontiguousarray(np.concatenate([sinT, -sinT], axis=0))
    sw = np.zeros((P, P), dtype=BF)
    for m in range(P):
        sw[(m + 64) % P, m] = 1
    c["swapmat"] = sw

    start = 2.0 ** (-2.0 ** (-(math.log2(H) - 3)))
    slopes = np.array([start * start ** i for i in range(H)], dtype=np.float64)
    r = np.exp(-slopes)
    i_ = np.arange(C0, dtype=np.float64)
    qdec = r[:, None] ** (i_[None, :] + 1.0)
    kdec = r[:, None] ** (C0 - 1.0 - i_[None, :])
    diff = i_[:, None] - i_[None, :]
    dmask = np.where(diff >= 0, r[:, None, None] ** np.maximum(diff, 0.0)[None], 0.0)
    c["qdec_b"] = np.ascontiguousarray(
        np.broadcast_to(qdec[:, None, :], (H, P, C0)).astype(np.float32))
    c["kdec_pc"] = np.ascontiguousarray(kdec.astype(np.float32).T)
    c["dmaskT"] = np.ascontiguousarray(dmask.transpose(0, 2, 1).astype(np.float32))
    c["rC_pc"] = np.ascontiguousarray(
        np.broadcast_to((r ** float(C0)).astype(np.float32)[None, :], (P, H)))

    arr = np.arange(1, BLK1 + 1, dtype=np.float32)
    c["l1_arr_row_b"] = np.ascontiguousarray(np.broadcast_to(arr[None, :], (P, BLK1)))
    c["l1_rev_pc"] = np.ascontiguousarray((float(BLK1) - arr).reshape(2, P).T)
    idx = arr[:, None] - arr[None, :]
    c["l1_posT"] = np.ascontiguousarray(
        np.maximum(idx, 0.0).T.reshape(2, P, BLK1).astype(np.float32))
    c["l1_maskT"] = np.ascontiguousarray(
        (idx >= 0.0).T.astype(np.float32).reshape(2, P, BLK1))
    c["l1_blk_const"] = np.full((P, 1), float(BLK1), dtype=np.float32)
    cp = np.zeros((P, 8), dtype=np.float32)
    cp[:, 0] = 1e-5
    cp[:, 1] = 1e-6
    cp[:, 2] = 1.0 / D
    cp[:, 3] = -0.5
    cp[:, 4] = float(BLK1)
    c["cpack"] = cp
    return c


def mk_layouts(inp):
    d = {}
    x = np.asarray(inp["L0_attn_x"])
    d["x0"] = [fm(x[b]) for b in range(N_CORES)]

    d["l0_wq"] = blkW(np.asarray(inp["L0_attn_Wq"]))
    d["l0_wk"] = blkW(np.asarray(inp["L0_attn_Wk"]))
    d["l0_wv"] = blkW(np.asarray(inp["L0_attn_Wv"]))
    d["l0_wo"] = blkW(np.asarray(inp["L0_attn_Wo"]))
    d["l0_rmsw"] = np.ascontiguousarray(
        np.asarray(inp["L0_attn_rms_w"]).reshape(KT, P).T.astype(np.float32))

    lut = np.asarray(inp["L0_ffn_lut"])
    d["l0_lut"] = np.ascontiguousarray(np.broadcast_to(lut[None, :], (P, 16)).astype(np.float32))
    giT = np.asarray(inp["L0_ffn_gate_idx"]).T.reshape(KT, P, I0)
    uiT = np.asarray(inp["L0_ffn_up_idx"]).T.reshape(KT, P, I0)
    diT = np.asarray(inp["L0_ffn_down_idx"]).T.reshape(OT0, P, D)
    diT48 = np.concatenate([diT, np.zeros((48 - OT0, P, D), diT.dtype)])
    d["l0_giT_c"] = [np.ascontiguousarray(giT[2 * cc:2 * cc + 2]) for cc in range(N_CORES)]
    d["l0_uiT_c"] = [np.ascontiguousarray(uiT[2 * cc:2 * cc + 2]) for cc in range(N_CORES)]
    d["l0_diT_c"] = [np.ascontiguousarray(diT48[6 * cc:6 * cc + 6]) for cc in range(N_CORES)]

    sl = np.asarray(inp["L1_attn_slope_rate"]).reshape(H)
    d["l1_slope_b"] = np.ascontiguousarray(np.broadcast_to(sl[None, :], (P, H)).astype(np.float32))
    d["l1_wqkvT"] = blkW(np.ascontiguousarray(np.asarray(inp["L1_attn_w_qkv"]).T))
    d["l1_wgateT"] = blkW(np.ascontiguousarray(np.asarray(inp["L1_attn_w_gate"]).T))
    d["l1_woutT"] = blkW(np.ascontiguousarray(np.asarray(inp["L1_attn_w_out"]).T))
    d["l1_normw"] = np.ascontiguousarray(
        np.asarray(inp["L1_attn_norm_w"]).reshape(KT, P).T.astype(np.float32))

    d["l1_wg8"] = blkW(np.asarray(inp["L1_ffn_w_gate"]))
    d["l1_wu8"] = blkW(np.asarray(inp["L1_ffn_w_up"]))
    d["l1_wdk"] = np.ascontiguousarray(
        np.asarray(inp["L1_ffn_w_down"]).reshape(OT1, P, KT, P))
    d["l1_sg"] = np.ascontiguousarray(np.asarray(inp["L1_ffn_s_gate"]).reshape(OT1, P).T.astype(np.float32))
    d["l1_su"] = np.ascontiguousarray(np.asarray(inp["L1_ffn_s_up"]).reshape(OT1, P).T.astype(np.float32))
    d["l1_sd"] = np.ascontiguousarray(np.asarray(inp["L1_ffn_s_down"]).reshape(KT, P).T.astype(np.float32))
    return d


class Slots:

    def __init__(self, b, name, n, persistent=False):
        self.b = b
        self.sems = [b.sem(f"{name}_{i}", persistent=persistent) for i in range(n)]
        self.uses = [0] * n

    def load(self, eng, slot, fn):
        self.uses[slot] += 1
        self.b.op(eng, fn, self.sems[slot], 16)
        return 16 * self.uses[slot]

    def wait(self, eng, slot, count):
        self.b.wait(eng, self.sems[slot], count)


def np_dt(dt):
    dt = np.dtype(dt)
    if dt == np.float32:
        return F32
    if dt == np.int32:
        return I32
    if dt == np.int8:
        return I8
    if dt == np.dtype(BF):
        return BF16
    raise ValueError(dt)



class Build:
    def __init__(self):
        self.nc = bass.Bass("TRN2", target_bir_lowering=False)
        self.prog = {e: [] for e in ENGINES}
        self.stack = ExitStack()
        self.sem_stack = ExitStack()
        self._sems = {}
        self._bar = None
        self.host = {}
        self._dram = {}
        self.bar_n = 0

    def sem(self, name, persistent=False):
        if name == "bar":
            if self._bar is None:
                self._bar = self.stack.enter_context(self.nc.semaphore(name))
            return self._bar
        if not hasattr(self, "_psems"):
            self._psems = {}
        if persistent or name in self._psems:
            if name not in self._psems:
                self._psems[name] = self.stack.enter_context(self.nc.semaphore(name))
            return self._psems[name]
        if name not in self._sems:
            self._sems[name] = self.sem_stack.enter_context(self.nc.semaphore(name))
        return self._sems[name]

    def phase_sems_reset(self):
        self.barrier()
        e = self.nc.sync
        for h in list(self._sems.values()):
            self.prog["sync"].append(lambda h=h: e.sem_clear(h))
        self.barrier()
        self.sem_stack.close()
        self.sem_stack = ExitStack()
        self._sems = {}

    def din(self, name, arr):
        if name not in self._dram:
            a = arr[0] if isinstance(arr, list) else arr
            tn = self.nc.dram_tensor(name, list(a.shape), np_dt(a.dtype), kind="ExternalInput")
            self._dram[name] = tn.ap()
            self.host[name] = arr
        return self._dram[name]

    def dout(self, name, shape, dtype):
        if name not in self._dram:
            self._dram[name] = self.nc.dram_tensor(name, list(shape), dtype, kind="ExternalOutput").ap()
        return self._dram[name]

    def dscratch(self, name, shape, dtype):
        if name not in self._dram:
            self._dram[name] = self.nc.dram_tensor(name, list(shape), dtype).ap()
        return self._dram[name]

    def region(self):
        return ExitStack()

    def sb(self, reg, name, shape, dtype, **kw):
        return reg.enter_context(self.nc.sbuf_tensor(name, list(shape), dtype, **kw))

    def ps(self, reg, name, shape, dtype=F32):
        return reg.enter_context(self.nc.psum_tensor(name, list(shape), dtype))

    def ei(self, eng, fn):
        self.prog[eng].append(fn)

    def wait(self, eng, sem, val):
        s = self.sem(sem) if isinstance(sem, str) else sem
        e = getattr(self.nc, eng)
        self.prog[eng].append(lambda: e.wait_ge(s, val))

    def op(self, eng, fn, sem=None, inc=1):
        s = self.sem(sem) if isinstance(sem, str) else sem
        if s is None:
            self.prog[eng].append(fn)
        else:
            self.prog[eng].append(lambda: fn().then_inc(s, inc))

    def pub(self, eng, sem, inc=1):
        s = self.sem(sem) if isinstance(sem, str) else sem
        e = getattr(self.nc, eng)
        self.prog[eng].append(lambda: e.drain().then_inc(s, inc))

    def barrier(self):
        s = self.sem("bar")
        self.bar_n += len(ENGINES)
        tgt = self.bar_n
        for eng in ENGINES:
            e = getattr(self.nc, eng)
            if eng == "sync":
                self.prog[eng].append(lambda e=e: e.nop().then_inc(s, 1))
            else:
                self.prog[eng].append(lambda e=e: e.drain().then_inc(s, 1))
        for eng in ENGINES:
            e = getattr(self.nc, eng)
            self.prog[eng].append(lambda e=e: e.wait_ge(s, tgt))

    def finalize(self, final_waits=()):
        for sem, val in final_waits:
            self.wait("sync", sem, val)
        nc = self.nc
        with self.stack:
            with nc.Block() as block:
                def mk(eng):
                    def run(_):
                        for f in self.prog[eng]:
                            f()
                    return run
                block.sync(mk("sync"))
                block.scalar(mk("scalar"))
                block.vector(mk("vector"))
                block.tensor(mk("tensor"))
                block.gpsimd(mk("gpsimd"))
        return nc



def emit_preln(b, tag, xin, c, pr):
    nc = b.nc
    r0 = b.region()
    ones_sb = b.sb(r0, f"{tag}ones", [P, 1], BF16)
    oner_sb = b.sb(r0, f"{tag}oner", [1, P], F32)
    cp_sb = b.sb(r0, f"{tag}cp", [P, 8], F32)
    rowr = [b.sb(r0, f"{tag}rr{i}", [1, 512], F32) for i in range(2)]
    rowmr = [b.sb(r0, f"{tag}rm{i}", [1, 512], F32) for i in range(2)]
    stps = b.ps(r0, f"{tag}stps", [P, 4096])

    sl_x = Slots(b, tag + "x", 2)
    s_c = b.sem(tag + "c")
    s_q = b.sem(tag + "q")
    s_m = b.sem(tag + "m")
    s_1 = b.sem(tag + "1")

    r1 = b.region()
    xa = [b.sb(r1, f"{tag}xa{i}", [P, T], F32) for i in range(2)]
    xb = [b.sb(r1, f"{tag}xb{i}", [P, T], BF16) for i in range(2)]
    xq = [b.sb(r1, f"{tag}xq{i}", [P, T], BF16) for i in range(2)]

    b.op("sync", lambda: nc.sync.dma_start(out=ones_sb[:], in_=b.din("ones_bf", c["ones_bf"])), s_1, 16)
    b.op("sync", lambda: nc.sync.dma_start(out=oner_sb[:], in_=b.din("ones_row_f32", c["ones_row_f32"])), s_1, 16)
    b.op("sync", lambda: nc.sync.dma_start(out=cp_sb[:], in_=b.din("cpack", c["cpack"])), s_1, 16)
    for k in range(KT):
        if k >= 2:
            b.wait("sync", s_c, k - 1)
            b.wait("sync", s_q, k - 1)
        sl_x.load("sync", k % 2, lambda k=k: nc.sync.dma_start(out=xa[k % 2][:], in_=xin[k]))

    for k in range(KT):
        sl_x.wait("vector", k % 2, 16 * (k // 2 + 1))
        if k >= 2:
            b.wait("vector", s_m, (k - 1) * NBT)
        b.op("vector", lambda k=k: nc.vector.tensor_copy(out=xb[k % 2][:], in_=xa[k % 2][:]))
        b.pub("vector", s_c)

    for k in range(KT):
        sl_x.wait("scalar", k % 2, 16 * (k // 2 + 1))
        if k >= 2:
            b.wait("scalar", s_m, (k - 1) * NBT)
        b.op("scalar", lambda k=k: nc.scalar.activation(out=xq[k % 2][:], in_=xa[k % 2][:], func=AF.Square))
        b.pub("scalar", s_q)

    b.wait("tensor", s_1, 48)
    for k in range(KT):
        b.wait("tensor", s_c, k + 1)
        b.wait("tensor", s_q, k + 1)
        for tb in range(NBT):
            st, sp = (k == 0), (k == KT - 1)
            b.op("tensor", lambda k=k, tb=tb, st=st, sp=sp: nc.tensor.matmul(
                out=stps.ap()[0:1, tb * 512:(tb + 1) * 512], lhsT=ones_sb[:],
                rhs=xb[k % 2][:, tb * 512:(tb + 1) * 512], start=st, stop=sp))
            b.op("tensor", lambda k=k, tb=tb, st=st, sp=sp: nc.tensor.matmul(
                out=stps.ap()[64:65, tb * 512:(tb + 1) * 512], lhsT=ones_sb[:],
                rhs=xq[k % 2][:, tb * 512:(tb + 1) * 512], start=st, stop=sp), s_m)

    b.barrier()
    r1.close()
    xn_res = b.sb(pr, tag + "xn", [P, KT * T], BF16, side="right")

    r2 = b.region()
    r_b = [b.sb(r2, f"{tag}rb{i}", [P, 512], F32) for i in range(2)]
    mr_b = [b.sb(r2, f"{tag}mrb{i}", [P, 512], F32) for i in range(2)]
    xaV = [b.sb(r2, f"{tag}xaV{i}", [P, 512], F32) for i in range(2)]
    xaP = [b.sb(r2, f"{tag}xaP{i}", [P, 512], F32) for i in range(2)]
    t1V = b.sb(r2, f"{tag}t1V", [P, 512], F32)
    t1P = b.sb(r2, f"{tag}t1P", [P, 512], F32)
    mq = b.sb(r2, f"{tag}mq", [1, NBT * 1024], F32)
    msq = b.sb(r2, f"{tag}msq", [1, 512], F32)

    sl_xv = Slots(b, tag + "xv", 2)
    sl_xp = Slots(b, tag + "xp", 2)
    s_mq = b.sem(tag + "mq")
    s_rmv = b.sem(tag + "rmv")
    s_rl = b.sem(tag + "rl")
    s_row = b.sem(tag + "row")
    s_rp = b.sem(tag + "rp")
    s_rev = b.sem(tag + "rev")
    s_av = b.sem(tag + "av")
    s_ap_ = b.sem(tag + "ap")

    def _v_apply(tb):
        for k in range(0, KT, 2):
            nv = tb * 8 + k // 2 + 1
            sl_xv.wait("vector", (nv - 1) % 2, 16 * ((nv - 1) // 2 + 1))
            b.wait("vector", s_rev, tb + 1)
            b.op("vector", lambda tb=tb, k=k, nv=nv: nc.vector.tensor_tensor(
                out=t1V[:], in0=xaV[(nv - 1) % 2][:], in1=r_b[tb % 2][:], op=ALU.mult))
            b.op("vector", lambda tb=tb, k=k: nc.vector.tensor_tensor(
                out=xn_res[:, k * T + tb * 512: k * T + (tb + 1) * 512],
                in0=t1V[:], in1=mr_b[tb % 2][:], op=ALU.subtract), s_av)

    for tb in range(NBT):
        sl = (tb * 512, (tb + 1) * 512)
        b.op("vector", lambda tb=tb, sl=sl: nc.vector.tensor_scalar(
            out=mq[0:1, tb * 1024:tb * 1024 + 512], in0=stps.ap()[0:1, sl[0]:sl[1]],
            scalar1=1.0 / D, scalar2=None, op0=ALU.mult))
        b.op("vector", lambda tb=tb, sl=sl: nc.vector.tensor_scalar(
            out=mq[0:1, tb * 1024 + 512:tb * 1024 + 1024], in0=stps.ap()[64:65, sl[0]:sl[1]],
            scalar1=1.0 / D, scalar2=None, op0=ALU.mult))
    b.pub("vector", s_mq)
    for tb in range(NBT + 1):
        if tb < NBT:
            if tb >= 2:
                b.wait("vector", s_rp, tb - 1)
            b.op("vector", lambda tb=tb: nc.vector.tensor_tensor(
                out=msq[0:1, :], in0=mq[0:1, tb * 1024:tb * 1024 + 512],
                in1=mq[0:1, tb * 1024:tb * 1024 + 512], op=ALU.mult))
            b.op("vector", lambda tb=tb: nc.vector.tensor_tensor(
                out=mq[0:1, tb * 1024 + 512:tb * 1024 + 1024],
                in0=mq[0:1, tb * 1024 + 512:tb * 1024 + 1024], in1=msq[0:1, :], op=ALU.subtract))
            b.pub("vector", s_rmv)
            b.wait("vector", s_rl, tb + 1)
            b.op("vector", lambda tb=tb: nc.vector.tensor_tensor(
                out=rowmr[tb % 2][0:1, :], in0=mq[0:1, tb * 1024:tb * 1024 + 512],
                in1=rowr[tb % 2][0:1, :], op=ALU.mult))
            b.pub("vector", s_row)
        if tb >= 1:
            _v_apply(tb - 1)

    b.wait("scalar", s_1, 48)
    for tb in range(NBT):
        b.wait("scalar", s_rmv, tb + 1)
        if tb >= 2:
            b.wait("scalar", s_rp, tb - 1)
        b.op("scalar", lambda tb=tb: nc.scalar.activation(
            out=rowr[tb % 2][0:1, :], in_=mq[0:1, tb * 1024 + 512:tb * 1024 + 1024],
            func=AF.Ln, bias=cp_sb[0:1, 0:1]))
        b.op("scalar", lambda tb=tb: nc.scalar.activation(
            out=rowr[tb % 2][0:1, :], in_=rowr[tb % 2][0:1, :], func=AF.Exp, scale=-0.5))
        b.pub("scalar", s_rl)
        b.wait("scalar", s_rp, tb + 1)
        if tb >= 2:
            b.wait("scalar", s_av, (tb - 1) * NBT)
            b.wait("scalar", s_ap_, (tb - 1) * NBT)
        b1, b2 = (2 * tb) % 8, (2 * tb + 1) % 8
        b.op("scalar", lambda tb=tb, b1=b1: nc.scalar.copy(
            out=r_b[tb % 2][:], in_=stps.ap()[:, b1 * 512:(b1 + 1) * 512]))
        b.op("scalar", lambda tb=tb, b2=b2: nc.scalar.copy(
            out=mr_b[tb % 2][:], in_=stps.ap()[:, b2 * 512:(b2 + 1) * 512]))
        b.pub("scalar", s_rev)

    b.wait("tensor", s_mq, 1)
    for tb in range(NBT):
        b.wait("tensor", s_row, tb + 1)
        if tb >= 4:
            b.wait("tensor", s_rev, tb - 3)
        b1, b2 = (2 * tb) % 8, (2 * tb + 1) % 8
        b.op("tensor", lambda tb=tb, b1=b1: nc.tensor.matmul(
            out=stps.ap()[:, b1 * 512:(b1 + 1) * 512], lhsT=oner_sb[:],
            rhs=rowr[tb % 2][0:1, :], start=True, stop=True))
        b.op("tensor", lambda tb=tb, b2=b2: nc.tensor.matmul(
            out=stps.ap()[:, b2 * 512:(b2 + 1) * 512], lhsT=oner_sb[:],
            rhs=rowmr[tb % 2][0:1, :], start=True, stop=True), s_rp)

    nv = np_ = 0
    for tb in range(NBT):
        for k in range(KT):
            if k % 2 == 0:
                nv += 1
                if nv > 2:
                    b.wait("sync", s_av, nv - 2)
                sl_xv.load("sync", (nv - 1) % 2, lambda tb=tb, k=k, nv=nv: nc.sync.dma_start(
                    out=xaV[(nv - 1) % 2][:], in_=xin[k][:, tb * 512:(tb + 1) * 512]))
            else:
                np_ += 1
                if np_ > 2:
                    b.wait("sync", s_ap_, np_ - 2)
                sl_xp.load("sync", (np_ - 1) % 2, lambda tb=tb, k=k, np_=np_: nc.sync.dma_start(
                    out=xaP[(np_ - 1) % 2][:], in_=xin[k][:, tb * 512:(tb + 1) * 512]))

    for tb in range(NBT):
        for k in range(1, KT, 2):
            np2 = tb * 8 + (k - 1) // 2 + 1
            sl_xp.wait("gpsimd", (np2 - 1) % 2, 16 * ((np2 - 1) // 2 + 1))
            b.wait("gpsimd", s_rev, tb + 1)
            b.op("gpsimd", lambda tb=tb, k=k, np2=np2: nc.gpsimd.tensor_tensor(
                out=t1P[:], in0=xaP[(np2 - 1) % 2][:], in1=r_b[tb % 2][:], op=ALU.mult))
            b.op("gpsimd", lambda tb=tb, k=k: nc.gpsimd.tensor_tensor(
                out=xn_res[:, k * T + tb * 512: k * T + (tb + 1) * 512],
                in0=t1P[:], in1=mr_b[tb % 2][:], op=ALU.subtract), s_ap_)

    b.barrier()
    r2.close()
    r0.close()
    return xn_res


def emit_l0a_qkv(b, c, lay, xn_res, phase_reg):
    nc = b.nc
    r = b.region()
    wfp = [b.sb(r, f"a2wf{i}", [P, KT * P], F32) for i in range(2)]
    wb = [b.sb(r, f"a2wb{i}", [P, KT * P], BF16) for i in range(2)]
    qtmp = [b.sb(r, f"a2qt{i}", [P, 512], BF16) for i in range(2)]
    rt1 = b.sb(r, "a2rt1", [P, 512], F32)
    rt2 = b.sb(r, "a2rt2", [P, 512], F32)
    orow = b.sb(r, "a2orow", [P, T], BF16)
    cos2 = b.sb(r, "a2cos2", [P, T], F32)
    sin2 = b.sb(r, "a2sin2", [P, T], F32)
    swp = b.sb(r, "a2swp", [P, P], BF16)
    ps = [b.ps(r, f"a2ps{i}", [P, 512]) for i in range(6)]
    psw = [b.ps(r, f"a2psw{i}", [P, 512]) for i in range(2)]

    sl_wl = Slots(b, "a2wl", 2)
    sl_out = Slots(b, "a2out", NBT, persistent=True)
    s_wc = b.sem("a2wc")
    s_pm = b.sem("a2pm")
    s_sw = b.sem("a2sw")
    s_ev = b.sem("a2ev")
    s_evd = b.sem("a2evd")
    s_rotq = b.sem("a2rotq")
    s_rotv = b.sem("a2rotv")
    s_cst = b.sem("a2cst")

    mk = b.dout if DBG == "qkv" else b.dscratch
    q_sc = mk("q_sc", [NC0, P, H * P], BF16)
    k_sc = mk("k_sc", [NC0, P, H * P], BF16)
    v_sc = mk("v_sc", [NC0, P, H * P], BF16)
    mats = [(b.din("l0_wq", lay["l0_wq"]), q_sc, True),
            (b.din("l0_wk", lay["l0_wk"]), k_sc, True),
            (b.din("l0_wv", lay["l0_wv"]), v_sc, False)]

    b.op("sync", lambda: nc.sync.dma_start(out=cos2[:], in_=b.din("cos2", c["cos2"])), s_cst, 16)
    b.op("sync", lambda: nc.sync.dma_start(out=sin2[:], in_=b.din("sin2pm", c["sin2pm"])), s_cst, 16)
    b.op("sync", lambda: nc.sync.dma_start(out=swp[:], in_=b.din("swapmat", c["swapmat"])), s_cst, 16)

    for jj in range(3 * H):
        if jj >= 2:
            b.wait("sync", s_wc, jj - 1)
        wa, _, _ = mats[jj // H]
        sl_wl.load("sync", jj % 2, lambda jj=jj, wa=wa: nc.sync.dma_start(
            out=wfp[jj % 2][:], in_=wa[jj % H]))

    b.wait("vector", s_cst, 48)
    for jj in range(3 * H):
        rot = mats[jj // H][2]
        sl_wl.wait("vector", jj % 2, 16 * (jj // 2 + 1))
        if jj >= 2:
            b.wait("vector", s_pm, (jj - 1) * 8)
        b.op("vector", lambda jj=jj: nc.vector.tensor_copy(out=wb[jj % 2][:], in_=wfp[jj % 2][:]))
        b.pub("vector", s_wc)
        if rot:
            for tb in range(NBT):
                g = jj * 8 + tb
                b.wait("vector", s_evd, g + 1)
                b.wait("vector", s_sw, g + 1)
                if jj >= 1:
                    sl_out.wait("vector", tb, 16 * jj)
                sl = (tb * 512, (tb + 1) * 512)
                b.op("vector", lambda g=g, sl=sl: nc.vector.tensor_tensor(
                    out=rt1[:], in0=qtmp[g % 2][:], in1=cos2[:, sl[0]:sl[1]], op=ALU.mult))
                b.op("vector", lambda g=g, sl=sl: nc.vector.tensor_tensor(
                    out=rt2[:], in0=psw[g % 2].ap()[:], in1=sin2[:, sl[0]:sl[1]], op=ALU.mult))
                b.op("vector", lambda sl=sl: nc.vector.tensor_tensor(
                    out=orow[:, sl[0]:sl[1]], in0=rt1[:], in1=rt2[:], op=ALU.add))
                b.pub("vector", s_rotq)

    for jj in range(3 * H):
        rot = mats[jj // H][2]
        for tb in range(NBT):
            g = jj * 8 + tb
            b.wait("scalar", s_pm, g + 1)
            if rot:
                if g >= 2:
                    b.wait("scalar", s_rotq, g - 1)
                    b.wait("scalar", s_sw, g - 1)
                b.op("scalar", lambda g=g: nc.scalar.copy(
                    out=qtmp[g % 2][:], in_=ps[g % 6].ap()[:]), s_ev)
                b.pub("scalar", s_evd)
            else:
                if jj >= 1:
                    sl_out.wait("scalar", tb, 16 * jj)
                b.op("scalar", lambda g=g, tb=tb: nc.scalar.copy(
                    out=orow[:, tb * 512:(tb + 1) * 512], in_=ps[g % 6].ap()[:]), s_ev)
                b.pub("scalar", s_rotv)

    b.wait("tensor", s_cst, 48)
    pend_swaps = []
    for jj in range(3 * H):
        rot = mats[jj // H][2]
        b.wait("tensor", s_wc, jj + 1)
        for tb in range(NBT):
            g = jj * 8 + tb
            if g >= 6:
                b.wait("tensor", s_ev, g - 5)
            for k in range(KT):
                st, sp = (k == 0), (k == KT - 1)
                b.op("tensor", lambda jj=jj, tb=tb, k=k, g=g, st=st, sp=sp: nc.tensor.matmul(
                    out=ps[g % 6].ap()[:], lhsT=wb[jj % 2][:, k * P:(k + 1) * P],
                    rhs=xn_res[:, k * T + tb * 512:k * T + (tb + 1) * 512],
                    start=st, stop=sp), s_pm if sp else None)
            if rot:
                pend_swaps.append(g)
            while pend_swaps and (pend_swaps[0] <= g - 2 or tb == NBT - 1):
                gs = pend_swaps.pop(0)
                b.wait("tensor", s_evd, gs + 1)
                if gs >= 2:
                    b.wait("tensor", s_rotq, gs - 1)
                b.op("tensor", lambda gs=gs: nc.tensor.matmul(
                    out=psw[gs % 2].ap()[:], lhsT=swp[:], rhs=qtmp[gs % 2][:],
                    start=True, stop=True), s_sw)
    assert not pend_swaps

    for jj in range(3 * H):
        _, dst, rot = mats[jj // H]
        j = jj % H
        for tb in range(NBT):
            g = jj * 8 + tb
            if rot:
                b.wait("gpsimd", s_rotq, g + 1)
            else:
                b.wait("gpsimd", s_rotv, (jj - 2 * H) * 8 + tb + 1)
            sl_out.load("gpsimd", tb, lambda dst=dst, j=j, tb=tb: nc.gpsimd.dma_start(
                out=dst[tb * 4:(tb + 1) * 4, :, j * P:(j + 1) * P].rearrange("c p m -> p c m"),
                in_=orow.ap()[:, tb * 512:(tb + 1) * 512].rearrange("p (c m) -> p c m", c=4)))
    for tb in range(NBT):
        sl_out.wait("gpsimd", tb, 16 * 3 * H)

    b.barrier()
    r.close()


def emit_l0a_scan(b, c, lay, xin, xout):
    nc = b.nc
    r = b.region()
    wob = b.sb(r, "a3wob", [P, H * KT * P], BF16)
    kv = b.sb(r, "a3kv", [P, H * P], F32)
    dm_sb = b.sb(r, "a3dm", [P, H * P], F32)
    qdb_sb = b.sb(r, "a3qdb", [P, H * P], F32)
    kdc_sb = b.sb(r, "a3kdc", [P, H], F32)
    rc_sb = b.sb(r, "a3rc", [P, H], F32)
    rmsw_sb = b.sb(r, "a3rmsw", [P, KT], F32)
    ids = b.sb(r, "a3ids", [P, P], BF16)
    ones_sb = b.sb(r, "a3ones", [P, 1], BF16)
    oner_sb = b.sb(r, "a3oner", [1, P], F32)
    cp_sb = b.sb(r, "a3cp", [P, 8], F32)
    wfp = [b.sb(r, "a3wf0", [P, KT * P], F32)]
    qch = [b.sb(r, f"a3q{i}", [P, H * P], BF16) for i in range(2)]
    kch = [b.sb(r, f"a3k{i}", [P, H * P], BF16) for i in range(2)]
    vch = [b.sb(r, f"a3v{i}", [P, H * P], BF16) for i in range(2)]
    ktd = b.sb(r, "a3ktd", [P, P], BF16)
    vtok = b.sb(r, "a3vtok", [P, P], BF16)
    scm = b.sb(r, "a3scm", [P, P], BF16)
    qd = b.sb(r, "a3qd", [P, P], F32)
    dkv = b.sb(r, "a3dkv", [P, P], F32)
    afm = b.sb(r, "a3afm", [P, H * P], F32)
    asq = b.sb(r, "a3asq", [P, P], BF16)
    rowb = b.sb(r, "a3rowb", [1, P], F32)
    rb_c = b.sb(r, "a3rbc", [P, P], F32)
    grbuf = b.sb(r, "a3gr", [P, H, 512], BF16)
    ywo = b.sb(r, "a3ywo", [P, KT, 512], F32)
    yb = [b.sb(r, f"a3yb{i}", [P, 512], BF16) for i in range(2)]
    ysq = [b.sb(r, f"a3ysq{i}", [P, 512], BF16) for i in range(2)]
    row2 = b.sb(r, "a3row2", [1, 2 * 512], F32)
    rowt = b.sb(r, "a3rowt", [1, 2 * 512], F32)
    rb2 = b.sb(r, "a3rb2", [P, 512], F32)
    mrb2 = b.sb(r, "a3mrb2", [P, 512], F32)
    xoldV = [b.sb(r, f"a3xoV{i}", [P, 512], F32) for i in range(2)]
    xoldP = [b.sb(r, f"a3xoP{i}", [P, 512], F32) for i in range(2)]
    ps_tk = b.ps(r, "a3ptk", [P, 1024], BF16)
    ps_tv = b.ps(r, "a3ptv", [P, 1024], BF16)
    ps_sc = b.ps(r, "a3psc", [P, 512], F32)
    ps_o = b.ps(r, "a3po", [P, 512], F32)
    ps_kv = b.ps(r, "a3pkv", [P, 512], F32)
    ps_rm = b.ps(r, "a3prm", [P, 512], F32)
    ps_wo = [b.ps(r, f"a3pw{i}", [P, 512], F32) for i in range(2)]

    q_sc, k_sc, v_sc = b._dram["q_sc"], b._dram["k_sc"], b._dram["v_sc"]

    s_ldc = b.sem("a3ldc")
    sl_wol = Slots(b, "a3wol", 1)
    s_wc = b.sem("a3wc")
    sl_ch = Slots(b, "a3ch", 2)
    s_t1 = b.sem("a3t1")
    s_t2 = b.sem("a3t2")
    s_m1 = b.sem("a3m1")
    s_o = b.sem("a3o")
    s_m5 = b.sem("a3m5")
    s_m6 = b.sem("a3m6")
    s_vt = b.sem("a3vt")
    s_dkv = b.sem("a3dkv")
    s_aq = b.sem("a3aq")
    s_scm = b.sem("a3scm")
    s_qd = b.sem("a3qd")
    s_ktd = b.sem("a3ktd")
    s_kv = b.sem("a3kv")
    s_rr = b.sem("a3rr")
    s_rp = b.sem("a3rp")
    s_rb = b.sem("a3rb")
    s_gr = b.sem("a3gr")
    s_wo = b.sem("a3wo")
    s_ye = b.sem("a3ye")
    s_cb = b.sem("a3cb")
    s_sq2 = b.sem("a3sq2")
    s_plm = b.sem("a3plm")
    s_r2 = b.sem("a3r2")
    s_rp2 = b.sem("a3rp2")
    s_rbe = b.sem("a3rbe")
    sl_xoV = Slots(b, "a3xoV", 2)
    sl_xoP = Slots(b, "a3xoP", 2)
    sl_xwV = Slots(b, "a3xwV", 2)
    sl_xwP = Slots(b, "a3xwP", 2)
    s_av = b.sem("a3av")
    s_ap = b.sem("a3ap")

    b.op("sync", lambda: nc.sync.dma_start(out=ids[:], in_=b.din("ident", c["ident"])), s_ldc, 16)
    b.op("sync", lambda: nc.sync.dma_start(out=ones_sb[:], in_=b.din("ones_bf", c["ones_bf"])), s_ldc, 16)
    b.op("sync", lambda: nc.sync.dma_start(out=oner_sb[:], in_=b.din("ones_row_f32", c["ones_row_f32"])), s_ldc, 16)
    b.op("sync", lambda: nc.sync.dma_start(out=cp_sb[:], in_=b.din("cpack", c["cpack"])), s_ldc, 16)
    b.op("sync", lambda: nc.sync.dma_start(
        out=dm_sb.ap().rearrange("p (h m) -> p h m", h=H),
        in_=b.din("dmaskT", c["dmaskT"]).rearrange("h p m -> p h m")), s_ldc, 16)
    b.op("sync", lambda: nc.sync.dma_start(
        out=qdb_sb.ap().rearrange("p (h m) -> p h m", h=H),
        in_=b.din("qdec_b", c["qdec_b"]).rearrange("h p m -> p h m")), s_ldc, 16)
    b.op("sync", lambda: nc.sync.dma_start(
        out=kdc_sb[:], in_=b.din("kdec_pc", c["kdec_pc"])), s_ldc, 16)
    b.op("sync", lambda: nc.sync.dma_start(
        out=rc_sb[:], in_=b.din("rC_pc", c["rC_pc"])), s_ldc, 16)
    b.op("sync", lambda: nc.sync.dma_start(
        out=rmsw_sb[:], in_=b.din("l0_rmsw", lay["l0_rmsw"])), s_ldc, 16)
    wo_d = b.din("l0_wo", lay["l0_wo"])
    for j in range(H):
        if j >= 1:
            b.wait("sync", s_wc, j)
        sl_wol.load("sync", 0, lambda j=j: nc.sync.dma_start(out=wfp[0][:], in_=wo_d[j]))
    b.wait("vector", s_ldc, 16 * 9)
    for j in range(H):
        sl_wol.wait("vector", 0, 16 * (j + 1))
        for k in range(KT):
            b.op("vector", lambda j=j, k=k: nc.vector.tensor_scalar(
                out=wob[:, (j * KT + k) * P:(j * KT + k + 1) * P],
                in0=wfp[0][:, k * P:(k + 1) * P],
                scalar1=rmsw_sb[:, k:k + 1], scalar2=None, op0=ALU.mult))
        b.pub("vector", s_wc)
    b.op("gpsimd", lambda: nc.gpsimd.memset(kv[:], 0.0))
    b.pub("gpsimd", s_kv)

    for ci in range(NC0):
        if ci >= 2:
            b.wait("sync", s_m5, (ci - 1) * H)
            b.wait("sync", s_ktd, (ci - 1) * H)
        sl_ch.load("sync", ci % 2, lambda ci=ci: nc.sync.dma_start(out=qch[ci % 2][:], in_=q_sc[ci]))
        sl_ch.uses[ci % 2] -= 1
        sl_ch.load("sync", ci % 2, lambda ci=ci: nc.sync.dma_start(out=kch[ci % 2][:], in_=k_sc[ci]))
        sl_ch.uses[ci % 2] -= 1
        sl_ch.load("sync", ci % 2, lambda ci=ci: nc.sync.dma_start(out=vch[ci % 2][:], in_=v_sc[ci]))
        if ci % 4 == 3:
            g = ci // 4

            def _ld(j, g=g):
                if j % 2 == 0:
                    nb = g * 8 + j // 2
                    if nb >= 2:
                        sl_xwV.wait("sync", nb % 2, 16 * (nb // 2))
                    sl_xoV.load("sync", nb % 2, lambda g=g, j=j, nb=nb: nc.sync.dma_start(
                        out=xoldV[nb % 2][:], in_=xin[j][:, g * 512:(g + 1) * 512]))
                else:
                    nb = g * 8 + (j - 1) // 2
                    if nb >= 2:
                        sl_xwP.wait("sync", nb % 2, 16 * (nb // 2))
                    sl_xoP.load("sync", nb % 2, lambda g=g, j=j, nb=nb: nc.sync.dma_start(
                        out=xoldP[nb % 2][:], in_=xin[j][:, g * 512:(g + 1) * 512]))

            def _st(j, g=g):
                if j % 2 == 0:
                    nb = g * 8 + j // 2
                    b.wait("sync", s_av, nb + 1)
                    sl_xwV.load("sync", nb % 2, lambda g=g, j=j, nb=nb: nc.sync.dma_start(
                        out=xout[j][:, g * 512:(g + 1) * 512], in_=xoldV[nb % 2][:]))
                else:
                    nb = g * 8 + (j - 1) // 2
                    b.wait("sync", s_ap, nb + 1)
                    sl_xwP.load("sync", nb % 2, lambda g=g, j=j, nb=nb: nc.sync.dma_start(
                        out=xout[j][:, g * 512:(g + 1) * 512], in_=xoldP[nb % 2][:]))

            for j in range(KT):
                if j >= 4:
                    _st(j - 4)
                _ld(j)
            for j in range(KT - 4, KT):
                _st(j)

    def hs(h):
        return slice(h * P, (h + 1) * P)

    b.wait("tensor", s_ldc, 16 * 9)
    b.wait("tensor", s_wc, H)
    for ci in range(NC0):
        g, cg = ci // 4, ci % 4
        sl_ch.wait("tensor", ci % 2, 48 * (ci // 2 + 1))
        if cg == 0 and ci >= 4:
            b.wait("tensor", s_rbe, ci // 4)
            b.wait("tensor", s_r2, 3 * (ci // 4))
        for h in range(H):
            n = ci * H + h
            if n >= 1:
                b.wait("tensor", s_ktd, n)
                b.wait("tensor", s_vt, n)
            b.op("tensor", lambda ci=ci, h=h: nc.tensor.transpose(
                out=ps_tk.ap()[:, 0:P], in_=kch[ci % 2][:, hs(h)], identity=ids[:]), s_t1)
            b.op("tensor", lambda ci=ci, h=h: nc.tensor.transpose(
                out=ps_tv.ap()[:, 0:P], in_=vch[ci % 2][:, hs(h)], identity=ids[:]), s_t2)
            if n >= 1:
                b.wait("tensor", s_ktd, n)
            b.op("tensor", lambda ci=ci, h=h: nc.tensor.matmul(
                out=ps_sc.ap()[:, 0:P], lhsT=kch[ci % 2][:, hs(h)],
                rhs=qch[ci % 2][:, hs(h)], start=True, stop=True), s_m1)
            b.wait("tensor", s_kv, (n + 2 - H) if ci >= 1 else 1)
            b.wait("tensor", s_ktd, n + 1)
            if n >= 1:
                b.wait("tensor", s_aq, n)
            b.op("tensor", lambda h=h: nc.tensor.matmul(
                out=ps_o.ap()[:, 0:P], lhsT=kv[:, hs(h)], rhs=qd[:], start=True, stop=False))
            b.wait("tensor", s_vt, n + 1)
            b.op("tensor", lambda: nc.tensor.matmul(
                out=ps_o.ap()[:, 0:P], lhsT=vtok[:], rhs=scm[:], start=False, stop=True), s_o)
            if n >= 1:
                b.wait("tensor", s_kv, n + 1)
            b.op("tensor", lambda: nc.tensor.matmul(
                out=ps_kv.ap()[:, 0:P], lhsT=ktd[:], rhs=vtok[:], start=True, stop=True), s_m5)
            b.wait("tensor", s_aq, n + 1)
            if h == 0 and ci >= 1:
                b.wait("tensor", s_rp, ci)
                b.wait("tensor", s_rb, ci)
            b.op("tensor", lambda h=h: nc.tensor.matmul(
                out=ps_rm.ap()[0:1, 0:P], lhsT=ones_sb[:], rhs=asq[:],
                start=(h == 0), stop=(h == H - 1)), s_m6)
        b.wait("tensor", s_rr, ci + 1)
        b.op("tensor", lambda: nc.tensor.matmul(
            out=ps_rm.ap()[:, P:2 * P], lhsT=oner_sb[:], rhs=rowb[0:1, :],
            start=True, stop=True), s_rp)
        if cg == 3:
            b.wait("tensor", s_gr, (ci + 1) * H)
            for j in range(KT):
                if g >= 1 or j >= 2:
                    b.wait("tensor", s_ye, g * KT + j - 1)
                for h in range(H):
                    b.op("tensor", lambda j=j, h=h, g=g: nc.tensor.matmul(
                        out=ps_wo[j % 2].ap()[:], lhsT=wob[:, (j * KT + h) * P:(j * KT + h + 1) * P],
                        rhs=grbuf.ap()[:, h, :], start=(h == 0), stop=(h == H - 1)),
                        s_wo if h == H - 1 else None, inc=H if h == H - 1 else 1)
            for j in range(KT):
                b.wait("tensor", s_cb, g * KT + j + 1)
                b.wait("tensor", s_sq2, g * KT + j + 1)
                b.op("tensor", lambda j=j: nc.tensor.matmul(
                    out=ps_sc.ap()[0:1, 0:512], lhsT=ones_sb[:], rhs=yb[j % 2][:],
                    start=(j == 0), stop=(j == KT - 1)))
                b.op("tensor", lambda j=j: nc.tensor.matmul(
                    out=ps_sc.ap()[64:65, 0:512], lhsT=ones_sb[:], rhs=ysq[j % 2][:],
                    start=(j == 0), stop=(j == KT - 1)), s_plm)
            b.wait("tensor", s_aq, (ci + 1) * H)
            b.wait("tensor", s_kv, (ci + 1) * H + 1)
            b.wait("tensor", s_r2, 3 * g + 3)
            if g >= 1:
                b.wait("tensor", s_rbe, g)
            b.op("tensor", lambda: nc.tensor.matmul(
                out=ps_o.ap()[:], lhsT=oner_sb[:], rhs=row2[0:1, 0:512], start=True, stop=True))
            b.op("tensor", lambda: nc.tensor.matmul(
                out=ps_kv.ap()[:], lhsT=oner_sb[:], rhs=row2[0:1, 512:1024], start=True, stop=True), s_rp2, inc=2)

    b.wait("vector", s_ldc, 16 * 9)
    for ci in range(NC0):
        g, cg = ci // 4, ci % 4
        sl_ch.wait("vector", ci % 2, 48 * (ci // 2 + 1))
        for h in range(H):
            n = ci * H + h
            b.wait("vector", s_t1, n + 1)
            b.wait("vector", s_m5, n)
            b.op("vector", lambda h=h: nc.vector.tensor_scalar(
                out=ktd[:], in0=ps_tk.ap()[:, 0:P], scalar1=kdc_sb[:, h:h + 1],
                scalar2=None, op0=ALU.mult))
            b.wait("vector", s_m1, n + 1)
            b.wait("vector", s_o, n)
            b.op("vector", lambda h=h: nc.vector.tensor_tensor(
                out=scm[:], in0=ps_sc.ap()[:, 0:P], in1=dm_sb[:, hs(h)], op=ALU.mult))
            b.op("vector", lambda ci=ci, h=h: nc.vector.tensor_tensor(
                out=qd[:], in0=qch[ci % 2][:, hs(h)], in1=qdb_sb[:, hs(h)], op=ALU.mult))
            b.pub("vector", s_ktd)
            b.wait("vector", s_m5, n + 1)
            b.op("vector", lambda h=h: nc.vector.scalar_tensor_tensor(
                out=kv[:, hs(h)], in0=kv[:, hs(h)], scalar=rc_sb[:, h:h + 1],
                in1=ps_kv.ap()[:, 0:P], op0=ALU.mult, op1=ALU.add))
            b.pub("vector", s_kv)
        b.wait("vector", s_rb, ci + 1)
        b.wait("vector", s_aq, (ci + 1) * H)
        if ci >= 4:
            b.wait("vector", s_wo, (ci // 4) * KT * H)
        for h in range(H):
            b.op("vector", lambda h=h, cg=cg: nc.vector.tensor_tensor(
                out=grbuf.ap()[:, h, cg * P:(cg + 1) * P], in0=afm[:, hs(h)], in1=rb_c[:], op=ALU.mult))
        b.pub("vector", s_gr, inc=H)
        if cg == 3:
            for j in range(KT):
                b.wait("vector", s_ye, g * KT + j + 1)
                if g * KT + j >= 2:
                    b.wait("vector", s_plm, g * KT + j - 1)
                b.op("vector", lambda j=j: nc.vector.tensor_copy(out=yb[j % 2][:], in_=ywo.ap()[:, j, :]))
                b.pub("vector", s_cb)
            b.wait("vector", s_plm, (g + 1) * KT)
            b.op("vector", lambda: nc.vector.tensor_scalar(
                out=rowt[0:1, 0:512], in0=ps_sc.ap()[0:1, 0:512], scalar1=1.0 / D, scalar2=None, op0=ALU.mult))
            b.op("vector", lambda: nc.vector.tensor_scalar(
                out=rowt[0:1, 512:1024], in0=ps_sc.ap()[64:65, 0:512], scalar1=1.0 / D, scalar2=None, op0=ALU.mult))
            b.op("vector", lambda: nc.vector.tensor_tensor(
                out=row2[0:1, 512:1024], in0=rowt[0:1, 0:512], in1=rowt[0:1, 0:512], op=ALU.mult))
            b.op("vector", lambda: nc.vector.tensor_tensor(
                out=row2[0:1, 512:1024], in0=rowt[0:1, 512:1024], in1=row2[0:1, 512:1024], op=ALU.subtract))
            b.pub("vector", s_r2)
            b.wait("vector", s_r2, 3 * g + 2)
            b.op("vector", lambda: nc.vector.tensor_tensor(
                out=row2[0:1, 512:1024], in0=rowt[0:1, 0:512], in1=row2[0:1, 0:512], op=ALU.mult))
            b.pub("vector", s_r2)
            for j in range(0, KT, 2):
                nb = g * 8 + j // 2
                sl_xoV.wait("vector", nb % 2, 16 * (nb // 2 + 1))
                b.wait("vector", s_rbe, g + 1)
                if nb >= 2:
                    sl_xwV.wait("vector", nb % 2, 16 * (nb // 2))
                b.op("vector", lambda j=j: nc.vector.tensor_tensor(
                    out=ywo.ap()[:, j, :], in0=ywo.ap()[:, j, :], in1=rb2[:], op=ALU.mult))
                b.op("vector", lambda j=j: nc.vector.tensor_tensor(
                    out=ywo.ap()[:, j, :], in0=ywo.ap()[:, j, :], in1=mrb2[:], op=ALU.subtract))
                b.op("vector", lambda j=j, nb=nb: nc.vector.tensor_tensor(
                    out=xoldV[nb % 2][:], in0=ywo.ap()[:, j, :], in1=xoldV[nb % 2][:], op=ALU.add))
                b.pub("vector", s_av)

    if DBG == "scan":
        dbg_attn = b.dout("dbg_attn", [NC0, P, H * P], F32)
        s_dbga = b.sem("a3dbga")
        dbg_ywo = b.dout("dbg_ywo", [NBT, P, KT * 512], F32)
        s_dbgy = b.sem("a3dbgy")
    b.wait("scalar", s_ldc, 16 * 9)
    for ci in range(NC0):
        g, cg = ci // 4, ci % 4
        for h in range(H):
            n = ci * H + h
            b.wait("scalar", s_t2, n + 1)
            b.wait("scalar", s_m5, n)
            b.op("scalar", lambda: nc.scalar.copy(out=vtok[:], in_=ps_tv.ap()[:, 0:P]))
            b.pub("scalar", s_vt)
            b.wait("scalar", s_o, n + 1)
            if DBG == "scan" and ci >= 1 and h == 0:
                b.wait("scalar", s_dbga, 17 * ci)
            if ci >= 1:
                b.wait("scalar", s_gr, n - H + 1)
            b.wait("scalar", s_m6, n)
            b.op("scalar", lambda h=h: nc.scalar.copy(out=afm[:, hs(h)], in_=ps_o.ap()[:, 0:P]))
            b.op("scalar", lambda: nc.scalar.activation(out=asq[:], in_=ps_o.ap()[:, 0:P], func=AF.Square))
            b.pub("scalar", s_aq)
        if DBG == "scan":
            b.pub("scalar", s_dbga)
            b.op("scalar", lambda ci=ci: nc.scalar.dma_start(
                out=dbg_attn[ci], in_=afm[:]), s_dbga, 16)
        b.wait("scalar", s_m6, (ci + 1) * H)
        b.wait("scalar", s_rp, ci)
        b.op("scalar", lambda: nc.scalar.activation(
            out=rowb[0:1, :], in_=ps_rm.ap()[0:1, 0:P], func=AF.Ln, scale=1.0 / D, bias=cp_sb[0:1, 0:1]))
        b.op("scalar", lambda: nc.scalar.activation(
            out=rowb[0:1, :], in_=rowb[0:1, :], func=AF.Exp, scale=-0.5))
        b.pub("scalar", s_rr)
        b.wait("scalar", s_rp, ci + 1)
        b.wait("scalar", s_gr, ci * H)
        b.op("scalar", lambda: nc.scalar.copy(out=rb_c[:], in_=ps_rm.ap()[:, P:2 * P]))
        b.pub("scalar", s_rb)
        if cg == 3:
            for j in range(KT):
                b.wait("scalar", s_wo, g * KT * H + (j + 1) * H)
                if g >= 1:
                    b.wait("scalar", s_av, g * 8)
                    b.wait("scalar", s_ap, g * 8)
                b.op("scalar", lambda j=j: nc.scalar.copy(
                    out=ywo.ap()[:, j, :], in_=ps_wo[j % 2].ap()[:]))
                b.pub("scalar", s_ye)
            for j in range(KT):
                if g * KT + j >= 2:
                    b.wait("scalar", s_plm, g * KT + j - 1)
                b.op("scalar", lambda j=j: nc.scalar.activation(
                    out=ysq[j % 2][:], in_=ywo.ap()[:, j, :], func=AF.Square))
                b.pub("scalar", s_sq2)
            if DBG == "scan":
                b.pub("scalar", s_dbgy)
                b.op("scalar", lambda g=g: nc.scalar.dma_start(
                    out=dbg_ywo[g], in_=ywo.ap().rearrange("p j m -> p (j m)")), s_dbgy, 16)
            b.wait("scalar", s_r2, 3 * g + 1)
            b.op("scalar", lambda: nc.scalar.activation(
                out=row2[0:1, 0:512], in_=row2[0:1, 512:1024], func=AF.Ln, bias=cp_sb[0:1, 0:1]))
            b.op("scalar", lambda: nc.scalar.activation(
                out=row2[0:1, 0:512], in_=row2[0:1, 0:512], func=AF.Exp, scale=-0.5))
            b.pub("scalar", s_r2)
            b.wait("scalar", s_rp2, 2 * (g + 1))
            b.op("scalar", lambda: nc.scalar.copy(out=rb2[:], in_=ps_o.ap()[:]))
            b.op("scalar", lambda: nc.scalar.copy(out=mrb2[:], in_=ps_kv.ap()[:]))
            b.pub("scalar", s_rbe)

    if DBG == "scan":
        b.wait("scalar", s_dbga, 17 * NC0)
        b.wait("scalar", s_dbgy, 17 * NBT)

    for ci in range(NC0):
        g, cg = ci // 4, ci % 4
        if cg == 3:
            for j in range(1, KT, 2):
                nb = g * 8 + (j - 1) // 2
                sl_xoP.wait("gpsimd", nb % 2, 16 * (nb // 2 + 1))
                b.wait("gpsimd", s_rbe, g + 1)
                if nb >= 2:
                    sl_xwP.wait("gpsimd", nb % 2, 16 * (nb // 2))
                b.op("gpsimd", lambda j=j: nc.gpsimd.tensor_tensor(
                    out=ywo.ap()[:, j, :], in0=ywo.ap()[:, j, :], in1=rb2[:], op=ALU.mult))
                b.op("gpsimd", lambda j=j: nc.gpsimd.tensor_tensor(
                    out=ywo.ap()[:, j, :], in0=ywo.ap()[:, j, :], in1=mrb2[:], op=ALU.subtract))
                b.op("gpsimd", lambda j=j, nb=nb: nc.gpsimd.tensor_tensor(
                    out=xoldP[nb % 2][:], in0=ywo.ap()[:, j, :], in1=xoldP[nb % 2][:], op=ALU.add))
                b.pub("gpsimd", s_ap)

    for i in range(2):
        sl_xwV.wait("sync", i, 16 * (NBT * KT // 4))
        sl_xwP.wait("sync", i, 16 * (NBT * KT // 4))
    b.barrier()
    r.close()



DBG = None


def emit_pregather(b, c, lay):
    nc = b.nc
    r = b.region()
    lut_sb = b.sb(r, "pglut", [P, 16], F32)
    s_lut = b.sem("pglut")
    b.op("sync", lambda: nc.sync.dma_start(out=lut_sb[:], in_=b.din("l0_lut", lay["l0_lut"])), s_lut, 16)
    wg_sc = b.dscratch("wg_sc", [KT, P, I0], BF16)
    wu_sc = b.dscratch("wu_sc", [KT, P, I0], BF16)
    wd_sc = b.dscratch("wd_sc", [48, P, D], BF16)
    ctb_g = b.dscratch("ctb_g", [2, P, I0], BF16)
    ctb_u = b.dscratch("ctb_u", [2, P, I0], BF16)
    ctb_d = b.dscratch("ctb_d", [6, P, D], BF16)
    gi_d = b.din("l0_giT_c", lay["l0_giT_c"])
    ui_d = b.din("l0_uiT_c", lay["l0_uiT_c"])
    di_d = b.din("l0_diT_c", lay["l0_diT_c"])
    units = ([(gi_d[k], ctb_g[k], I0) for k in range(2)]
             + [(ui_d[k], ctb_u[k], I0) for k in range(2)]
             + [(di_d[k], ctb_d[k], D) for k in range(6)])
    nu = len(units)
    idx = b.sb(r, "pgi", [P, I0], I32)
    idxf = b.sb(r, "pgf", [P, I0], BF16)
    msk = b.sb(r, "pgm", [P, I0], BF16)
    acc = [b.sb(r, f"pga{i}", [P, I0], BF16) for i in range(2)]
    sl_i = Slots(b, "pgi", 1)
    s_d = b.sem("pgd")
    sl_o = Slots(b, "pgo", 2)
    for k, (src, dst, W) in enumerate(units):
        if k >= 1:
            b.wait("sync", s_d, k)
        sl_i.load("sync", 0, lambda src=src, W=W: nc.sync.dma_start(out=idx[:, :W], in_=src))
    for k, (src, dst, W) in enumerate(units):
        sl_i.wait("vector", 0, 16 * (k + 1))
        if k == 0:
            b.wait("vector", s_lut, 16)
        if k >= 2:
            sl_o.wait("vector", k % 2, 16 * (k // 2))
        b.op("vector", lambda W=W: nc.vector.tensor_copy(out=idxf[:, :W], in_=idx[:, :W]))
        for j in range(16):
            if j == 0:
                b.op("vector", lambda k=k, j=j, W=W: nc.vector.tensor_scalar(
                    out=acc[k % 2][:, :W], in0=idxf[:, :W], scalar1=float(j), scalar2=lut_sb[:, j:j + 1],
                    op0=ALU.is_equal, op1=ALU.mult))
            else:
                b.op("vector", lambda j=j, W=W: nc.vector.tensor_scalar(
                    out=msk[:, :W], in0=idxf[:, :W], scalar1=float(j), scalar2=lut_sb[:, j:j + 1],
                    op0=ALU.is_equal, op1=ALU.mult))
                b.op("vector", lambda k=k, W=W: nc.vector.tensor_tensor(
                    out=acc[k % 2][:, :W], in0=acc[k % 2][:, :W], in1=msk[:, :W], op=ALU.add))
        b.pub("vector", s_d)
    for k, (src, dst, W) in enumerate(units):
        b.wait("scalar", s_d, k + 1)
        sl_o.load("scalar", k % 2, lambda dst=dst, k=k, W=W: nc.scalar.dma_start(
            out=dst, in_=acc[k % 2][:, :W]))
    for i in range(2):
        sl_o.wait("scalar", i, 16 * ((nu + 1 - i) // 2))
        sl_o.wait("gpsimd", i, 16 * ((nu + 1 - i) // 2))
    ccs = [b.sem("ccg", persistent=True), b.sem("ccu", persistent=True),
           b.sem("ccd", persistent=True)]
    for cc_s, ctb, out_sc in ((ccs[0], ctb_g, wg_sc), (ccs[1], ctb_u, wu_sc),
                              (ccs[2], ctb_d, wd_sc)):
        b.op("gpsimd", lambda ctb=ctb, out_sc=out_sc: nc.gpsimd.collective_compute(
            "AllGather", ALU.bypass, replica_groups=[list(range(N_CORES))],
            ins=[ctb.rearrange("b p i -> (b p) i").opt()],
            outs=[out_sc.rearrange("b p i -> (b p) i").opt()]), cc_s, 1)
    b.cc_sems = ccs
    return r


def ph_l0a(b, c, lay, xin, xout):
    nc = b.nc
    rg = emit_pregather(b, c, lay)
    b.barrier()
    rg.close()
    pr = b.region()
    xn = emit_preln(b, "a1", xin, c, pr)
    if DBG == "preln":
        dbg = b.dout("dbg_xn", [KT, P, T], BF16)
        sd = b.sem("dbgs")
        for k in range(KT):
            b.op("sync", lambda k=k: nc.sync.dma_start(
                out=dbg[k], in_=xn[:, k * T:(k + 1) * T]), sd, 16)
        b.wait("sync", sd, 16 * KT)
        pr.close()
        return
    emit_l0a_qkv(b, c, lay, xn, pr)
    pr.close()
    if DBG == "qkv":
        return
    emit_l0a_scan(b, c, lay, xin, xout)


PHASES = {"l0a": ph_l0a}
GROUPS = [["l0a", "l0f", "l1a", "l1f"]]
STATE_NAMES = {"l0a": "x1", "l0f": "x2", "l1a": "x3", "l1f": "x4"}


def _register_phases():
    PHASES["l0f"] = ph_l0f
    for nm, fn in list(globals().items()):
        if nm.startswith("ph_") and callable(fn):
            PHASES[nm[3:]] = fn


def build_group(group, c, lay, xin_arrs):
    _register_phases()
    """group: list of phase names. xin_arrs: per-core list of [KT,P,T] f32."""
    b = Build()
    xcur = b.din("xin_" + group[0], xin_arrs)
    for i, ph in enumerate(group):
        last = i == len(group) - 1
        nm = STATE_NAMES[ph]
        if last:
            xnext = b.dout(nm, [KT, P, T], F32)
        else:
            xnext = b.dscratch(nm, [KT, P, T], F32)
        PHASES[ph](b, c, lay, xcur, xnext)
        if not last:
            b.phase_sems_reset()
        xcur = xnext
    b.finalize()
    return b, STATE_NAMES[group[-1]]


def kernel(**inputs):
    c = mk_consts()
    lay = mk_layouts(inputs)
    state = lay["x0"]
    for group in GROUPS:
        b, outname = build_group(group, c, lay, state)
        in_maps = []
        for core in range(N_CORES):
            m = {}
            for name, arr in b.host.items():
                m[name] = arr[core] if isinstance(arr, list) else arr
            in_maps.append(m)
        res = run_bass_kernel_spmd(b.nc, in_maps, core_ids=list(range(N_CORES)))
        state = [np.ascontiguousarray(res.results[core][outname]) for core in range(N_CORES)]
    out = np.empty((N_CORES, T, D), dtype=np.float32)
    for core in range(N_CORES):
        out[core] = state[core].reshape(D, T).T
    return out



def emit_postln_group(b, tag, E, g, gt, ysrc, xin, xout, bufs, nofj=KT, GW=512, yready=None):
    nc = b.nc
    ones_sb, oner_sb, cp_sb = bufs["ones"], bufs["oner"], bufs["cp"]
    yb, ysq, row2, rowt = bufs["yb"], bufs["ysq"], bufs["row2"], bufs["rowt"]
    rb2, mrb2, xoldV, xoldP = bufs["rb2"], bufs["mrb2"], bufs["xoldV"], bufs["xoldP"]
    pl_ps, rp_a, rp_b = bufs["pl_ps"], bufs["rp_a"], bufs["rp_b"]
    s_cb, s_sq2, s_plm = E["cb"], E["sq2"], E["plm"]
    s_r2, s_rp2, s_rbe = E["r2"], E["rp2"], E["rbe"]
    s_av, s_ap = E["av"], E["ap"]
    sl_xoV, sl_xoP, sl_xwV, sl_xwP = E["sl_xoV"], E["sl_xoP"], E["sl_xwV"], E["sl_xwP"]
    KTJ = nofj

    def t_part():
        for j in range(KTJ):
            b.wait("tensor", s_cb, g * KTJ + j + 1)
            b.wait("tensor", s_sq2, g * KTJ + j + 1)
            if j == 0 and g >= 1:
                b.wait("tensor", s_rp2, 2 * g)
                b.wait("tensor", s_r2, 3 * g)
            b.op("tensor", lambda j=j: nc.tensor.matmul(
                out=pl_ps.ap()[0:1, 0:GW], lhsT=ones_sb[:], rhs=yb[j % 2][:, 0:GW],
                start=(j == 0), stop=(j == KTJ - 1)))
            b.op("tensor", lambda j=j: nc.tensor.matmul(
                out=pl_ps.ap()[64:65, 0:GW], lhsT=ones_sb[:], rhs=ysq[j % 2][:, 0:GW],
                start=(j == 0), stop=(j == KTJ - 1)), s_plm)
        b.wait("tensor", s_r2, 3 * (g + 1))
        if g >= 1:
            b.wait("tensor", s_rbe, g)
        b.op("tensor", lambda: nc.tensor.matmul(
            out=rp_a.ap()[:, 0:GW], lhsT=oner_sb[:], rhs=row2[0:1, 0:GW], start=True, stop=True))
        b.op("tensor", lambda: nc.tensor.matmul(
            out=rp_b.ap()[:, 0:GW], lhsT=oner_sb[:], rhs=row2[0:1, GW:2 * GW], start=True, stop=True), s_rp2, inc=2)

    def v_part():
        for j in range(KTJ):
            if g * KTJ + j >= 2:
                b.wait("vector", s_plm, g * KTJ + j - 1)
            if yready is not None:
                yready(j)
            b.op("vector", lambda j=j: nc.vector.tensor_copy(out=yb[j % 2][:, 0:GW], in_=ysrc(j)))
            b.pub("vector", s_cb)
        b.wait("vector", s_plm, (g + 1) * KTJ)
        b.op("vector", lambda: nc.vector.tensor_scalar(
            out=rowt[0:1, 0:GW], in0=pl_ps.ap()[0:1, 0:GW], scalar1=1.0 / D, scalar2=None, op0=ALU.mult))
        b.op("vector", lambda: nc.vector.tensor_scalar(
            out=rowt[0:1, GW:2 * GW], in0=pl_ps.ap()[64:65, 0:GW], scalar1=1.0 / D, scalar2=None, op0=ALU.mult))
        b.op("vector", lambda: nc.vector.tensor_tensor(
            out=row2[0:1, GW:2 * GW], in0=rowt[0:1, 0:GW], in1=rowt[0:1, 0:GW], op=ALU.mult))
        b.op("vector", lambda: nc.vector.tensor_tensor(
            out=row2[0:1, GW:2 * GW], in0=rowt[0:1, GW:2 * GW], in1=row2[0:1, GW:2 * GW], op=ALU.subtract))
        b.pub("vector", s_r2)
        b.wait("vector", s_r2, 3 * g + 2)
        b.op("vector", lambda: nc.vector.tensor_tensor(
            out=row2[0:1, GW:2 * GW], in0=rowt[0:1, 0:GW], in1=row2[0:1, 0:GW], op=ALU.mult))
        b.pub("vector", s_r2)
        for j in range(0, KTJ, 2):
            nb = g * (KTJ // 2) + j // 2
            sl_xoV.wait("vector", nb % 2, 16 * (nb // 2 + 1))
            b.wait("vector", s_rbe, g + 1)
            if nb >= 2:
                sl_xwV.wait("vector", nb % 2, 16 * (nb // 2))
            b.op("vector", lambda j=j: nc.vector.tensor_tensor(
                out=ysrc(j), in0=ysrc(j), in1=rb2[:, 0:GW], op=ALU.mult))
            b.op("vector", lambda j=j: nc.vector.tensor_tensor(
                out=ysrc(j), in0=ysrc(j), in1=mrb2[:, 0:GW], op=ALU.subtract))
            b.op("vector", lambda j=j, nb=nb: nc.vector.tensor_tensor(
                out=xoldV[nb % 2][:, 0:GW], in0=ysrc(j), in1=xoldV[nb % 2][:, 0:GW], op=ALU.add))
            b.pub("vector", s_av)

    def s_part():
        for j in range(KTJ):
            if g * KTJ + j >= 2:
                b.wait("scalar", s_plm, g * KTJ + j - 1)
            b.wait("scalar", s_cb, g * KTJ + j + 1)
            b.op("scalar", lambda j=j: nc.scalar.activation(
                out=ysq[j % 2][:, 0:GW], in_=ysrc(j), func=AF.Square))
            b.pub("scalar", s_sq2)
        b.wait("scalar", s_r2, 3 * g + 1)
        b.op("scalar", lambda: nc.scalar.activation(
            out=row2[0:1, 0:GW], in_=row2[0:1, GW:2 * GW], func=AF.Ln, bias=cp_sb[0:1, 0:1]))
        b.op("scalar", lambda: nc.scalar.activation(
            out=row2[0:1, 0:GW], in_=row2[0:1, 0:GW], func=AF.Exp, scale=-0.5))
        b.pub("scalar", s_r2)
        b.wait("scalar", s_rp2, 2 * (g + 1))
        if g >= 1:
            b.wait("scalar", s_av, g * (KTJ // 2))
            b.wait("scalar", s_ap, g * (KTJ // 2))
        b.op("scalar", lambda: nc.scalar.copy(out=rb2[:, 0:GW], in_=rp_a.ap()[:, 0:GW]))
        b.op("scalar", lambda: nc.scalar.copy(out=mrb2[:, 0:GW], in_=rp_b.ap()[:, 0:GW]))
        b.pub("scalar", s_rbe)

    def g_part():
        for j in range(1, KTJ, 2):
            nb = g * (KTJ // 2) + (j - 1) // 2
            sl_xoP.wait("gpsimd", nb % 2, 16 * (nb // 2 + 1))
            b.wait("gpsimd", s_rbe, g + 1)
            if nb >= 2:
                sl_xwP.wait("gpsimd", nb % 2, 16 * (nb // 2))
            b.op("gpsimd", lambda j=j: nc.gpsimd.tensor_tensor(
                out=ysrc(j), in0=ysrc(j), in1=rb2[:, 0:GW], op=ALU.mult))
            b.op("gpsimd", lambda j=j: nc.gpsimd.tensor_tensor(
                out=ysrc(j), in0=ysrc(j), in1=mrb2[:, 0:GW], op=ALU.subtract))
            b.op("gpsimd", lambda j=j, nb=nb: nc.gpsimd.tensor_tensor(
                out=xoldP[nb % 2][:, 0:GW], in0=ysrc(j), in1=xoldP[nb % 2][:, 0:GW], op=ALU.add))
            b.pub("gpsimd", s_ap)

    def sp_part():
        def _ld(j):
            if j % 2 == 0:
                nb = g * (KTJ // 2) + j // 2
                if nb >= 2:
                    sl_xwV.wait("sync", nb % 2, 16 * (nb // 2))
                sl_xoV.load("sync", nb % 2, lambda j=j, nb=nb: nc.sync.dma_start(
                    out=xoldV[nb % 2][:, 0:GW], in_=xin[j][:, gt * GW:(gt + 1) * GW]))
            else:
                nb = g * (KTJ // 2) + (j - 1) // 2
                if nb >= 2:
                    sl_xwP.wait("sync", nb % 2, 16 * (nb // 2))
                sl_xoP.load("sync", nb % 2, lambda j=j, nb=nb: nc.sync.dma_start(
                    out=xoldP[nb % 2][:, 0:GW], in_=xin[j][:, gt * GW:(gt + 1) * GW]))

        def _st(j):
            if j % 2 == 0:
                nb = g * (KTJ // 2) + j // 2
                b.wait("sync", s_av, nb + 1)
                sl_xwV.load("sync", nb % 2, lambda j=j, nb=nb: nc.sync.dma_start(
                    out=xout[j][:, gt * GW:(gt + 1) * GW], in_=xoldV[nb % 2][:, 0:GW]))
            else:
                nb = g * (KTJ // 2) + (j - 1) // 2
                b.wait("sync", s_ap, nb + 1)
                sl_xwP.load("sync", nb % 2, lambda j=j, nb=nb: nc.sync.dma_start(
                    out=xout[j][:, gt * GW:(gt + 1) * GW], in_=xoldP[nb % 2][:, 0:GW]))

        for j in range(KTJ):
            if j >= 4:
                _st(j - 4)
            _ld(j)
        for j in range(KTJ - 4, KTJ):
            _st(j)

    return t_part, v_part, s_part, g_part, sp_part


def mk_postln_sems(b, tag):
    return {
        "cb": b.sem(tag + "cb"), "sq2": b.sem(tag + "sq2"), "plm": b.sem(tag + "plm"),
        "r2": b.sem(tag + "r2"), "rp2": b.sem(tag + "rp2"), "rbe": b.sem(tag + "rbe"),
        "av": b.sem(tag + "av"), "ap": b.sem(tag + "ap"),
        "sl_xoV": Slots(b, tag + "xoV", 2), "sl_xoP": Slots(b, tag + "xoP", 2),
        "sl_xwV": Slots(b, tag + "xwV", 2), "sl_xwP": Slots(b, tag + "xwP", 2),
    }


def mk_postln_bufs(b, reg, tag, c, ps_reg=None, ps_override=None):
    pr = ps_reg if ps_reg is not None else reg
    if ps_override is None:
        ps_override = {}
    return {
        "ones": b.sb(reg, tag + "ones", [P, 1], BF16),
        "oner": b.sb(reg, tag + "oner", [1, P], F32),
        "cp": b.sb(reg, tag + "cp", [P, 8], F32),
        "yb": [b.sb(reg, tag + f"yb{i}", [P, 512], BF16) for i in range(2)],
        "ysq": [b.sb(reg, tag + f"ysq{i}", [P, 512], BF16) for i in range(2)],
        "row2": b.sb(reg, tag + "row2", [1, 1024], F32),
        "rowt": b.sb(reg, tag + "rowt", [1, 1024], F32),
        "rb2": b.sb(reg, tag + "rb2", [P, 512], F32),
        "mrb2": b.sb(reg, tag + "mrb2", [P, 512], F32),
        "xoldV": [b.sb(reg, tag + f"xoV{i}", [P, 512], F32) for i in range(2)],
        "xoldP": [b.sb(reg, tag + f"xoP{i}", [P, 512], F32) for i in range(2)],
        "pl_ps": ps_override.get("pl") or b.ps(pr, tag + "plps", [P, 512]),
        "rp_a": ps_override.get("rpa") or b.ps(pr, tag + "rpa", [P, 512]),
        "rp_b": ps_override.get("rpb") or b.ps(pr, tag + "rpb", [P, 512]),
    }


def emit_postln_consts(b, bufs, c, sem):
    nc = b.nc
    b.op("sync", lambda: nc.sync.dma_start(out=bufs["ones"][:], in_=b.din("ones_bf", c["ones_bf"])), sem, 16)
    b.op("sync", lambda: nc.sync.dma_start(out=bufs["oner"][:], in_=b.din("ones_row_f32", c["ones_row_f32"])), sem, 16)
    b.op("sync", lambda: nc.sync.dma_start(out=bufs["cp"][:], in_=b.din("cpack", c["cpack"])), sem, 16)



def emit_gather(b, tag, eng, idxT_d, lut_sb, s_lut, out_sc, nb, W):
    nc = b.nc
    r = b.region()
    idx = [b.sb(r, f"{tag}i0", [P, W], I32)]
    idxf = b.sb(r, f"{tag}f", [P, W], BF16)
    msk = b.sb(r, f"{tag}m", [P, W], BF16)
    acc = [b.sb(r, f"{tag}a{i}", [P, W], BF16) for i in range(2)]
    sl_i = Slots(b, tag + "i", 1)
    s_d = b.sem(tag + "d")
    sl_o = Slots(b, tag + "o", 2)
    e = getattr(nc, eng)

    for k in range(nb):
        if k >= 1:
            b.wait("sync", s_d, k)
        sl_i.load("sync", 0, lambda k=k: nc.sync.dma_start(out=idx[0][:], in_=idxT_d[k]))

    for k in range(nb):
        sl_i.wait(eng, 0, 16 * (k + 1))
        if k == 0:
            b.wait(eng, s_lut, 16)
        if k >= 2:
            sl_o.wait(eng, k % 2, 16 * (k // 2))
        b.op(eng, lambda k=k: e.tensor_copy(out=idxf[:], in_=idx[0][:]))
        for j in range(16):
            if j == 0:
                b.op(eng, lambda k=k, j=j: e.tensor_scalar(
                    out=acc[k % 2][:], in0=idxf[:], scalar1=float(j), scalar2=lut_sb[:, j:j + 1],
                    op0=ALU.is_equal, op1=ALU.mult))
            else:
                b.op(eng, lambda k=k, j=j: e.tensor_scalar(
                    out=msk[:], in0=idxf[:], scalar1=float(j), scalar2=lut_sb[:, j:j + 1],
                    op0=ALU.is_equal, op1=ALU.mult))
                b.op(eng, lambda k=k: e.tensor_tensor(
                    out=acc[k % 2][:], in0=acc[k % 2][:], in1=msk[:], op=ALU.add))
        b.pub(eng, s_d)

    for k in range(nb):
        b.wait("scalar", s_d, k + 1)
        sl_o.load("scalar", k % 2, lambda k=k: nc.scalar.dma_start(out=out_sc[k], in_=acc[k % 2][:]))
    for i in range(2):
        sl_o.wait("scalar", i, 16 * ((nb + 1 - i) // 2))
    return r


def emit_l0f_gateup(b, c, lay, xn_res, pr):
    nc = b.nc
    r = b.region()
    wgt = [b.sb(r, f"f2wg{i}", [P, KT, P], BF16) for i in range(2)]
    wut = [b.sb(r, f"f2wu{i}", [P, KT, P], BF16) for i in range(2)]
    hrow = [b.sb(r, f"f2h{i}", [P, T], BF16) for i in range(2)]
    ps = [b.ps(r, f"f2ps{i}", [P, 512]) for i in range(6)]
    wg_sc, wu_sc = b._dram["wg_sc"], b._dram["wu_sc"]
    h_sc = b.dscratch("h_sc", [OT0, P, T], BF16)
    sl_w = Slots(b, "f2w", 2)
    s_pm = b.sem("f2pm")
    s_evg = b.sem("f2evg")
    s_evu = b.sem("f2evu")
    sl_h = Slots(b, "f2ho", 2, persistent=True)

    def bank_owner(g_all):
        o_, u_ = divmod(g_all, 16)
        return ("g", o_ * 8 + u_ + 1) if u_ < 8 else ("u", o_ * 8 + (u_ - 8) + 1)

    for o in range(OT0):
        if o >= 2:
            b.wait("sync", s_pm, (o - 1) * 16)
        sl_w.load("sync", o % 2, lambda o=o: nc.sync.dma_start(
            out=wgt[o % 2].ap(),
            in_=wg_sc[:, :, o * P:(o + 1) * P].rearrange("k p m -> p k m")))
        sl_w.uses[o % 2] -= 1
        sl_w.load("sync", o % 2, lambda o=o: nc.sync.dma_start(
            out=wut[o % 2].ap(),
            in_=wu_sc[:, :, o * P:(o + 1) * P].rearrange("k p m -> p k m")))

    for o in range(OT0):
        sl_w.wait("tensor", o % 2, 32 * (o // 2 + 1))
        for u in range(16):
            g_all = o * 16 + u
            wt = wgt if u < 8 else wut
            tb = u % 8
            bank = g_all % 6
            if g_all >= 6:
                own, cnt = bank_owner(g_all - 6)
                b.wait("tensor", s_evg if own == "g" else s_evu, cnt)
            for k in range(KT):
                st, sp = (k == 0), (k == KT - 1)
                b.op("tensor", lambda o=o, u=u, tb=tb, k=k, bank=bank, wt=wt, st=st, sp=sp: nc.tensor.matmul(
                    out=ps[bank].ap()[:], lhsT=wt[o % 2][:, k, :],
                    rhs=xn_res[:, k * T + tb * 512:k * T + (tb + 1) * 512],
                    start=st, stop=sp), s_pm if sp else None)

    for o in range(OT0):
        for tb in range(NBT):
            g_all = o * 16 + tb
            bank = g_all % 6
            b.wait("scalar", s_pm, g_all + 1)
            if tb == 0 and o >= 2:
                sl_h.wait("scalar", o % 2, 16 * (o // 2))
            b.op("scalar", lambda o=o, tb=tb, bank=bank: nc.scalar.activation(
                out=hrow[o % 2][:, tb * 512:(tb + 1) * 512], in_=ps[bank].ap()[:], func=AF.Silu))
            b.pub("scalar", s_evg)

    for o in range(OT0):
        for tb in range(NBT):
            g_all = o * 16 + 8 + tb
            bank = g_all % 6
            b.wait("vector", s_pm, g_all + 1)
            b.wait("vector", s_evg, o * 8 + tb + 1)
            b.op("vector", lambda o=o, tb=tb, bank=bank: nc.vector.tensor_tensor(
                out=hrow[o % 2][:, tb * 512:(tb + 1) * 512],
                in0=hrow[o % 2][:, tb * 512:(tb + 1) * 512],
                in1=ps[bank].ap()[:], op=ALU.mult))
            b.pub("vector", s_evu)

    for o in range(OT0):
        b.wait("gpsimd", s_evu, (o + 1) * 8)
        sl_h.load("gpsimd", o % 2, lambda o=o: nc.gpsimd.dma_start(out=h_sc[o], in_=hrow[o % 2][:]))
    for i in range(2):
        sl_h.wait("gpsimd", i, 16 * ((OT0 + 1 - i) // 2))
    b.barrier()
    r.close()


def emit_l0f_down(b, c, lay, xin, xout):
    nc = b.nc
    r = b.region()
    KOC = [8, 8, 8, 8, 8, 3]
    NTQ = 4
    ydown = b.sb(r, "f3y", [P, KT, 1024], F32)
    wdt = [b.sb(r, f"f3w{i}", [P, 8, D], BF16) for i in range(2)]
    hq = [b.sb(r, f"f3h{i}", [P, 8, 1024], BF16) for i in range(2)]
    ps = [b.ps(r, f"f3ps{i}", [P, 512]) for i in range(4)]
    pbufs = mk_postln_bufs(b, r, "f3p", c)
    E = mk_postln_sems(b, "f3p")
    s_cst = b.sem("f3cst")
    emit_postln_consts(b, pbufs, c, s_cst)
    wd_sc = b._dram["wd_sc"]
    h_sc = b._dram["h_sc"]
    sl_wd = Slots(b, "f3wd", 2)
    sl_hq = Slots(b, "f3hq", 2)
    s_mm = b.sem("f3mm")
    s_ev = b.sem("f3ev")

    tgt_wd = {}
    tgt_hq = {}
    wd_cnt = [0, 0]
    hq_cnt = [0, 0]

    def unit(tq, ci, j, tb):
        return ((tq * 6 + ci) * KT + j) * 2 + tb

    def pg(tq, gg):
        return tq * 2 + gg

    parts = {}
    for tq in range(NTQ):
        for ci, kosz in enumerate(KOC):
            ko0 = sum(KOC[:ci])
            idx = tq * 6 + ci
            slot = idx % 2
            if idx >= 2:
                b.wait("sync", s_mm, (idx - 1) * 32)
            for kk in range(kosz):
                sl_wd.load("sync", slot, lambda ko0=ko0, kk=kk, slot=slot: nc.sync.dma_start(
                    out=wdt[slot].ap()[:, kk, :], in_=wd_sc[ko0 + kk]))
                sl_hq.load("sync", slot, lambda ko0=ko0, kk=kk, slot=slot, tq=tq: nc.sync.dma_start(
                    out=hq[slot].ap()[:, kk, :], in_=h_sc[ko0 + kk][:, tq * 1024:(tq + 1) * 1024]))
            wd_cnt[slot] += 16 * kosz
            hq_cnt[slot] += 16 * kosz
            tgt_wd[idx] = wd_cnt[slot]
            tgt_hq[idx] = hq_cnt[slot]
        for gg in range(2):
            g = pg(tq, gg)
            parts[g] = emit_postln_group(
                b, "f3p", E, g, g, (lambda j, tq=tq, gg=gg: ydown.ap()[:, j, gg * 512:(gg + 1) * 512]),
                xin, xout, pbufs)
        for gg in range(2):
            parts[pg(tq, gg)][4]()

    b.wait("tensor", s_cst, 48)
    for tq in range(NTQ):
        for ci, kosz in enumerate(KOC):
            idx = tq * 6 + ci
            slot = idx % 2
            sl_wd.wait("tensor", slot, tgt_wd[idx])
            sl_hq.wait("tensor", slot, tgt_hq[idx])
            for j in range(KT):
                for kk in range(kosz):
                    for tb in range(2):
                        bank = (j % 2) * 2 + tb
                        u = unit(tq, ci, j, tb)
                        if kk == 0 and u >= 4:
                            b.wait("tensor", s_ev, u - 3)
                        st, sp = (kk == 0), (kk == kosz - 1)
                        b.op("tensor", lambda slot=slot, kk=kk, j=j, tb=tb, bank=bank, st=st, sp=sp: nc.tensor.matmul(
                            out=ps[bank].ap()[:], lhsT=wdt[slot][:, kk, j * P:(j + 1) * P],
                            rhs=hq[slot].ap()[:, kk, tb * 512:(tb + 1) * 512],
                            start=st, stop=sp), s_mm if sp else None)
        for gg in range(2):
            parts[pg(tq, gg)][0]()

    for tq in range(NTQ):
        if tq >= 1:
            b.wait("vector", E["ap"], tq * 16)
        for ci, kosz in enumerate(KOC):
            for j in range(KT):
                for tb in range(2):
                    u = unit(tq, ci, j, tb)
                    b.wait("vector", s_mm, u + 1)
                    if ci == 0:
                        b.op("vector", lambda j=j, tb=tb, bank=(j % 2) * 2 + tb: nc.vector.tensor_copy(
                            out=ydown.ap()[:, j, tb * 512:(tb + 1) * 512], in_=ps[bank].ap()[:]))
                    else:
                        b.op("vector", lambda j=j, tb=tb, bank=(j % 2) * 2 + tb: nc.vector.tensor_tensor(
                            out=ydown.ap()[:, j, tb * 512:(tb + 1) * 512],
                            in0=ydown.ap()[:, j, tb * 512:(tb + 1) * 512],
                            in1=ps[bank].ap()[:], op=ALU.add))
                    b.pub("vector", s_ev)
        for gg in range(2):
            parts[pg(tq, gg)][1]()

    for tq in range(NTQ):
        for gg in range(2):
            parts[pg(tq, gg)][2]()
    for tq in range(NTQ):
        for gg in range(2):
            parts[pg(tq, gg)][3]()

    for i in range(2):
        E["sl_xwV"].wait("sync", i, 16 * (NTQ * 2 * (KT // 2) // 2))
        E["sl_xwP"].wait("sync", i, 16 * (NTQ * 2 * (KT // 2) // 2))
    b.barrier()
    r.close()


def ph_l0f(b, c, lay, xin, xout):
    nc = b.nc
    for cc_s in b.cc_sems:
        b.wait("sync", cc_s, 1)

    pr = b.region()
    xn = emit_preln(b, "f1", xin, c, pr)
    emit_l0f_gateup(b, c, lay, xn, pr)
    pr.close()
    emit_l0f_down(b, c, lay, xin, xout)



def emit_l1a_qkvg(b, c, lay, xn_res, pr):
    nc = b.nc
    r = b.region()
    wfp = [b.sb(r, f"b2wf{i}", [P, KT * P], F32) for i in range(2)]
    wb = [b.sb(r, f"b2wb{i}", [P, KT * P], BF16) for i in range(2)]
    orow = b.sb(r, "b2or", [P, T], BF16)
    ps = [b.ps(r, f"b2ps{i}", [P, 512]) for i in range(6)]
    q_sc = b.dscratch("q1_sc", [NB1, P, H, BLK1], BF16)
    k_sc = b.dscratch("k1_sc", [NB1, P, H, BLK1], BF16)
    v_sc = b.dscratch("v1_sc", [NB1, P, H, BLK1], BF16)
    g_sc = b.dscratch("g1_sc", [NB1, P, H, BLK1], BF16)
    wq_d = b.din("l1_wqkvT", lay["l1_wqkvT"])
    wg_d = b.din("l1_wgateT", lay["l1_wgateT"])
    sl_w = Slots(b, "b2w", 2)
    s_wc = b.sem("b2wc")
    s_pm = b.sem("b2pm")
    s_ev = b.sem("b2ev")
    sl_o = Slots(b, "b2o", 8, persistent=True)

    NJ = 48 + 16
    for jj in range(NJ):
        if jj >= 2:
            b.wait("sync", s_wc, jj - 1)
        src = (lambda jj=jj: wq_d[jj] if jj < 48 else wg_d[jj - 48])
        sl_w.load("sync", jj % 2, lambda jj=jj, src=src: nc.sync.dma_start(
            out=wfp[jj % 2][:], in_=src()))

    for jj in range(NJ):
        sl_w.wait("vector", jj % 2, 16 * (jj // 2 + 1))
        if jj >= 2:
            b.wait("vector", s_pm, (jj - 1) * 8)
        b.op("vector", lambda jj=jj: nc.vector.tensor_copy(out=wb[jj % 2][:], in_=wfp[jj % 2][:]))
        b.pub("vector", s_wc)

    for jj in range(NJ):
        b.wait("tensor", s_wc, jj + 1)
        for tb in range(NBT):
            g = jj * 8 + tb
            if g >= 6:
                b.wait("tensor", s_ev, g - 5)
            for k in range(KT):
                st, sp = (k == 0), (k == KT - 1)
                b.op("tensor", lambda jj=jj, tb=tb, k=k, g=g, st=st, sp=sp: nc.tensor.matmul(
                    out=ps[g % 6].ap()[:], lhsT=wb[jj % 2][:, k * P:(k + 1) * P],
                    rhs=xn_res[:, k * T + tb * 512:k * T + (tb + 1) * 512],
                    start=st, stop=sp), s_pm if sp else None)

    s_evd = b.sem("b2evd")
    for jj in range(NJ):
        func = AF.Silu if jj < 48 else AF.Sigmoid
        for tb in range(NBT):
            g = jj * 8 + tb
            b.wait("scalar", s_pm, g + 1)
            if jj >= 1:
                sl_o.wait("scalar", tb, 16 * jj)
            b.op("scalar", lambda g=g, tb=tb, func=func: nc.scalar.activation(
                out=orow[:, tb * 512:(tb + 1) * 512], in_=ps[g % 6].ap()[:], func=func), s_ev)
            b.pub("scalar", s_evd)

    for jj in range(NJ):
        if jj < 48:
            h, kind = jj // 3, jj % 3
            dst = [q_sc, k_sc, v_sc][kind]
        else:
            h, dst = jj - 48, g_sc
        for tb in range(NBT):
            g = jj * 8 + tb
            b.wait("gpsimd", s_evd, g + 1)
            sl_o.load("gpsimd", tb, lambda dst=dst, h=h, tb=tb: nc.gpsimd.dma_start(
                out=dst[tb * 2:(tb + 1) * 2, :, h, :].rearrange("cb p m -> p cb m"),
                in_=orow.ap()[:, tb * 512:(tb + 1) * 512].rearrange("p (cb m) -> p cb m", cb=2)))
    for tb in range(NBT):
        sl_o.wait("gpsimd", tb, 16 * NJ)
    b.barrier()
    r.close()


def emit_l1a_scan(b, c, lay, xin, xout):
    nc = b.nc
    r = b.region()
    wob = b.sb(r, "b3wo", [P, H * KT * P], BF16)
    kv = b.sb(r, "b3kv", [P, H * P], F32)
    qdec = b.sb(r, "b3qd", [P, H * BLK1], BF16)
    diag = b.sb(r, "b3dg", [P, H * 2 * BLK1], BF16)
    kdec = b.sb(r, "b3kd", [P, 2 * H], F32)
    bdec = b.sb(r, "b3bd", [P, H], F32)
    negs = b.sb(r, "b3ns", [P, H], F32)
    slope_sb = b.sb(r, "b3sl", [P, H], F32)
    arrb = b.sb(r, "b3ar", [P, BLK1], F32)
    revc = b.sb(r, "b3rv", [P, 2], F32)
    posb = b.sb(r, "b3po", [P, 2 * BLK1], F32)
    maskb = b.sb(r, "b3mk", [P, 2 * BLK1], F32)
    blkc = b.sb(r, "b3bc", [P, 1], F32)
    nwpc = b.sb(r, "b3nw", [P, KT], F32)
    ids = b.sb(r, "b3id", [P, P], BF16)
    etmp = b.sb(r, "b3et", [P, BLK1], F32)
    wfp = [b.sb(r, "b3wf0", [P, KT * P], F32)]
    qch = [b.sb(r, "b3q0", [P, H * BLK1], BF16)]
    kch = [b.sb(r, "b3k0", [P, H * BLK1], BF16)]
    vch = [b.sb(r, "b3v0", [P, H * BLK1], BF16)]
    gch = [b.sb(r, "b3g0", [P, H * BLK1], BF16)]
    ktd = b.sb(r, "b3ktd", [P, BLK1], BF16)
    vtok = b.sb(r, "b3vt", [P, BLK1], BF16)
    scm = b.sb(r, "b3sc", [P, 2 * BLK1], BF16)
    qd = b.sb(r, "b3qdd", [P, BLK1], F32)
    dkv = b.sb(r, "b3dk", [P, P], F32)
    afm = b.sb(r, "b3af", [P, H * BLK1], BF16)
    asq = b.sb(r, "b3as", [P, BLK1], BF16)
    rowb = b.sb(r, "b3rw", [1, BLK1], F32)
    rb_c = b.sb(r, "b3rb", [P, BLK1], F32)
    gtmp = b.sb(r, "b3gt", [P, BLK1], F32)
    grbuf = b.sb(r, "b3gr", [P, H, BLK1], BF16)
    ywo = b.sb(r, "b3yw", [P, KT, BLK1], F32)
    ps_tk = b.ps(r, "b3ptk", [P, 1024], BF16)
    ps_tv = b.ps(r, "b3ptv", [P, 1024], BF16)
    ps_sc = [b.ps(r, f"b3psc{i}", [P, 512]) for i in range(2)]
    ps_o = b.ps(r, "b3po2", [P, 512])
    ps_kv = b.ps(r, "b3pkv", [P, 512])
    ps_rm = b.ps(r, "b3prm", [P, 512])
    ps_wo = [b.ps(r, "b3pw0", [P, 512])]
    pbufs = mk_postln_bufs(b, r, "b3p", c, ps_reg=None,
                           ps_override={"pl": ps_sc[0], "rpa": ps_sc[1], "rpb": ps_o})
    E = mk_postln_sems(b, "b3p")
    s_cst = b.sem("b3cst")
    emit_postln_consts(b, pbufs, c, s_cst)

    q_sc, k_sc = b._dram["q1_sc"], b._dram["k1_sc"]
    v_sc, g_sc = b._dram["v1_sc"], b._dram["g1_sc"]
    wo_d = b.din("l1_woutT", lay["l1_woutT"])

    s_ldc = b.sem("b3ldc")
    sl_wol = Slots(b, "b3wol", 1)
    s_wc = b.sem("b3wc")
    sl_ch = Slots(b, "b3ch", 1)
    s_dec = b.sem("b3dec")
    s_t1 = b.sem("b3t1")
    s_t2 = b.sem("b3t2")
    s_m1 = b.sem("b3m1")
    s_o = b.sem("b3o")
    s_m5 = b.sem("b3m5")
    s_m6 = b.sem("b3m6")
    s_vt = b.sem("b3vt")
    s_dkv = b.sem("b3dkv")
    s_aq = b.sem("b3aq")
    s_scm = b.sem("b3scm")
    s_qd = b.sem("b3qd2")
    s_ktd = b.sem("b3ktd2")
    s_kv = b.sem("b3kv2")
    s_rr = b.sem("b3rr")
    s_rp = b.sem("b3rp")
    s_rb = b.sem("b3rb2")
    s_gr = b.sem("b3gr2")
    s_wo = b.sem("b3wo2")
    s_ye = b.sem("b3ye")

    for nm, dst, src in [("id", ids, ("ident", c["ident"])),
                         ("sl", slope_sb, ("l1_slope_b", lay["l1_slope_b"])),
                         ("ar", arrb, ("l1_arr_row_b", c["l1_arr_row_b"])),
                         ("rv", revc, ("l1_rev_pc", c["l1_rev_pc"])),
                         ("bc", blkc, ("l1_blk_const", c["l1_blk_const"])),
                         ("nw", nwpc, ("l1_normw", lay["l1_normw"]))]:
        b.op("sync", lambda dst=dst, src=src: nc.sync.dma_start(out=dst[:], in_=b.din(*src)), s_ldc, 16)
    b.op("sync", lambda: nc.sync.dma_start(
        out=posb.ap().rearrange("p (h m) -> p h m", h=2),
        in_=b.din("l1_posT", c["l1_posT"]).rearrange("h p m -> p h m")), s_ldc, 16)
    b.op("sync", lambda: nc.sync.dma_start(
        out=maskb.ap().rearrange("p (h m) -> p h m", h=2),
        in_=b.din("l1_maskT", c["l1_maskT"]).rearrange("h p m -> p h m")), s_ldc, 16)
    NLDC = 8
    for j in range(H):
        if j >= 1:
            b.wait("sync", s_wc, j)
        sl_wol.load("sync", 0, lambda j=j: nc.sync.dma_start(out=wfp[0][:], in_=wo_d[j]))

    s_dge = b.sem("b3dge")
    s_dgv = b.sem("b3dgv")
    b.wait("vector", s_ldc, 16 * NLDC)
    b.op("vector", lambda: nc.vector.tensor_scalar(
        out=negs[:], in0=slope_sb[:], scalar1=-1.0, scalar2=None, op0=ALU.mult))
    b.pub("vector", s_dec)
    b.wait("scalar", s_ldc, 16 * NLDC)
    b.wait("scalar", s_dec, 1)
    for h in range(H):
        b.op("scalar", lambda h=h: nc.scalar.activation(
            out=qdec[:, h * BLK1:(h + 1) * BLK1], in_=arrb[:], func=AF.Exp,
            scale=negs[:, h:h + 1]))
        for half in range(2):
            b.op("scalar", lambda h=h, half=half: nc.scalar.activation(
                out=kdec[:, 2 * h + half:2 * h + half + 1], in_=revc[:, half:half + 1],
                func=AF.Exp, scale=negs[:, h:h + 1]))
        b.op("scalar", lambda h=h: nc.scalar.activation(
            out=bdec[:, h:h + 1], in_=blkc[:], func=AF.Exp, scale=negs[:, h:h + 1]))
    for i in range(2 * H):
        h, half = divmod(i, 2)
        if i >= 1:
            b.wait("scalar", s_dgv, i)
        b.op("scalar", lambda h=h, half=half, i=i: nc.scalar.activation(
            out=etmp[:, (i % 2) * 0:BLK1] if False else etmp[:],
            in_=posb[:, half * BLK1:(half + 1) * BLK1], func=AF.Exp,
            scale=negs[:, h:h + 1]))
        b.pub("scalar", s_dge)
        b.wait("vector", s_dge, i + 1)
        b.op("vector", lambda h=h, half=half: nc.vector.tensor_tensor(
            out=diag[:, (h * 2 + half) * BLK1:(h * 2 + half + 1) * BLK1],
            in0=etmp[:], in1=maskb[:, half * BLK1:(half + 1) * BLK1], op=ALU.mult))
        b.pub("vector", s_dgv)
    b.pub("scalar", s_dec)

    for j in range(H):
        sl_wol.wait("vector", 0, 16 * (j + 1))
        for k in range(KT):
            b.op("vector", lambda j=j, k=k: nc.vector.tensor_scalar(
                out=wob[:, (j * KT + k) * P:(j * KT + k + 1) * P],
                in0=wfp[0][:, k * P:(k + 1) * P],
                scalar1=nwpc[:, k:k + 1], scalar2=None, op0=ALU.mult))
        b.pub("vector", s_wc)
    b.op("gpsimd", lambda: nc.gpsimd.memset(kv[:], 0.0))
    b.pub("gpsimd", s_kv)

    parts = {}
    for nb in range(NB1):
        if nb >= 1:
            b.wait("sync", s_m5, nb * H)
            b.wait("sync", s_ktd, nb * H)
            b.wait("sync", s_gr, nb * H)
        for src, dstb in [(q_sc, qch), (k_sc, kch), (v_sc, vch), (g_sc, gch)]:
            sl_ch.load("sync", 0, lambda src=src, dstb=dstb, nb=nb: nc.sync.dma_start(
                out=dstb[0].ap().rearrange("p (h m) -> p h m", h=H), in_=src[nb]))
            sl_ch.uses[0] -= 1
        sl_ch.uses[0] += 1
        parts[nb] = emit_postln_group(
            b, "b3p", E, nb, nb, (lambda j: ywo.ap()[:, j, :]),
            xin, xout, pbufs, GW=BLK1,
            yready=(lambda j, nb=nb: b.wait("vector", s_ye, nb * KT + j + 1)))
        parts[nb][4]()

    def hs2(h):
        return slice(h * BLK1, (h + 1) * BLK1)

    b.wait("tensor", s_ldc, 16 * NLDC)
    b.wait("tensor", s_wc, H)
    for nb in range(NB1):
        sl_ch.wait("tensor", 0, 64 * (nb + 1))
        if nb >= 1:
            b.wait("tensor", E["rbe"], nb)
            b.wait("tensor", E["r2"], 3 * nb)
        for h in range(H):
            n = nb * H + h
            if n >= 1:
                b.wait("tensor", s_ktd, n)
                b.wait("tensor", s_vt, n)
            for half in range(2):
                b.op("tensor", lambda nb=nb, h=h, half=half: nc.tensor.transpose(
                    out=ps_tk.ap()[:, half * P:(half + 1) * P],
                    in_=kch[0][:, h * BLK1 + half * P: h * BLK1 + (half + 1) * P],
                    identity=ids[:]), s_t1 if half == 1 else None)
            for half in range(2):
                b.op("tensor", lambda nb=nb, h=h, half=half: nc.tensor.transpose(
                    out=ps_tv.ap()[:, half * P:(half + 1) * P],
                    in_=vch[0][:, h * BLK1 + half * P: h * BLK1 + (half + 1) * P],
                    identity=ids[:]), s_t2 if half == 1 else None)
            if n >= 1:
                b.wait("tensor", s_ktd, n)
            for half in range(2):
                b.op("tensor", lambda nb=nb, h=h, half=half: nc.tensor.matmul(
                    out=ps_sc[half].ap()[:, 0:BLK1],
                    lhsT=kch[0][:, h * BLK1 + half * P: h * BLK1 + (half + 1) * P],
                    rhs=qch[0][:, hs2(h)], start=True, stop=True),
                    s_m1 if half == 1 else None)
            b.wait("tensor", s_kv, (n + 2 - H) if nb >= 1 else 1)
            b.wait("tensor", s_ktd, n + 1)
            if n >= 1:
                b.wait("tensor", s_aq, n)
            b.op("tensor", lambda h=h: nc.tensor.matmul(
                out=ps_o.ap()[:, 0:BLK1], lhsT=kv[:, h * P:(h + 1) * P], rhs=qd[:],
                start=True, stop=False))
            b.wait("tensor", s_vt, n + 1)
            for half in range(2):
                b.op("tensor", lambda half=half: nc.tensor.matmul(
                    out=ps_o.ap()[:, 0:BLK1], lhsT=vtok[:, half * P:(half + 1) * P],
                    rhs=scm[:, half * BLK1:(half + 1) * BLK1], start=False,
                    stop=(half == 1)), s_o if half == 1 else None)
            if n >= 1:
                b.wait("tensor", s_kv, n + 1)
            for half in range(2):
                b.op("tensor", lambda half=half: nc.tensor.matmul(
                    out=ps_kv.ap()[:, 0:P], lhsT=ktd[:, half * P:(half + 1) * P],
                    rhs=vtok[:, half * P:(half + 1) * P], start=(half == 0),
                    stop=(half == 1)), s_m5 if half == 1 else None)
            b.wait("tensor", s_aq, n + 1)
            if h == 0 and nb >= 1:
                b.wait("tensor", s_rp, nb)
                b.wait("tensor", s_rb, nb)
            b.op("tensor", lambda h=h: nc.tensor.matmul(
                out=ps_rm.ap()[0:1, 0:BLK1], lhsT=ones_dummy[:] if False else pbufs["ones"][:],
                rhs=asq[:], start=(h == 0), stop=(h == H - 1)), s_m6)
        b.wait("tensor", s_rr, nb + 1)
        b.op("tensor", lambda: nc.tensor.matmul(
            out=ps_rm.ap()[:, BLK1:2 * BLK1], lhsT=pbufs["oner"][:], rhs=rowb[0:1, :],
            start=True, stop=True), s_rp)
        b.wait("tensor", s_gr, (nb + 1) * H)
        for j in range(KT):
            if nb * KT + j >= 1:
                b.wait("tensor", s_ye, nb * KT + j)
            for h in range(H):
                b.op("tensor", lambda j=j, h=h, nb=nb: nc.tensor.matmul(
                    out=ps_wo[0].ap()[:, 0:BLK1], lhsT=wob[:, (j * KT + h) * P:(j * KT + h + 1) * P],
                    rhs=grbuf.ap()[:, h, :], start=(h == 0), stop=(h == H - 1)),
                    s_wo if h == H - 1 else None, inc=H)
        parts[nb][0]()

    b.wait("vector", s_dgv, 2 * H)
    b.wait("vector", s_dec, 2)
    for nb in range(NB1):
        sl_ch.wait("vector", 0, 64 * (nb + 1))
        for h in range(H):
            n = nb * H + h
            b.wait("vector", s_t1, n + 1)
            b.wait("vector", s_m5, n)
            for half in range(2):
                b.op("vector", lambda h=h, half=half: nc.vector.tensor_scalar(
                    out=ktd[:, half * P:(half + 1) * P], in0=ps_tk.ap()[:, half * P:(half + 1) * P],
                    scalar1=kdec[:, 2 * h + half:2 * h + half + 1], scalar2=None, op0=ALU.mult))
            b.wait("vector", s_m1, n + 1)
            b.wait("vector", s_o, n)
            for half in range(2):
                b.op("vector", lambda h=h, half=half: nc.vector.tensor_tensor(
                    out=scm[:, half * BLK1:(half + 1) * BLK1], in0=ps_sc[half].ap()[:, 0:BLK1],
                    in1=diag[:, (h * 2 + half) * BLK1:(h * 2 + half + 1) * BLK1], op=ALU.mult))
            b.op("vector", lambda nb=nb, h=h: nc.vector.tensor_tensor(
                out=qd[:], in0=qch[0][:, hs2(h)], in1=qdec[:, hs2(h)], op=ALU.mult))
            b.pub("vector", s_ktd)
            b.wait("vector", s_m5, n + 1)
            b.op("vector", lambda h=h: nc.vector.scalar_tensor_tensor(
                out=kv[:, h * P:(h + 1) * P], in0=kv[:, h * P:(h + 1) * P],
                scalar=bdec[:, h:h + 1], in1=ps_kv.ap()[:, 0:P], op0=ALU.mult, op1=ALU.add))
            b.pub("vector", s_kv)
        b.wait("vector", s_rb, nb + 1)
        b.wait("vector", s_aq, (nb + 1) * H)
        b.wait("vector", s_wo, (nb) * KT * H)
        for h in range(H):
            b.op("vector", lambda h=h: nc.vector.tensor_tensor(
                out=gtmp[:], in0=afm[:, hs2(h)], in1=rb_c[:], op=ALU.mult))
            b.op("vector", lambda h=h, nb=nb: nc.vector.tensor_tensor(
                out=grbuf.ap()[:, h, :], in0=gtmp[:], in1=gch[0][:, hs2(h)], op=ALU.mult))
        b.pub("vector", s_gr, inc=H)
        parts[nb][1]()

    b.wait("scalar", s_cst, 48)
    for nb in range(NB1):
        gt = nb // 2
        for h in range(H):
            n = nb * H + h
            b.wait("scalar", s_t2, n + 1)
            b.wait("scalar", s_m5, n)
            b.op("scalar", lambda: nc.scalar.copy(out=vtok[:], in_=ps_tv.ap()[:, 0:2 * P]))
            b.pub("scalar", s_vt)
            b.wait("scalar", s_o, n + 1)
            if nb >= 1:
                b.wait("scalar", s_gr, n - H + 1)
            b.wait("scalar", s_m6, n)
            b.op("scalar", lambda h=h: nc.scalar.copy(out=afm[:, hs2(h)], in_=ps_o.ap()[:, 0:BLK1]))
            b.op("scalar", lambda: nc.scalar.activation(out=asq[:], in_=ps_o.ap()[:, 0:BLK1], func=AF.Square))
            b.pub("scalar", s_aq)
        b.wait("scalar", s_m6, (nb + 1) * H)
        b.wait("scalar", s_rp, nb)
        b.op("scalar", lambda: nc.scalar.activation(
            out=rowb[0:1, :], in_=ps_rm.ap()[0:1, 0:BLK1], func=AF.Ln, scale=1.0 / D,
            bias=pbufs["cp"][0:1, 1:2]))
        b.op("scalar", lambda: nc.scalar.activation(
            out=rowb[0:1, :], in_=rowb[0:1, :], func=AF.Exp, scale=-0.5))
        b.pub("scalar", s_rr)
        b.wait("scalar", s_rp, nb + 1)
        b.wait("scalar", s_gr, nb * H)
        b.op("scalar", lambda: nc.scalar.copy(out=rb_c[:], in_=ps_rm.ap()[:, BLK1:2 * BLK1]))
        b.pub("scalar", s_rb)
        for j in range(KT):
            b.wait("scalar", s_wo, nb * KT * H + (j + 1) * H)
            if nb >= 1:
                b.wait("scalar", E["av"], nb * 8)
                b.wait("scalar", E["ap"], nb * 8)
            b.op("scalar", lambda j=j, nb=nb: nc.scalar.copy(
                out=ywo.ap()[:, j, :], in_=ps_wo[0].ap()[:, 0:BLK1]))
            b.pub("scalar", s_ye)
        parts[nb][2]()

    for nb in range(NB1):
        parts[nb][3]()

    for i in range(2):
        E["sl_xwV"].wait("sync", i, 16 * (NB1 * (KT // 2) // 2))
        E["sl_xwP"].wait("sync", i, 16 * (NB1 * (KT // 2) // 2))
    b.barrier()
    r.close()


def ph_l1a(b, c, lay, xin, xout):
    pr = b.region()
    xn = emit_preln(b, "b1", xin, c, pr)
    emit_l1a_qkvg(b, c, lay, xn, pr)
    pr.close()
    emit_l1a_scan(b, c, lay, xin, xout)



def emit_l1f_gateup(b, c, lay, xn_res, pr):
    nc = b.nc
    r = b.region()
    w8g = [b.sb(r, f"e2g{i}", [P, KT * P], I8) for i in range(2)]
    w8u = [b.sb(r, f"e2u{i}", [P, KT * P], I8) for i in range(2)]
    wbg = [b.sb(r, f"e2bg{i}", [P, KT * P], BF16) for i in range(2)]
    wbu = [b.sb(r, f"e2bu{i}", [P, KT * P], BF16) for i in range(2)]
    hrow = [b.sb(r, f"e2h{i}", [P, T], BF16) for i in range(2)]
    sg_sb = b.sb(r, "e2sg", [P, OT1], F32)
    su_sb = b.sb(r, "e2su", [P, OT1], F32)
    ps = [b.ps(r, f"e2ps{i}", [P, 512]) for i in range(6)]
    wg_d = b.din("l1_wg8", lay["l1_wg8"])
    wu_d = b.din("l1_wu8", lay["l1_wu8"])
    h_sc = b.dscratch("h2_sc", [OT1, P, T], BF16)
    sl_w = Slots(b, "e2w", 2)
    s_sld = b.sem("e2sld")
    s_wc = b.sem("e2wc")
    s_pm = b.sem("e2pm")
    s_evg = b.sem("e2evg")
    s_evu = b.sem("e2evu")
    sl_h = Slots(b, "e2ho", 2, persistent=True)

    b.op("sync", lambda: nc.sync.dma_start(out=sg_sb[:], in_=b.din("l1_sg", lay["l1_sg"])), s_sld, 16)
    b.op("sync", lambda: nc.sync.dma_start(out=su_sb[:], in_=b.din("l1_su", lay["l1_su"])), s_sld, 16)

    def bank_owner(g_all):
        o_, u_ = divmod(g_all, 16)
        return ("g", o_ * 8 + u_ + 1) if u_ < 8 else ("u", o_ * 8 + (u_ - 8) + 1)

    for o in range(OT1):
        if o >= 2:
            b.wait("sync", s_wc, o - 1)
        sl_w.load("sync", o % 2, lambda o=o: nc.sync.dma_start(out=w8g[o % 2][:], in_=wg_d[o]))
        sl_w.uses[o % 2] -= 1
        sl_w.load("sync", o % 2, lambda o=o: nc.sync.dma_start(out=w8u[o % 2][:], in_=wu_d[o]))

    def _upev(o):
        for tb in range(NBT):
            g_all = o * 16 + 8 + tb
            bank = g_all % 6
            b.wait("vector", s_pm, g_all + 1)
            b.wait("vector", s_evg, o * 8 + tb + 1)
            b.op("vector", lambda o=o, tb=tb, bank=bank: nc.vector.scalar_tensor_tensor(
                out=hrow[o % 2][:, tb * 512:(tb + 1) * 512], in0=ps[bank].ap()[:],
                scalar=su_sb[:, o:o + 1], in1=hrow[o % 2][:, tb * 512:(tb + 1) * 512],
                op0=ALU.mult, op1=ALU.mult))
            b.pub("vector", s_evu)

    for o in range(OT1 + 1):
        if o < OT1:
            sl_w.wait("vector", o % 2, 32 * (o // 2 + 1))
            if o >= 2:
                b.wait("vector", s_pm, (o - 1) * 16)
            b.op("vector", lambda o=o: nc.vector.tensor_copy(out=wbg[o % 2][:], in_=w8g[o % 2][:]))
            b.op("vector", lambda o=o: nc.vector.tensor_copy(out=wbu[o % 2][:], in_=w8u[o % 2][:]))
            b.pub("vector", s_wc)
        if o >= 1:
            _upev(o - 1)

    for o in range(OT1):
        b.wait("tensor", s_wc, o + 1)
        for u in range(16):
            g_all = o * 16 + u
            wt = wbg if u < 8 else wbu
            tb = u % 8
            bank = g_all % 6
            if g_all >= 6:
                own, cnt = bank_owner(g_all - 6)
                b.wait("tensor", s_evg if own == "g" else s_evu, cnt)
            for k in range(KT):
                st, sp = (k == 0), (k == KT - 1)
                b.op("tensor", lambda o=o, tb=tb, k=k, bank=bank, wt=wt, st=st, sp=sp: nc.tensor.matmul(
                    out=ps[bank].ap()[:], lhsT=wt[o % 2][:, k * P:(k + 1) * P],
                    rhs=xn_res[:, k * T + tb * 512:k * T + (tb + 1) * 512],
                    start=st, stop=sp), s_pm if sp else None)

    b.wait("scalar", s_sld, 32)
    for o in range(OT1):
        for tb in range(NBT):
            g_all = o * 16 + tb
            bank = g_all % 6
            b.wait("scalar", s_pm, g_all + 1)
            if tb == 0 and o >= 2:
                sl_h.wait("scalar", o % 2, 16 * (o // 2))
            b.op("scalar", lambda o=o, tb=tb, bank=bank: nc.scalar.activation(
                out=hrow[o % 2][:, tb * 512:(tb + 1) * 512], in_=ps[bank].ap()[:],
                func=AF.Silu, scale=sg_sb[:, o:o + 1]))
            b.pub("scalar", s_evg)

    for o in range(OT1):
        b.wait("gpsimd", s_evu, (o + 1) * 8)
        sl_h.load("gpsimd", o % 2, lambda o=o: nc.gpsimd.dma_start(out=h_sc[o], in_=hrow[o % 2][:]))
    for i in range(2):
        sl_h.wait("gpsimd", i, 16 * ((OT1 + 1 - i) // 2))
    b.barrier()
    r.close()


def emit_l1f_wdcast(b, c, lay):
    nc = b.nc
    r = b.region()
    w8 = [b.sb(r, f"ewc8{i}", [P, KT * P], I8) for i in range(2)]
    wb = [b.sb(r, f"ewcb{i}", [P, KT * P], BF16) for i in range(2)]
    wd_d = b.din("l1_wdk", lay["l1_wdk"])
    wdb_sc = b.dscratch("wdb_sc", [OT1, P, KT * P], BF16)
    sl_i = Slots(b, "ewci", 2)
    s_c = b.sem("ewcc")
    sl_o = Slots(b, "ewco", 2)
    for ko in range(OT1):
        if ko >= 2:
            b.wait("sync", s_c, ko - 1)
        sl_i.load("sync", ko % 2, lambda ko=ko: nc.sync.dma_start(
            out=w8[ko % 2][:], in_=wd_d[ko].rearrange("p j m -> p (j m)")))
    for ko in range(OT1):
        sl_i.wait("vector", ko % 2, 16 * (ko // 2 + 1))
        if ko >= 2:
            sl_o.wait("vector", ko % 2, 16 * (ko // 2))
        b.op("vector", lambda ko=ko: nc.vector.tensor_copy(out=wb[ko % 2][:], in_=w8[ko % 2][:]))
        b.pub("vector", s_c)
    for ko in range(OT1):
        b.wait("scalar", s_c, ko + 1)
        sl_o.load("scalar", ko % 2, lambda ko=ko: nc.scalar.dma_start(
            out=wdb_sc[ko], in_=wb[ko % 2][:]))
    for i in range(2):
        sl_o.wait("scalar", i, 16 * ((OT1 + 1 - i) // 2))
    return r


def emit_l1f_down(b, c, lay, xin, xout):
    nc = b.nc
    r = b.region()
    KOC = [8] * 8
    NTQ = 4
    ydown = b.sb(r, "e3y", [P, KT, 1024], F32)
    wdc2 = [b.sb(r, f"e3w{i}", [P, 8, KT * P], BF16) for i in range(2)]
    hq = [b.sb(r, f"e3h{i}", [P, 8, 1024], BF16) for i in range(2)]
    sd_sb = b.sb(r, "e3sd", [P, KT], F32)
    ps = [b.ps(r, f"e3ps{i}", [P, 512]) for i in range(4)]
    pbufs = mk_postln_bufs(b, r, "e3p", c)
    E = mk_postln_sems(b, "e3p")
    s_cst = b.sem("e3cst")
    emit_postln_consts(b, pbufs, c, s_cst)
    b.op("sync", lambda: nc.sync.dma_start(out=sd_sb[:], in_=b.din("l1_sd", lay["l1_sd"])), s_cst, 16)
    wdb_sc = b._dram["wdb_sc"]
    h_sc = b._dram["h2_sc"]
    sl_wd = Slots(b, "e3wd", 2)
    sl_hq = Slots(b, "e3hq", 2)
    s_mm = b.sem("e3mm")
    s_ev = b.sem("e3ev")
    s_wdc = b.sem("e3wdc")

    tgt_wd, tgt_hq = {}, {}
    wd_cnt, hq_cnt = [0, 0], [0, 0]

    def unit(tq, ci, j, tb):
        return ((tq * 8 + ci) * KT + j) * 2 + tb

    parts = {}
    for tq in range(NTQ):
        for ci, kosz in enumerate(KOC):
            ko0 = sum(KOC[:ci])
            idx = tq * 8 + ci
            slot = idx % 2
            if idx >= 2:
                b.wait("sync", s_mm, (idx - 1) * 32)
            for kk in range(kosz):
                sl_wd.load("sync", slot, lambda ko0=ko0, kk=kk, slot=slot: nc.sync.dma_start(
                    out=wdc2[slot].ap()[:, kk, :], in_=wdb_sc[ko0 + kk]))
                sl_hq.load("sync", slot, lambda ko0=ko0, kk=kk, slot=slot, tq=tq: nc.sync.dma_start(
                    out=hq[slot].ap()[:, kk, :], in_=h_sc[ko0 + kk][:, tq * 1024:(tq + 1) * 1024]))
            wd_cnt[slot] += 16 * kosz
            hq_cnt[slot] += 16 * kosz
            tgt_wd[idx] = wd_cnt[slot]
            tgt_hq[idx] = hq_cnt[slot]
        for gg in range(2):
            g = tq * 2 + gg
            parts[g] = emit_postln_group(
                b, "e3p", E, g, g, (lambda j, gg=gg: ydown.ap()[:, j, gg * 512:(gg + 1) * 512]),
                xin, xout, pbufs)
        for gg in range(2):
            parts[tq * 2 + gg][4]()

    for tq in range(NTQ):
        if tq >= 1:
            b.wait("vector", E["ap"], tq * 16)
        for ci, kosz in enumerate(KOC):
            idx = tq * 8 + ci
            slot = idx % 2
            for j in range(KT):
                for tb in range(2):
                    u = unit(tq, ci, j, tb)
                    b.wait("vector", s_mm, u + 1)
                    if ci == 0:
                        b.op("vector", lambda j=j, tb=tb, bank=(j % 2) * 2 + tb: nc.vector.tensor_scalar(
                            out=ydown.ap()[:, j, tb * 512:(tb + 1) * 512], in0=ps[bank].ap()[:],
                            scalar1=sd_sb[:, j:j + 1], scalar2=None, op0=ALU.mult))
                    else:
                        b.op("vector", lambda j=j, tb=tb, bank=(j % 2) * 2 + tb: nc.vector.scalar_tensor_tensor(
                            out=ydown.ap()[:, j, tb * 512:(tb + 1) * 512], in0=ps[bank].ap()[:],
                            scalar=sd_sb[:, j:j + 1],
                            in1=ydown.ap()[:, j, tb * 512:(tb + 1) * 512],
                            op0=ALU.mult, op1=ALU.add))
                    b.pub("vector", s_ev)
        for gg in range(2):
            parts[tq * 2 + gg][1]()

    b.wait("tensor", s_cst, 64)
    for tq in range(NTQ):
        for ci, kosz in enumerate(KOC):
            idx = tq * 8 + ci
            slot = idx % 2
            sl_hq.wait("tensor", slot, tgt_hq[idx])
            sl_wd.wait("tensor", slot, tgt_wd[idx])
            for j in range(KT):
                for kk in range(kosz):
                    for tb in range(2):
                        bank = (j % 2) * 2 + tb
                        u = unit(tq, ci, j, tb)
                        if kk == 0 and u >= 4:
                            b.wait("tensor", s_ev, u - 3)
                        st, sp = (kk == 0), (kk == kosz - 1)
                        b.op("tensor", lambda slot=slot, kk=kk, j=j, tb=tb, bank=bank, st=st, sp=sp: nc.tensor.matmul(
                            out=ps[bank].ap()[:], lhsT=wdc2[slot][:, kk, j * P:(j + 1) * P],
                            rhs=hq[slot].ap()[:, kk, tb * 512:(tb + 1) * 512],
                            start=st, stop=sp), s_mm if sp else None)
        for gg in range(2):
            parts[tq * 2 + gg][0]()

    for tq in range(NTQ):
        for gg in range(2):
            parts[tq * 2 + gg][2]()
    for tq in range(NTQ):
        for gg in range(2):
            parts[tq * 2 + gg][3]()

    for i in range(2):
        E["sl_xwV"].wait("sync", i, 16 * (NTQ * 2 * (KT // 2) // 2))
        E["sl_xwP"].wait("sync", i, 16 * (NTQ * 2 * (KT // 2) // 2))
    b.barrier()
    r.close()


def ph_l1f(b, c, lay, xin, xout):
    rw = emit_l1f_wdcast(b, c, lay)
    pr = b.region()
    xn = emit_preln(b, "e1", xin, c, pr)
    emit_l1f_gateup(b, c, lay, xn, pr)
    pr.close()
    rw.close()
    emit_l1f_down(b, c, lay, xin, xout)
